# version 12
# baseline (speedup 1.0000x reference)
import numpy as np
from contextlib import ExitStack

import concourse.bass as bass
import concourse.mybir as mybir
import concourse.bass_utils as _bass_utils
from concourse.bass_utils import run_bass_kernel_spmd

if not getattr(_bass_utils, "_ldw_opt_patched", False):
    _orig_run_command = _bass_utils.run_command

    def _run_command_ldw(argv, **kwargs):
        argv = ["--enable-ldw-opt=true" if a == "--enable-ldw-opt=false"
                else a for a in argv]
        return _orig_run_command(argv, **kwargs)

    _bass_utils.run_command = _run_command_ldw
    _bass_utils._ldw_opt_patched = True

F32 = mybir.dt.float32
BF16 = mybir.dt.bfloat16
AF = mybir.ActivationFunctionType
ALU = mybir.AluOpType
AX = mybir.AxisListType

NCORE = 8
D = 4096
S = 1024
NF = 9984
NP = 7 * S
BOT = 64
RDIM = 48
NSLOT = 1024
VB = 256
HC = D // 128
EPS = 1e-5

ENGINES = ("tensor", "scalar", "vector", "gpsimd", "sync")
STAGE_MARKS = []
SERIAL = {"vector", "scalar", "gpsimd"}


class Sem:
    __slots__ = ("h", "n")

    def __init__(self, h):
        self.h = h
        self.n = 0

    def tok(self):
        return (self, self.n)


class Build:
    def __init__(self, nc, ctx):
        self.nc = nc
        self.q = {e: [] for e in ENGINES}
        self.last_wait = {e: {} for e in ENGINES}
        self.esem = {}
        self.all_sems = []
        self.ctx = ctx

    def sem(self, name, barrier=True):
        s = Sem(self.ctx.enter_context(self.nc.semaphore(name)))
        if barrier:
            self.all_sems.append(s)
        return s

    def wait(self, eng, tok):
        if tok is None:
            return
        sem, val = tok
        if val <= 0:
            return
        lw = self.last_wait[eng]
        if lw.get(id(sem), 0) >= val:
            return
        lw[id(sem)] = val
        self.q[eng].append(lambda e, s=sem.h, v=val: e.wait_ge(s, v))

    def op(self, eng, fn, waits=(), sem=None, inc=1, serial=True):
        if eng in SERIAL and serial:
            self.wait(eng, self.esem[eng].tok())
        for t in waits:
            self.wait(eng, t)
        if sem is not None:
            sem.n += inc
            self.q[eng].append(
                lambda e, f=fn, s=sem.h, k=inc: f(e).then_inc(s, k))
            return sem.tok()
        self.q[eng].append(fn)
        return None

    def dma(self, eng, out, in_, sem, waits=()):
        return self.op(eng, lambda e, o=out, i=in_: e.dma_start(out=o, in_=i),
                       waits=waits, sem=sem, inc=16, serial=False)

    def barrier(self, excl=()):
        toks = [s.tok() for s in self.all_sems if s.n > 0]
        for e in ENGINES:
            if e in excl:
                continue
            for t in toks:
                self.wait(e, t)

    def finalize(self, block):
        for name in ENGINES:
            fns = self.q[name]
            getattr(block, name)(lambda eng, fns=fns: [f(eng) for f in fns])


def rep(ap2d, n, w):
    return ap2d.rearrange("p (o w) -> p o w", o=1).to_broadcast(
        [ap2d.shape[0], n, w])


def build_program(nt=2048, dbg=()):
    assert nt % 512 == 0
    ntt = nt // 512
    nc = bass.Bass("TRN2", target_bir_lowering=False)
    dbg = set(dbg)
    STAGE_MARKS.clear()

    def dram(name, shape, dtype):
        kind = "ExternalOutput" if name in dbg else "Internal"
        return nc.dram_tensor(name, shape, dtype, kind=kind).ap()

    x_in = nc.dram_tensor("x_in", [D, nt], F32, kind="ExternalInput").ap()
    out_t = nc.dram_tensor("out_t", [D, nt], F32, kind="ExternalOutput").ap()
    wi = {}
    for name, shape in [
        ("upc", [8, 128, HC, 128]), ("gatec", [8, 128, HC, 128]),
        ("wcc", [2, HC, 128, 28, 128]),
        ("fgc", [NF // 256, 128, HC, 256]), ("fuc", [NF // 256, 128, HC, 256]),
        ("fdc", [2, HC, 128, 39, 128]),
        ("wqc", [128, HC, 64]), ("WrT", [BOT, RDIM]), ("akT", [BOT, NSLOT]),
        ("avc", [128, 8, D]), ("wvdc", [128, HC, VB]), ("wvuc", [128, 2, D]),
        ("wu1c", [8, 128, HC, 128]), ("wu2c", [128, 8]),
        ("lp_c", [128, 8]), ("bu1_c", [128, 8]), ("lr_row", [1, NSLOT]),
        ("d_row", [1, 6]), ("bu2_s", [1, 1]), ("gw1_s", [1, 1]),
        ("gb_s", [1, 1]),
    ]:
        wi[name] = nc.dram_tensor(name, shape, F32, kind="ExternalInput").ap()

    Pbuf = dram("Pbuf", [7, S, nt], BF16)
    pbuf = dram("pbuf", [NF, nt], BF16)
    aarr = dram("aarr", [D, nt], F32)
    x2b = dram("x2b", [D, nt], F32)
    h2b = dram("h2b", [D, nt], BF16)
    v2b = dram("v2b", [D, nt], BF16)
    rowscr = dram("rowscr", [8, nt], F32)
    cc_in = dram("cc_in", [1, 128], F32)
    cc_out = dram("cc_out", [1, 128], F32)
    h1d = dram("h1d", [D, nt], BF16) if "h1d" in dbg else None
    wsd = dram("wsd", [NSLOT, nt], BF16) if "wsd" in dbg else None
    g8 = dram("g8", [8, nt], F32) if "g8" in dbg else None

    with ExitStack() as ctx:
        b = Build(nc, ctx)
        E = ctx.enter_context

        T = b.sem("T")
        SS = b.sem("SS")
        V = b.sem("V")
        GP = b.sem("GP")
        b.esem = {"vector": V, "scalar": SS, "gpsimd": GP}
        ST = b.sem("ST")
        CS = b.sem("CS")
        CC = b.sem("CC", barrier=False)
        CC2 = b.sem("CC2", barrier=False)
        WA = b.sem("WA")
        WB = b.sem("WB")
        LA = b.sem("LA")
        LB = b.sem("LB")
        LC = b.sem("LC")
        LD = b.sem("LD")
        G2 = b.sem("G2")
        G3 = b.sem("G3")
        LE = b.sem("LE")
        SA = b.sem("SA")
        SB = b.sem("SB")
        LF = b.sem("LF")
        CS0 = b.sem("CS0")
        CS1 = b.sem("CS1")
        CSU = b.sem("CSU")
        SR = b.sem("SR")
        AC = b.sem("AC")
        SX = [b.sem("SX0"), b.sem("SX1")]
        G4 = b.sem("G4")
        G5 = b.sem("G5")
        G6 = b.sem("G6")
        G7 = b.sem("G7")
        GRP = [LA, LB, G2, G3]
        GRPS = [[LA, LB, G2, G3], [G4, G5, G6, G7]]
        LEF = [LE, LF]

        ones1f = E(nc.sbuf_tensor("ones1f", [1, 128], F32))
        onesk = E(nc.sbuf_tensor("onesk", [128, 1], BF16))
        oneskf = E(nc.sbuf_tensor("oneskf", [128, 1], F32))
        ident = E(nc.sbuf_tensor("ident", [128, 128], BF16))
        lp_n = E(nc.sbuf_tensor("lp_n", [128, 8], F32))
        bu1_t = E(nc.sbuf_tensor("bu1_t", [128, 8], F32))
        wu2_t = E(nc.sbuf_tensor("wu2_t", [128, 8], BF16))
        E6 = E(nc.sbuf_tensor("E6", [128, 6], F32))
        sc_misc = E(nc.sbuf_tensor("sc_misc", [1, 32], F32))
        constt = E(nc.sbuf_tensor("constt", [1, 4], F32))
        b.op("gpsimd", lambda e: e.memset(constt[0:1, 0:1], EPS), sem=GP)
        b.op("gpsimd", lambda e: e.memset(constt[0:1, 1:2], 1.0), sem=GP)
        nc.const_aps.register(F32, EPS, constt[0:1, 0:1])
        nc.const_aps.register(F32, 1.0, constt[0:1, 1:2])

        t_lp = b.dma("sync", lp_n[:], wi["lp_c"], LD)
        t_bu1 = b.dma("sync", bu1_t[:], wi["bu1_c"], LD)
        t_wu2 = b.dma("gpsimd", wu2_t[:], wi["wu2c"], CS)
        t_d = b.dma("sync", sc_misc[0:1, 20:26], wi["d_row"], LD)
        t_bu2 = b.dma("sync", sc_misc[0:1, 0:1], wi["bu2_s"], LD)
        t_gw1 = b.dma("sync", sc_misc[0:1, 1:2], wi["gw1_s"], LD)
        t_gb = b.dma("sync", sc_misc[0:1, 2:3], wi["gb_s"], LD)
        tok_ld = t_gb
        t_o1 = b.op("gpsimd", lambda e: e.memset(ones1f[:], 1.0), sem=GP)
        t_ok = b.op("gpsimd", lambda e: e.memset(onesk[:], 1.0), sem=GP)
        t_okf = b.op("gpsimd", lambda e: e.memset(oneskf[:], 1.0), sem=GP)
        b.op("gpsimd", lambda e: e.memset(ident[:], 0.0), sem=GP)
        t_id = b.op("gpsimd", lambda e: e.affine_select(
            out=ident[:], in_=ident[:], compare_op=ALU.not_equal, fill=1.0,
            base=0, pattern=[[-1, 128]], channel_multiplier=1), sem=GP)
        t_lpn = b.op("vector", lambda e: e.tensor_scalar_mul(
            lp_n[:], lp_n[:], -1.0), waits=[tok_ld], sem=V)
        t_e6a = b.op("scalar", lambda e: e.activation(
            sc_misc[0:1, 8:14], sc_misc[0:1, 20:26], AF.Exp, scale=-1.0),
            waits=[tok_ld], sem=SS)
        with nc.psum_tensor("ps_set", [128, 8], F32) as ps_set:
            t_e6m = b.op("tensor", lambda e: e.matmul(
                ps_set[:, 0:6], ones1f[0:1, :], sc_misc[0:1, 8:14],
                start=True, stop=True), waits=[t_e6a, t_o1], sem=T)
            b.op("scalar", lambda e: e.copy(E6[:], ps_set[:, 0:6]),
                 waits=[t_e6m], sem=SS)
        b.barrier()

        def ln_rows(ps_sum, ps_sq, rows, wtoks, w):
            mean = rows[0:1, 0:w]
            r = rows[0:1, w:2 * w]
            b.op("vector", lambda e: e.tensor_scalar_mul(
                mean, ps_sum[0:1, 0:w], 1.0 / D), waits=wtoks, sem=V)
            b.op("vector", lambda e: e.tensor_scalar_mul(
                r, ps_sq[0:1, 0:w], 1.0 / D), waits=wtoks, sem=V)
            b.op("vector", lambda e: e.tensor_tensor(
                ps_sq[0:1, 0:w], mean, mean, ALU.mult), sem=V)
            tvar = b.op("vector", lambda e: e.tensor_tensor(
                r, r, ps_sq[0:1, 0:w], ALU.subtract), sem=V)
            b.op("scalar", lambda e: e.activation(
                r, r, AF.Sqrt, bias=EPS), waits=[tvar], sem=SS)
            trc = b.op("vector", lambda e: e.reciprocal(r, r),
                       waits=[SS.tok()], sem=V)
            tnm = b.op("vector", lambda e: e.scalar_tensor_tensor(
                mean, mean, -1.0, r, ALU.mult, ALU.mult), sem=V)
            return [trc, tnm, tvar]

        def bcast2(ps_a, ps_b, rows, wtoks, w, dst_a=None, dst_b=None):
            ta = b.op("tensor", lambda e: e.matmul(
                ps_a[:, 0:w], ones1f[0:1, :], rows[0:1, w:2 * w],
                start=True, stop=True), waits=wtoks, sem=T)
            tb = b.op("tensor", lambda e: e.matmul(
                ps_b[:, 0:w], ones1f[0:1, :], rows[0:1, 0:w],
                start=True, stop=True), sem=T)
            return [ta, tb]

        def stats_pass(src_tile, w, HCn, cbt, cqt, ps_sum, ps_sq, ld_tok,
                       mm_toks, extra_w=()):
            for c in range(HCn):
                pc = c % 2
                sq_c = cqt[:, pc * w:(pc + 1) * w]
                tsq = b.op("scalar", lambda e, sq_c=sq_c, c=c: e.activation(
                    sq_c, src_tile[:, c * w:(c + 1) * w], AF.Square),
                    waits=[ld_tok(c), mm_toks.get(("q", c - 2))] + list(extra_w),
                    sem=SS)
                tm1 = b.op("tensor", lambda e, c=c: e.matmul(
                    ps_sum[:, 0:w], oneskf[:, 0:1],
                    src_tile[:, c * w:(c + 1) * w],
                    start=(c == 0), stop=(c == HCn - 1)),
                    waits=[ld_tok(c), t_okf] + list(extra_w),
                    sem=T)
                tm2 = b.op("tensor", lambda e, sq_c=sq_c, c=c: e.matmul(
                    ps_sq[:, 0:w], onesk[:, 0:1], sq_c,
                    start=(c == 0), stop=(c == HCn - 1)),
                    waits=[tsq], sem=T)
                mm_toks[("s", c)] = tm1
                mm_toks[("q", c)] = tm2
            return [tm1, tm2]

        with ExitStack() as hctx:
            h1sb = hctx.enter_context(
                nc.sbuf_tensor("h1sb", [128, HC * nt], BF16))

            def _st_ln1():
                with ExitStack() as sctx:
                    Es = sctx.enter_context
                    TL = 256
                    ntl = nt // TL
                    xt_ = [Es(nc.sbuf_tensor(f"xt1_{i}", [128, HC * TL], F32))
                           for i in range(2)]
                    cq_ = [Es(nc.sbuf_tensor(f"cq1_{i}", [128, 2 * TL], BF16))
                           for i in range(2)]
                    rows0 = Es(nc.sbuf_tensor("rw1", [1, 2 * TL], F32))
                    rows_ = [rows0, rows0]
                    ps_sum_ = [Es(nc.psum_tensor(f"pss1{i}", [1, TL], F32))
                               for i in range(2)]
                    ps_sq_ = [Es(nc.psum_tensor(f"psq1{i}", [1, TL], F32))
                              for i in range(2)]
                    ps_A_ = [Es(nc.psum_tensor(f"psA1{i}", [128, TL], F32))
                             for i in range(2)]
                    ps_B_ = [Es(nc.psum_tensor(f"psB1{i}", [128, TL], F32))
                             for i in range(2)]
                    done = {}
                    for it in range(ntl):
                        j = it % 2
                        gt = {}
                        for c in range(HC):
                            gt[c // 8] = b.dma(
                                "sync", xt_[j][:, c * TL:(c + 1) * TL],
                                x_in[c * 128:(c + 1) * 128,
                                     it * TL:(it + 1) * TL],
                                GRPS[j][c // 8], waits=[done.get(it - 2)])
                        mm_toks = {}
                        stoks = stats_pass(
                            xt_[j], TL, HC, None, cq_[j], ps_sum_[j],
                            ps_sq_[j], lambda c, gt=gt: gt[c // 8], mm_toks,
                            extra_w=[done.get(it - 2)] if it >= 2 else ())
                        rt = ln_rows(ps_sum_[j], ps_sq_[j], rows_[j], stoks, TL)
                        bt = bcast2(ps_A_[j], ps_B_[j], rows_[j], rt, TL)
                        xv = xt_[j][:].rearrange("p (c w) -> p c w", w=TL)
                        h1v = h1sb[:].rearrange("p (c t) -> p c t", t=nt)[
                            :, :, it * TL:(it + 1) * TL]
                        b.op("vector", lambda e, xv=xv, j=j: e.tensor_tensor(
                            xv, xv, rep(ps_A_[j][:], HC, TL), ALU.mult),
                            waits=bt + list(gt.values()), sem=V)
                        tap2 = b.op("vector",
                                    lambda e, xv=xv, h1v=h1v, j=j: e.tensor_tensor(
                                        h1v, xv, rep(ps_B_[j][:], HC, TL),
                                        ALU.add), sem=V)
                        done[it] = tap2
                    prev_apply = tap2
                    if h1d is not None:
                        b.dma("scalar",
                              h1d.rearrange("(c p) t -> p c t", p=128),
                              h1sb[:].rearrange("p (c t) -> p c t", t=nt),
                              ST, waits=[prev_apply])
                    return prev_apply
            STAGE_MARKS.append(('_st_ln1', {e: len(b.q[e]) for e in ENGINES}))
            ln1_done = _st_ln1()
            b.barrier(excl=("gpsimd",))

            def _st_ab():
                with ExitStack() as sctx:
                    Es = sctx.enter_context
                    wup = Es(nc.sbuf_tensor("wup", [128, HC * 128], BF16))
                    wgt = Es(nc.sbuf_tensor("wgt", [128, HC * 128], BF16))
                    Uj = Es(nc.sbuf_tensor("Uj", [128, nt], F32))
                    dj = Es(nc.sbuf_tensor("dj", [128, nt], F32))
                    keepj = Es(nc.sbuf_tensor("keepj", [128, nt], F32))
                    Ra = Es(nc.sbuf_tensor("Ra", [128, nt], F32))
                    Rb = Es(nc.sbuf_tensor("Rb", [128, nt], F32))
                    Pst = [Es(nc.sbuf_tensor(f"Pst{i}", [128, nt], BF16))
                           for i in range(2)]
                    U_bf = Es(nc.sbuf_tensor("U_bf", [128, nt], BF16))
                    ps_U = Es(nc.psum_tensor("ps_U", [128, ntt * 512], F32))
                    ps_d = Es(nc.psum_tensor("ps_d", [128, ntt * 512], F32))
                    pass
                    pass
                    wup_use = wgt_use = ln1_done
                    evU_prev = evd_prev = None
                    uj_free = []
                    dj_free = None
                    pst_tok = [None, None]
                    tab_wait = None
                    st0_prev = None
                    for sj in range(8):
                        wup_tok = b.dma(
                            "gpsimd", wup[:].rearrange("p (c j) -> p c j", j=128),
                            wi["upc"][sj], WA,
                            waits=[wup_use])
                        wgt_tok = b.dma(
                            "gpsimd", wgt[:].rearrange("p (c j) -> p c j", j=128),
                            wi["gatec"][sj], WB,
                            waits=[wgt_use])
                        for tt in range(ntt):
                            for c in range(HC):
                                last = (tt == ntt - 1 and c == HC - 1)
                                b.op("tensor", lambda e, tt=tt, c=c: e.matmul(
                                    ps_U[:, tt * 512:(tt + 1) * 512],
                                    wup[:, c * 128:(c + 1) * 128],
                                    h1sb[:, c * nt + tt * 512:
                                         c * nt + tt * 512 + 512],
                                    start=(c == 0), stop=(c == HC - 1)),
                                    waits=([wup_tok, evU_prev]
                                           if (tt == 0 and c == 0) else ()),
                                    sem=(T if last else None))
                        mmU = T.tok()
                        wup_use = mmU
                        for tt in range(ntt):
                            for c in range(HC):
                                last = (tt == ntt - 1 and c == HC - 1)
                                b.op("tensor", lambda e, tt=tt, c=c: e.matmul(
                                    ps_d[:, tt * 512:(tt + 1) * 512],
                                    wgt[:, c * 128:(c + 1) * 128],
                                    h1sb[:, c * nt + tt * 512:
                                         c * nt + tt * 512 + 512],
                                    start=(c == 0), stop=(c == HC - 1)),
                                    waits=([wgt_tok, evd_prev]
                                           if (tt == 0 and c == 0) else ()),
                                    sem=(T if last else None))
                        mmD = T.tok()
                        wgt_use = mmD
                        evU = b.op("scalar", lambda e: e.copy(Uj[:], ps_U[:]),
                                   waits=[mmU] + uj_free, sem=SS)
                        evU_prev = evU
                        evd = b.op("scalar", lambda e, sj=sj: e.activation(
                            dj[:], ps_d[:], AF.Identity, bias=lp_n[:, sj:sj + 1]),
                            waits=[mmD, dj_free, t_lpn], sem=SS)
                        evd_prev = evd
                        tab = b.op("scalar", lambda e: e.activation(
                            Ra[:], dj[:], AF.Abs), waits=[tab_wait], sem=SS)
                        tke = b.op("vector", lambda e: e.tensor_scalar(
                            keepj[:], Ra[:], 2.5, None, ALU.is_le),
                            waits=[tab], sem=V)
                        evU2 = b.op("scalar", lambda e: e.copy(
                            U_bf[:], Uj[:]), waits=[st0_prev], sem=SS)
                        st0 = b.dma("sync",
                                    Pbuf[0, sj * 128:(sj + 1) * 128, :],
                                    U_bf[:], CSU, waits=[evU2])
                        st0_prev = st0
                        src = Uj
                        Rd = [Ra, Rb]
                        tr1 = None
                        for m in range(1, 7):
                            dst = Rd[m % 2]
                            tr = b.op("vector", lambda e, s=src, d=dst: e.tensor_tensor(
                                d[:], s[:], dj[:], ALU.mult),
                                waits=[evU, evd], sem=V)
                            if m == 1:
                                tr1 = tr
                            pc = m % 2
                            tp = b.op("vector", lambda e, d=dst, m=m, pc=pc: e.tensor_scalar(
                                Pst[pc][:], d[:], E6[:, m - 1:m], None, ALU.mult),
                                waits=[pst_tok[pc]], sem=V)
                            if m >= 5:
                                tp = b.op("vector", lambda e, pc=pc: e.tensor_tensor(
                                    Pst[pc][:], Pst[pc][:], keepj[:], ALU.mult),
                                    sem=V)
                            if m == 6:
                                tab_wait = tp
                            pst_tok[pc] = b.dma(
                                "sync", Pbuf[m, sj * 128:(sj + 1) * 128, :],
                                Pst[pc][:], (CS0 if pc == 0 else CS1),
                                waits=[tp])
                            src = dst
                        uj_free = [tr1]
                        dj_free = tp
            STAGE_MARKS.append(('_st_ab', {e: len(b.q[e]) for e in ENGINES}))
            _st_ab()
            b.barrier()

            def _st_d():
                with ExitStack() as sctx:
                    Es = sctx.enter_context
                    wg_ = [Es(nc.sbuf_tensor(f"wgD{i}", [128, HC * 256], BF16))
                           for i in range(2)]
                    wu_ = [Es(nc.sbuf_tensor(f"wuD{i}", [128, HC * 256], BF16))
                           for i in range(2)]
                    sg_ = [Es(nc.sbuf_tensor(f"sgD{i}", [128, 512], F32))
                           for i in range(2)]
                    pb_ = [Es(nc.sbuf_tensor(f"pbD{i}", [128, 512], BF16))
                           for i in range(2)]
                    ps_g = [Es(nc.psum_tensor(f"ps_g{i}", [128, 512], F32))
                            for i in range(2)]
                    ps_u = [Es(nc.psum_tensor(f"ps_u{i}", [128, 512], F32))
                            for i in range(2)]
                    pass
                    pass
                    NFG = NF // 256
                    wuse = {}
                    ev_toks = {}
                    st_toks = {}
                    for fg in range(NFG):
                        pw = fg % 2
                        wsem = WA if pw == 0 else WB
                        wg_tok = b.dma(
                            "gpsimd",
                            wg_[pw][:].rearrange("p (c j) -> p c j", j=256),
                            wi["fgc"][fg], wsem,
                            waits=[wuse.get(fg - 2)])
                        wu_tok = b.dma(
                            "gpsimd",
                            wu_[pw][:].rearrange("p (c j) -> p c j", j=256),
                            wi["fuc"][fg], wsem,
                            waits=[wuse.get(fg - 2)])
                        for j in range(2):
                            for tt in range(ntt):
                                un = (fg * 2 + j) * ntt + tt
                                ui = un % 2
                                prev_ev = ev_toks.get(un - 2)
                                for c in range(HC):
                                    b.op("tensor", lambda e, pw=pw, j=j, tt=tt, c=c, ui=ui: e.matmul(
                                        ps_g[ui][:],
                                        wg_[pw][:, c * 256 + j * 128:
                                                c * 256 + j * 128 + 128],
                                        h1sb[:, c * nt + tt * 512:
                                             c * nt + tt * 512 + 512],
                                        start=(c == 0), stop=(c == HC - 1)),
                                        waits=([wu_tok, prev_ev] if c == 0 else ()))
                                for c in range(HC):
                                    last = c == HC - 1
                                    b.op("tensor", lambda e, pw=pw, j=j, tt=tt, c=c, ui=ui: e.matmul(
                                        ps_u[ui][:],
                                        wu_[pw][:, c * 256 + j * 128:
                                                c * 256 + j * 128 + 128],
                                        h1sb[:, c * nt + tt * 512:
                                             c * nt + tt * 512 + 512],
                                        start=(c == 0), stop=(c == HC - 1)),
                                        waits=([wu_tok, prev_ev] if c == 0 else ()),
                                        sem=(T if last else None))
                                mmt = T.tok()
                                if j == 1 and tt == ntt - 1:
                                    wuse[fg] = mmt
                                tsg = b.op("scalar", lambda e, ui=ui: e.activation(
                                    sg_[ui][:], ps_g[ui][:], AF.Silu),
                                    waits=[mmt, st_toks.get(un - 2)], sem=SS)
                                tpb = b.op("vector", lambda e, ui=ui: e.tensor_tensor(
                                    pb_[ui][:], sg_[ui][:], ps_u[ui][:], ALU.mult),
                                    waits=[tsg], sem=V)
                                ev_toks[un] = tpb
                                frow = fg * 256 + j * 128
                                st_toks[un] = b.dma(
                                    "scalar",
                                    pbuf[frow:frow + 128,
                                         tt * 512:(tt + 1) * 512],
                                    pb_[ui][:], (SA if ui == 0 else SB),
                                    waits=[tpb])
            STAGE_MARKS.append(('_st_d', {e: len(b.q[e]) for e in ENGINES}))
            _st_d()
            b.barrier()

        def _st_c():
            with ExitStack() as sctx:
                Es = sctx.enter_context
                Ptile = Es(nc.sbuf_tensor("Ptile", [128, 28 * nt], BF16))
                wc_ = [Es(nc.sbuf_tensor(f"wcC{i}", [128, 28 * 128], BF16))
                       for i in range(2)]
                evb = [Es(nc.sbuf_tensor(f"evC{i}", [128, nt], F32))
                       for i in range(2)]
                ain = [Es(nc.sbuf_tensor(f"ainC{i}", [128, nt], F32))
                       for i in range(2)]
                ps_C = [Es(nc.psum_tensor(f"ps_C{i}", [128, ntt * 512], F32))
                        for i in range(2)]
                pass
                carry = {}
                for ph in range(2):
                    koff = ph * 28
                    gtok = {}
                    for ci in range(28):
                        cg = koff + ci
                        m, sc = divmod(cg, 8)
                        gtok[ci // 7] = b.dma(
                            "sync", Ptile[:, ci * nt:(ci + 1) * nt],
                            Pbuf[m, sc * 128:(sc + 1) * 128, :],
                            GRP[ci // 7],
                            waits=[carry.get("mm")])
                    wtok = {}
                    wuse = {-2: carry.get("mm"), -1: carry.get("mm")}
                    tevs = {-2: carry.get("ev0"), -1: carry.get("ev1")}
                    stp = {-2: carry.get("st0"), -1: carry.get("st1")}
                    for h in range(HC):
                        pw = h % 2
                        wtok[h] = b.dma(
                            "gpsimd",
                            wc_[pw][:].rearrange("p (c j) -> p c j", j=128),
                            wi["wcc"][ph, h],
                            WA if pw == 0 else WB, waits=[wuse.get(h - 2)])
                        if ph == 1:
                            ld_a = b.dma("sync", ain[pw][:],
                                         aarr[h * 128:(h + 1) * 128, :],
                                         (LC if pw == 0 else LF),
                                         waits=[tevs.get(h - 2)])
                        for ci in range(28):
                            for tt in range(ntt):
                                last = (ci == 27 and tt == ntt - 1)
                                b.op("tensor", lambda e, pw=pw, ci=ci, tt=tt: e.matmul(
                                    ps_C[pw][:, tt * 512:(tt + 1) * 512],
                                    wc_[pw][:, ci * 128:(ci + 1) * 128],
                                    Ptile[:, ci * nt + tt * 512:
                                          ci * nt + tt * 512 + 512],
                                    start=(ci == 0), stop=(ci == 27)),
                                    waits=(([wtok[h], tevs.get(h - 2),
                                             gtok[ci // 7]]
                                            if h < 2 else
                                            [wtok[h], tevs.get(h - 2)])
                                           if (ci % 7 == 0 and tt == 0)
                                           else ()),
                                    sem=(T if last else None))
                        mmt = T.tok()
                        wuse[h] = mmt
                        if ph == 0:
                            tev = b.op("scalar", lambda e, pw=pw: e.copy(
                                evb[pw][:], ps_C[pw][:]),
                                waits=[mmt, stp.get(h - 2)], sem=SS)
                        else:
                            tev = b.op("vector", lambda e, pw=pw: e.tensor_tensor(
                                evb[pw][:], ain[pw][:], ps_C[pw][:], ALU.add),
                                waits=[mmt, ld_a, stp.get(h - 2)], sem=V)
                        tevs[h] = tev
                        stp[h] = b.dma(
                            "scalar", aarr[h * 128:(h + 1) * 128, :],
                            evb[pw][:], (SA if pw == 0 else SB), waits=[tev])
                    carry = {"mm": wuse[HC - 1], "ev0": tevs[HC - 2],
                             "ev1": tevs[HC - 1], "st0": stp[HC - 2],
                             "st1": stp[HC - 1]}
        STAGE_MARKS.append(('_st_c', {e: len(b.q[e]) for e in ENGINES}))
        _st_c()
        b.barrier()

        def _st_e():
            with ExitStack() as sctx:
                Es = sctx.enter_context
                ptile = Es(nc.sbuf_tensor("ptile", [128, 39 * nt], BF16))
                wd_ = [Es(nc.sbuf_tensor(f"wdE{i}", [128, 39 * 128], BF16))
                       for i in range(2)]
                evb = [Es(nc.sbuf_tensor(f"evE{i}", [128, nt], F32))
                       for i in range(2)]
                ain0 = Es(nc.sbuf_tensor("ainE0", [128, nt], F32))
                ain = [ain0, ain0]
                ps_E = [Es(nc.psum_tensor(f"ps_E{i}", [128, ntt * 512], F32))
                        for i in range(2)]
                pass
                carry = {}
                for ph in range(2):
                    koff = ph * 39
                    gtok = {}
                    for ci in range(39):
                        fr = (koff + ci) * 128
                        gtok[min(ci // 10, 3)] = b.dma(
                            "sync", ptile[:, ci * nt:(ci + 1) * nt],
                            pbuf[fr:fr + 128, :],
                            GRP[min(ci // 10, 3)],
                            waits=[carry.get("mm")])
                    wtok = {}
                    wuse = {-2: carry.get("mm"), -1: carry.get("mm")}
                    tevs = {-2: carry.get("ev0"), -1: carry.get("ev1")}
                    stp = {-2: carry.get("st0"), -1: carry.get("st1")}
                    tev = carry.get("ev1")
                    for h in range(HC):
                        pw = h % 2
                        wtok[h] = b.dma(
                            "gpsimd",
                            wd_[pw][:].rearrange("p (c j) -> p c j", j=128),
                            wi["fdc"][ph, h],
                            WA if pw == 0 else WB, waits=[wuse.get(h - 2)])
                        ld_a = b.dma("sync", ain[pw][:],
                                     aarr[h * 128:(h + 1) * 128, :], LC,
                                     waits=[tev])
                        for ci in range(39):
                            for tt in range(ntt):
                                last = (ci == 38 and tt == ntt - 1)
                                b.op("tensor", lambda e, pw=pw, ci=ci, tt=tt: e.matmul(
                                    ps_E[pw][:, tt * 512:(tt + 1) * 512],
                                    wd_[pw][:, ci * 128:(ci + 1) * 128],
                                    ptile[:, ci * nt + tt * 512:
                                          ci * nt + tt * 512 + 512],
                                    start=(ci == 0), stop=(ci == 38)),
                                    waits=(([wtok[h], tevs.get(h - 2),
                                             gtok[min(ci // 10, 3)]]
                                            if h < 2 else
                                            [wtok[h], tevs.get(h - 2)])
                                           if (ci % 10 == 0 and tt == 0)
                                           else ()),
                                    sem=(T if last else None))
                        mmt = T.tok()
                        wuse[h] = mmt
                        tev = b.op("vector", lambda e, pw=pw: e.tensor_tensor(
                            evb[pw][:], ain[pw][:], ps_E[pw][:], ALU.add),
                            waits=[mmt, ld_a, stp.get(h - 2)], sem=V)
                        tevs[h] = tev
                        stp[h] = b.dma(
                            "scalar", aarr[h * 128:(h + 1) * 128, :],
                            evb[pw][:], (SA if pw == 0 else SB), waits=[tev])
                    carry = {"mm": wuse[HC - 1], "ev0": tevs[HC - 2],
                             "ev1": tevs[HC - 1], "st0": stp[HC - 2],
                             "st1": stp[HC - 1]}
        STAGE_MARKS.append(('_st_e', {e: len(b.q[e]) for e in ENGINES}))
        _st_e()
        b.barrier()

        def _st_ln23():
            with ExitStack() as sctx:
                Es = sctx.enter_context
                TL = 256
                ntl = nt // TL
                at_ = [Es(nc.sbuf_tensor(f"at2_{i}", [128, HC * TL], F32))
                       for i in range(2)]
                xt_ = [Es(nc.sbuf_tensor(f"xt2_{i}", [128, HC * TL], F32))
                       for i in range(2)]
                h2_ = [Es(nc.sbuf_tensor(f"h2t_{i}", [128, HC * TL], BF16))
                       for i in range(2)]
                cq_ = [Es(nc.sbuf_tensor(f"cq2_{i}", [128, 2 * TL], BF16))
                       for i in range(2)]
                rows0 = Es(nc.sbuf_tensor("rw2", [1, 2 * TL], F32))
                rows_ = [rows0, rows0]
                ps_sum_ = [Es(nc.psum_tensor(f"pss2{i}", [1, TL], F32))
                           for i in range(2)]
                ps_sq_ = [Es(nc.psum_tensor(f"psq2{i}", [1, TL], F32))
                          for i in range(2)]
                ps_A_ = [Es(nc.psum_tensor(f"psA2{i}", [128, TL], F32))
                         for i in range(2)]
                ps_B_ = [Es(nc.psum_tensor(f"psB2{i}", [128, TL], F32))
                         for i in range(2)]
                done = {}
                h2done = {}
                prev_str = None
                for it in range(ntl):
                    j = it % 2
                    gt = {}
                    lxl = None
                    for c in range(HC):
                        gt[c // 8] = b.dma(
                            "sync", at_[j][:, c * TL:(c + 1) * TL],
                            aarr[c * 128:(c + 1) * 128,
                                 it * TL:(it + 1) * TL],
                            GRPS[j][c // 8], waits=[done.get(it - 2)])
                        lxl = b.dma(
                            "sync", xt_[j][:, c * TL:(c + 1) * TL],
                            x_in[c * 128:(c + 1) * 128,
                                 it * TL:(it + 1) * TL],
                            LEF[j], waits=[done.get(it - 2)])
                    mm_toks = {}
                    stoks = stats_pass(
                        at_[j], TL, HC, None, cq_[j], ps_sum_[j], ps_sq_[j],
                        lambda c, gt=gt: gt[c // 8], mm_toks,
                        extra_w=[done.get(it - 2)] if it >= 2 else ())
                    rt = ln_rows(ps_sum_[j], ps_sq_[j], rows_[j], stoks, TL)
                    bt = bcast2(ps_A_[j], ps_B_[j], rows_[j], rt, TL)
                    av = at_[j][:].rearrange("p (c w) -> p c w", w=TL)
                    xv = xt_[j][:].rearrange("p (c w) -> p c w", w=TL)
                    b.op("vector", lambda e, av=av, j=j: e.tensor_tensor(
                        av, av, rep(ps_A_[j][:], HC, TL), ALU.mult),
                        waits=bt + list(gt.values()), sem=V)
                    b.op("vector", lambda e, av=av, j=j: e.tensor_tensor(
                        av, av, rep(ps_B_[j][:], HC, TL), ALU.add), sem=V)
                    t3 = b.op("vector", lambda e, av=av, xv=xv: e.tensor_tensor(
                        xv, xv, av, ALU.add),
                        waits=[lxl], sem=V)
                    stx2 = b.dma(
                        "scalar",
                        out_t.rearrange("(c p) t -> p c t", p=128)[
                            :, :, it * TL:(it + 1) * TL],
                        xv, SX[j], waits=[t3])
                    mm_toks2 = {}
                    stoks = stats_pass(xt_[j], TL, HC, None, cq_[j],
                                       ps_sum_[j], ps_sq_[j],
                                       lambda c: t3, mm_toks2)
                    rt = ln_rows(ps_sum_[j], ps_sq_[j], rows_[j], stoks, TL)
                    bt = bcast2(ps_A_[j], ps_B_[j], rows_[j], rt, TL)
                    b.op("vector", lambda e, av=av, xv=xv, j=j: e.tensor_tensor(
                        av, xv, rep(ps_A_[j][:], HC, TL), ALU.mult),
                        waits=bt, sem=V)
                    h2v = h2_[j][:].rearrange("p (c w) -> p c w", w=TL)
                    t5 = b.op("vector",
                              lambda e, av=av, h2v=h2v, j=j: e.tensor_tensor(
                                  h2v, av, rep(ps_B_[j][:], HC, TL), ALU.add),
                              waits=[h2done.get(it - 2)], sem=V)
                    sth2 = b.dma(
                        "scalar",
                        h2b.rearrange("(c p) t -> p c t", p=128)[
                            :, :, it * TL:(it + 1) * TL],
                        h2v, SX[j], waits=[t5])
                    h2done[it] = sth2
                    done[it] = sth2
        STAGE_MARKS.append(('_st_ln23', {e: len(b.q[e]) for e in ENGINES}))
        _st_ln23()
        b.barrier()

        with ExitStack() as fctx:
            wT = fctx.enter_context(
                nc.sbuf_tensor("wT", [128, 8 * nt], BF16))
            av_sb = fctx.enter_context(
                nc.sbuf_tensor("av_sb", [128, 8 * D], BF16))
            wvd_sb = fctx.enter_context(
                nc.sbuf_tensor("wvd_sb", [128, HC * VB], BF16))
            wvu_sb = fctx.enter_context(
                nc.sbuf_tensor("wvu_sb", [128, 2 * D], BF16))
            pass
            pass
            pass
            tav = b.dma("gpsimd",
                        av_sb[:].rearrange("p (s h) -> p s h", h=D),
                        wi["avc"], CS)
            tvd = b.dma("gpsimd",
                        wvd_sb[:].rearrange("p (c m) -> p c m", m=VB),
                        wi["wvdc"], CS)
            tvu = b.dma("gpsimd",
                        wvu_sb[:].rearrange("p (c m) -> p c m", m=D),
                        wi["wvuc"], CS)
            with ExitStack() as qctx:
                q_bf = qctx.enter_context(
                    nc.sbuf_tensor("q_bf", [BOT, nt], BF16))
                q_bf8 = qctx.enter_context(
                    nc.sbuf_tensor("q_bf8", [BOT, nt], BF16))

                def _st_f1():
                    with ExitStack() as sctx:
                        Es = sctx.enter_context
                        h2tile = Es(nc.sbuf_tensor("h2tile", [128, HC * 512], BF16))
                        wq_t = Es(nc.sbuf_tensor("wq_t", [128, HC * 64], BF16))
                        wu1_ = [Es(nc.sbuf_tensor(f"wu1_{i}", [128, HC * 128], BF16))
                                for i in range(2)]
                        uub = Es(nc.sbuf_tensor("uub", [128, 512], BF16))
                        sqf = Es(nc.sbuf_tensor("sqf", [128, 2 * 512], BF16))
                        rows = Es(nc.sbuf_tensor("rows3", [1, 1024], F32))
                        lsl = Es(nc.sbuf_tensor("lsl", [1, 512], F32))
                        ccrow = Es(nc.sbuf_tensor("ccrow", [1, 128], F32))
                        ps_q = Es(nc.psum_tensor("ps_q", [64, 512], F32))
                        ps_uu = [Es(nc.psum_tensor(f"ps_uu{i}", [128, 512], F32))
                                 for i in range(2)]
                        ps_le = Es(nc.psum_tensor("ps_le", [1, 512], F32))
                        ps_sum = Es(nc.psum_tensor("ps_sum3", [1, 512], F32))
                        ps_sq3 = Es(nc.psum_tensor("ps_sq3", [1, 512], F32))
                        pass
                        pass
                        twq = b.dma("gpsimd",
                                    wq_t[:].rearrange("p (c j) -> p c j", j=64),
                                    wi["wqc"], CS)
                        wu1_use = {}
                        uu_prev = None
                        tl_prev = None
                        prev_lvst = None
                        prev_tv = None
                        prev_sq = None
                        tz = b.op("gpsimd", lambda e: e.memset(
                            ccrow[:], 0.0), sem=GP)
                        prev_str = None
                        for it in range(ntt):
                            lh = b.dma(
                                "sync",
                                h2tile[:].rearrange("p (c w) -> p c w", w=512),
                                h2b.rearrange("(c p) t -> p c t", p=128)[
                                    :, :, it * 512:(it + 1) * 512],
                                LA if it % 2 == 0 else LB,
                                waits=[wu1_use.get((it - 1, 7))])
                            for c in range(HC):
                                b.op("tensor", lambda e, c=c: e.matmul(
                                    ps_q[:], wq_t[:, c * 64:(c + 1) * 64],
                                    h2tile[:, c * 512:(c + 1) * 512],
                                    start=(c == 0), stop=(c == HC - 1)),
                                    waits=([twq, lh, prev_sq] if c == 0 else ()),
                                    sem=(T if c == HC - 1 else None))
                            mq = T.tok()
                            tq1 = b.op("scalar", lambda e, it=it: e.activation(
                                q_bf[:, it * 512:(it + 1) * 512], ps_q[:],
                                AF.Identity), waits=[mq], sem=SS)
                            prev_sq = b.op("scalar", lambda e, it=it: e.activation(
                                q_bf8[:, it * 512:(it + 1) * 512], ps_q[:],
                                AF.Identity, scale=0.125), waits=[mq], sem=SS)
                            sqp = None
                            for c in range(HC):
                                pc = c % 2
                                sq_c = sqf[:, pc * 512:(pc + 1) * 512]
                                tsq = b.op("scalar", lambda e, sq_c=sq_c, c=c: e.activation(
                                    sq_c, h2tile[:, c * 512:(c + 1) * 512],
                                    AF.Square),
                                    waits=[lh, sqp], sem=SS)
                                b.op("tensor", lambda e, c=c: e.matmul(
                                    ps_sum[:], onesk[:, 0:1],
                                    h2tile[:, c * 512:(c + 1) * 512],
                                    start=(c == 0), stop=(c == HC - 1)),
                                    waits=[lh, prev_tv] if c == 0 else (),
                                    sem=(T if c == HC - 1 else None))
                                tm2 = b.op("tensor", lambda e, sq_c=sq_c, c=c: e.matmul(
                                    ps_sq3[:], onesk[:, 0:1], sq_c,
                                    start=(c == 0), stop=(c == HC - 1)),
                                    waits=[tsq], sem=T)
                                sqp = tm2
                            mstat = T.tok()
                            tv1 = b.op("vector", lambda e: e.tensor_scalar_mul(
                                rows[0:1, 0:512], ps_sum[0:1, 0:512], 1.0 / D),
                                waits=[mstat, prev_lvst], sem=V)
                            b.op("vector", lambda e: e.tensor_tensor(
                                rows[0:1, 0:512], rows[0:1, 0:512],
                                rows[0:1, 0:512], ALU.mult), sem=V)
                            b.op("vector", lambda e: e.tensor_scalar_mul(
                                rows[0:1, 512:1024], ps_sq3[0:1, 0:512], 1.0 / D),
                                sem=V)
                            tvv = b.op("vector", lambda e: e.tensor_tensor(
                                rows[0:1, 512:1024], rows[0:1, 512:1024],
                                rows[0:1, 0:512], ALU.subtract), sem=V)
                            tlnv = b.op("scalar", lambda e: e.activation(
                                rows[0:1, 512:1024], rows[0:1, 512:1024],
                                AF.Ln, bias=1.0), waits=[tvv], sem=SS)
                            prev_lvst = b.dma(
                                "scalar",
                                rowscr[2:3, it * 512:(it + 1) * 512],
                                rows[0:1, 512:1024], SR, waits=[tlnv])
                            tps = b.op("vector", lambda e: e.reduce_sum(
                                ccrow[0:1, 1:2], rows[0:1, 512:1024], AX.X),
                                waits=[tlnv, tz], sem=V)
                            prev_tv = b.op("vector", lambda e: e.tensor_tensor(
                                ccrow[0:1, 0:1], ccrow[0:1, 0:1],
                                ccrow[0:1, 1:2], ALU.add), sem=V)
                            for sj in range(8):
                                pw = sj % 2
                                wpr = wu1_use.get((it, sj - 2),
                                                  wu1_use.get((it - 1, sj + 6)))
                                wtk = b.dma(
                                    "gpsimd",
                                    wu1_[pw][:].rearrange("p (c j) -> p c j", j=128),
                                    wi["wu1c"][sj],
                                    WA if pw == 0 else WB, waits=[wpr])
                                for c in range(HC):
                                    b.op("tensor", lambda e, pw=pw, c=c: e.matmul(
                                        ps_uu[pw][:],
                                        wu1_[pw][:, c * 128:(c + 1) * 128],
                                        h2tile[:, c * 512:(c + 1) * 512],
                                        start=(c == 0), stop=(c == HC - 1)),
                                        waits=([wtk, uu_prev] if c == 0 else ()),
                                        sem=(T if c == HC - 1 else None))
                                muu = T.tok()
                                wu1_use[(it, sj)] = muu
                                tg = b.op("scalar", lambda e, pw=pw, sj=sj: e.activation(
                                    uub[:], ps_uu[pw][:], AF.Gelu_apprx_tanh,
                                    bias=bu1_t[:, sj:sj + 1]),
                                    waits=[muu, t_bu1, tl_prev], sem=SS)
                                tl_prev = b.op("tensor", lambda e, sj=sj: e.matmul(
                                    ps_le[:], wu2_t[:, sj:sj + 1], uub[:],
                                    start=(sj == 0), stop=(sj == 7)),
                                    waits=[tg, t_wu2], sem=T)
                                uu_prev = tg
                            tle = b.op("scalar", lambda e: e.activation(
                                lsl[0:1, :], ps_le[:],
                                AF.Identity, bias=sc_misc[0:1, 0:1]),
                                waits=[tl_prev, tok_ld, prev_str], sem=SS)
                            prev_str = b.dma(
                                "scalar",
                                rowscr[3:4, it * 512:(it + 1) * 512],
                                lsl[0:1, :], ST, waits=[tle])
                            tl_prev = tle
                        tzz = b.op("vector", lambda e: e.memset(
                            ccrow[0:1, 1:128], 0.0), waits=[prev_tv], sem=V)
                        tst = b.dma("scalar", cc_in, ccrow[:], ST,
                                    waits=[prev_tv, tzz])
                        tcc = b.op("gpsimd", lambda e: e.collective_compute(
                            "AllReduce", ALU.add,
                            replica_groups=[list(range(NCORE))],
                            ins=[cc_in], outs=[cc_out]),
                            waits=[tst], sem=CC, serial=False)
                    return tcc
                STAGE_MARKS.append(('_st_f1', {e: len(b.q[e]) for e in ENGINES}))
                tcc = _st_f1()
                b.barrier()

                def _st_f2():
                    with ExitStack() as sctx:
                        Es = sctx.enter_context
                        ak_sb = Es(nc.sbuf_tensor("ak_sb", [64, NSLOT], BF16))
                        wr_sb = Es(nc.sbuf_tensor("wr_sb", [64, RDIM], BF16))
                        rk_bf = Es(nc.sbuf_tensor("rk_bf", [RDIM, NSLOT], BF16))
                        rq_bf = Es(nc.sbuf_tensor("rq_bf", [RDIM, nt], BF16))
                        LRb = Es(nc.sbuf_tensor("LRb", [128, NSLOT], F32))
                        lrr = Es(nc.sbuf_tensor("lrr", [1, 1024], F32))
                        rout = Es(nc.sbuf_tensor("rout", [128, NSLOT], F32))
                        attn = Es(nc.sbuf_tensor("attn", [128, NSLOT], F32))
                        am_t = Es(nc.sbuf_tensor("am_t", [128, NSLOT], F32))
                        e_t = Es(nc.sbuf_tensor("e_t", [128, NSLOT], F32))
                        msk = Es(nc.sbuf_tensor("msk", [128, NSLOT], F32))
                        t8 = Es(nc.sbuf_tensor("t8", [128, 8], F32))
                        thr = Es(nc.sbuf_tensor("thr", [128, 1], F32))
                        nmx = Es(nc.sbuf_tensor("nmx", [128, 1], F32))
                        ssm = Es(nc.sbuf_tensor("ssm", [128, 1], F32))
                        wbf = Es(nc.sbuf_tensor("wbf", [128, NSLOT], BF16))
                        ps_r = Es(nc.psum_tensor("ps_r", [128, NSLOT], F32))
                        ps_a2 = Es(nc.psum_tensor("ps_a2", [128, NSLOT], F32))
                        ps_t = Es(nc.psum_tensor("ps_t", [128, 128], BF16))
                        tak = b.dma("gpsimd", ak_sb[:], wi["akT"], CS)
                        twr = b.dma("gpsimd", wr_sb[:], wi["WrT"], CS)
                        tl1 = b.dma("sync", lrr[0:1, 0:1024],
                                    wi["lr_row"][:, 0:1024], LD)
                        tl2 = tl1
                        with nc.psum_tensor("ps_rk", [RDIM, NSLOT], F32) as ps_rk:
                            for hh in range(2):
                                tmb = b.op("tensor", lambda e, hh=hh: e.matmul(
                                    ps_r[:, hh * 512:(hh + 1) * 512],
                                    ones1f[0:1, :],
                                    lrr[0:1, hh * 512:(hh + 1) * 512],
                                    start=True, stop=True),
                                    waits=[tl1, tl2], sem=T)
                            tlrb = b.op("scalar", lambda e: e.copy(
                                LRb[:], ps_r[:]), waits=[tmb], sem=SS)
                            for hh in range(2):
                                b.op("tensor", lambda e, hh=hh: e.matmul(
                                    ps_rk[:, hh * 512:(hh + 1) * 512], wr_sb[:],
                                    ak_sb[:, hh * 512:(hh + 1) * 512],
                                    start=True, stop=True),
                                    waits=[twr],
                                    sem=(T if hh == 1 else None))
                            mrk = T.tok()
                            trk = b.op("scalar", lambda e: e.copy(
                                rk_bf[:], ps_rk[:]), waits=[mrk], sem=SS)
                            rq_prev = None
                            for tt in range(ntt):
                                mrq = b.op("tensor", lambda e, tt=tt: e.matmul(
                                    ps_rk[:, 0:512], wr_sb[:],
                                    q_bf[:, tt * 512:(tt + 1) * 512],
                                    start=True, stop=True),
                                    waits=[trk, rq_prev], sem=T)
                                rq_prev = b.op("scalar", lambda e, tt=tt: e.activation(
                                    rq_bf[:, tt * 512:(tt + 1) * 512],
                                    ps_rk[:, 0:512], AF.Identity,
                                    scale=float(1.0 / np.sqrt(RDIM))),
                                    waits=[mrq], sem=SS)
                        trq = rq_prev
                        prev = {}
                        tpp = None
                        for tc in range(nt // 128):
                            for hh in range(2):
                                b.op("tensor", lambda e, tc=tc, hh=hh: e.matmul(
                                    ps_r[:, hh * 512:(hh + 1) * 512],
                                    rq_bf[:, tc * 128:(tc + 1) * 128],
                                    rk_bf[:, hh * 512:(hh + 1) * 512],
                                    start=True, stop=True),
                                    waits=([trq, tlrb, prev.get("rout")]
                                           if hh == 0 else ()),
                                    sem=(T if hh == 1 else None))
                            mr = T.tok()
                            tro = b.op("vector", lambda e: e.tensor_tensor(
                                rout[:], ps_r[:], LRb[:], ALU.add),
                                waits=[mr], sem=V)
                            for hh in range(2):
                                b.op("tensor", lambda e, tc=tc, hh=hh: e.matmul(
                                    ps_a2[:, hh * 512:(hh + 1) * 512],
                                    q_bf8[:, tc * 128:(tc + 1) * 128],
                                    ak_sb[:, hh * 512:(hh + 1) * 512],
                                    start=True, stop=True),
                                    waits=[prev.get("attn")] if hh == 0 else (),
                                    sem=(T if hh == 1 else None))
                            ma = T.tok()
                            tat = b.op("vector", lambda e: e.tensor_tensor(
                                attn[:], ps_a2[:], LRb[:], ALU.add),
                                waits=[ma], sem=V)
                            tt8 = b.op("vector", lambda e: e.max(
                                out=t8[:], in_=rout[:]), waits=[tro], sem=V)
                            b.op("vector", lambda e: e.tensor_reduce(
                                thr[:], t8[:], AX.X, ALU.min), sem=V)
                            tmk = b.op("vector", lambda e: e.tensor_scalar(
                                msk[:], rout[:], thr[:, 0:1], None, ALU.is_ge),
                                sem=V)
                            tam = b.op("vector", lambda e: e.scalar_tensor_tensor(
                                am_t[:], attn[:], 30000.0, msk[:], ALU.add,
                                ALU.mult), waits=[tat, prev.get("am")], sem=V)
                            tnx = b.op("vector", lambda e: e.reduce_max(
                                nmx[:], am_t[:], AX.X, negate=True), sem=V)
                            tex = b.op("scalar", lambda e: e.activation(
                                e_t[:], am_t[:], AF.Exp, bias=nmx[:, 0:1]),
                                waits=[tnx, prev.get("e")], sem=SS)
                            tsm = b.op("vector", lambda e: e.reduce_sum(
                                ssm[:], e_t[:], AX.X), waits=[tex], sem=V)
                            b.op("vector", lambda e: e.reciprocal(
                                ssm[:], ssm[:]), sem=V)
                            twb = b.op("vector", lambda e: e.tensor_scalar(
                                wbf[:], e_t[:], ssm[:, 0:1], None, ALU.mult),
                                waits=[prev.get("wbf")], sem=V)
                            prev["rout"] = tmk
                            prev["attn"] = tam
                            prev["am"] = tex
                            prev["e"] = twb
                            for sb in range(8):
                                ttr = b.op("tensor", lambda e, sb=sb: e.transpose(
                                    ps_t[:], wbf[:, sb * 128:(sb + 1) * 128],
                                    ident[:]), waits=[twb, t_id, tpp], sem=T)
                                tpp = b.op("vector", lambda e, sb=sb, tc=tc: e.tensor_copy(
                                    wT[:, sb * nt + tc * 128:
                                       sb * nt + tc * 128 + 128], ps_t[:]),
                                    waits=[ttr], sem=V)
                            prev["wbf"] = tpp
                        if wsd is not None:
                            b.dma("scalar",
                                  wsd.rearrange("(s p) t -> p s t", p=128),
                                  wT[:].rearrange("p (s t) -> p s t", t=nt),
                                  ST, waits=[tpp])
                STAGE_MARKS.append(('_st_f2', {e: len(b.q[e]) for e in ENGINES}))
                _st_f2()
                b.barrier()

            G = fctx.enter_context(nc.sbuf_tensor("G", [128, nt], F32))
            def _st_gate():
                with ExitStack() as gctx:
                    Eg = gctx.enter_context
                    grow = Eg(nc.sbuf_tensor("grow", [1, nt], F32))
                    lvrow = Eg(nc.sbuf_tensor("lvrow", [1, nt], F32))
                    lnrow = Eg(nc.sbuf_tensor("lnrow", [1, nt], F32))
                    ccrow2 = Eg(nc.sbuf_tensor("ccrow2", [1, 128], F32))
                    ps_G = Eg(nc.psum_tensor("ps_G", [128, 512], F32))
                    b.dma("sync", lvrow[0:1, :], rowscr[2:3, :], LD)
                    trw = b.dma("sync", lnrow[0:1, :], rowscr[3:4, :], LD)
                    tcr = b.dma("gpsimd", ccrow2[:], cc_out, CC2, waits=[tcc])
                    tmean = b.op("vector", lambda e: e.tensor_scalar(
                        sc_misc[0:1, 3:4], ccrow2[0:1, 0:1],
                        1.0 / (NCORE * nt), 1e-6, ALU.mult, ALU.add),
                        waits=[tcr, trw], sem=V)
                    tmr = b.op("vector", lambda e: e.reciprocal(
                        sc_misc[0:1, 3:4], sc_misc[0:1, 3:4]),
                        waits=[tmean], sem=V)
                    tnv = b.op("vector", lambda e: e.tensor_scalar(
                        lvrow[0:1, :], lvrow[0:1, :], sc_misc[0:1, 3:4], 0.5,
                        ALU.mult, ALU.mult), waits=[tmr, trw], sem=V)
                    tsg = b.op("scalar", lambda e: e.activation(
                        lnrow[0:1, :], lnrow[0:1, :], AF.Sigmoid),
                        waits=[trw], sem=SS)
                    b.op("vector", lambda e: e.scalar_tensor_tensor(
                        grow[0:1, :], lnrow[0:1, :], 2.5, lvrow[0:1, :],
                        ALU.mult, ALU.add), waits=[tsg, tnv], sem=V)
                    b.op("vector", lambda e: e.tensor_scalar(
                        grow[0:1, :], grow[0:1, :], 0.0, 5.0,
                        ALU.max, ALU.min), sem=V)
                    b.op("vector", lambda e: e.tensor_scalar(
                        grow[0:1, :], grow[0:1, :], 0.5, float(1.0 / 1.5),
                        ALU.subtract, ALU.mult), sem=V)
                    b.op("vector", lambda e: e.tensor_scalar(
                        grow[0:1, :], grow[0:1, :], 0.0, 1.0,
                        ALU.max, ALU.min), sem=V)
                    tga = b.op("vector", lambda e: e.tensor_scalar(
                        grow[0:1, :], grow[0:1, :], sc_misc[0:1, 1:2],
                        sc_misc[0:1, 2:3], ALU.mult, ALU.add),
                        waits=[t_gw1, t_gb], sem=V)
                    tgs = b.op("scalar", lambda e: e.activation(
                        grow[0:1, :], grow[0:1, :], AF.Sigmoid),
                        waits=[tga], sem=SS)
                    b.op("vector", lambda e: e.tensor_scalar(
                        lvrow[0:1, :], grow[0:1, :], 0.05, None, ALU.is_ge),
                        waits=[tgs], sem=V)
                    tgf = b.op("vector", lambda e: e.tensor_tensor(
                        grow[0:1, :], grow[0:1, :], lvrow[0:1, :], ALU.mult),
                        sem=V)
                    if g8 is not None:
                        b.dma("scalar", g8[0:1, :], grow[0:1, :], ST,
                              waits=[tgf])
                    gd = None
                    for k in range(nt // 512):
                        tbg = b.op("tensor", lambda e, k=k: e.matmul(
                            ps_G[:], ones1f[0:1, :],
                            grow[0:1, k * 512:(k + 1) * 512],
                            start=True, stop=True), waits=[tgf, gd], sem=T)
                        gd = b.op("scalar", lambda e, k=k: e.copy(
                            G[:, k * 512:(k + 1) * 512], ps_G[:]),
                            waits=[tbg], sem=SS)
                    return gd
            g_done = _st_gate()


            def _st_f3():
                with ExitStack() as sctx:
                    Es = sctx.enter_context
                    aux1 = Es(nc.sbuf_tensor("aux1", [128, HC * 512], BF16))
                    v1b = Es(nc.sbuf_tensor("v1b", [128, 2 * 512], BF16))
                    v2st = Es(nc.sbuf_tensor("v2st", [128, HC * 512], BF16))
                    ps_x = [Es(nc.psum_tensor(f"ps_x{i}", [128, 512], F32))
                            for i in range(2)]
                    pe_prev = {}
                    st_prev = None
                    for tt in range(ntt):
                        for hc in range(HC):
                            pw = hc % 2
                            for sc in range(8):
                                b.op("tensor", lambda e, pw=pw, hc=hc, sc=sc, tt=tt: e.matmul(
                                    ps_x[pw][:],
                                    av_sb[:, sc * D + hc * 128:
                                          sc * D + hc * 128 + 128],
                                    wT[:, sc * nt + tt * 512:
                                       sc * nt + tt * 512 + 512],
                                    start=(sc == 0), stop=(sc == 7)),
                                    waits=([tvu, g_done,
                                        pe_prev.get(("a", hc - 2)),
                                            st_prev] if sc == 0 else ()),
                                    sem=(T if sc == 7 else None))
                            mx = T.tok()
                            teva = b.op("scalar", lambda e, pw=pw, hc=hc: e.copy(
                                aux1[:, hc * 512:(hc + 1) * 512], ps_x[pw][:]),
                                waits=[mx], sem=SS)
                            pe_prev[("a", hc)] = teva
                        for vc in range(2):
                            pw = vc % 2
                            for hc in range(HC):
                                b.op("tensor", lambda e, pw=pw, vc=vc, hc=hc: e.matmul(
                                    ps_x[pw][:],
                                    wvd_sb[:, hc * VB + vc * 128:
                                           hc * VB + vc * 128 + 128],
                                    aux1[:, hc * 512:(hc + 1) * 512],
                                    start=(hc == 0), stop=(hc == HC - 1)),
                                    waits=([tvd, teva, pe_prev.get(("v", vc - 2))]
                                           if hc == 0 else ()),
                                    sem=(T if hc == HC - 1 else None))
                            mx = T.tok()
                            tevv = b.op("scalar", lambda e, pw=pw, vc=vc: e.copy(
                                v1b[:, vc * 512:(vc + 1) * 512], ps_x[pw][:]),
                                waits=[mx], sem=SS)
                            pe_prev[("v", vc)] = tevv
                        for hc in range(HC):
                            pw = hc % 2
                            for vc in range(2):
                                b.op("tensor", lambda e, pw=pw, hc=hc, vc=vc: e.matmul(
                                    ps_x[pw][:],
                                    wvu_sb[:, vc * D + hc * 128:
                                           vc * D + hc * 128 + 128],
                                    v1b[:, vc * 512:(vc + 1) * 512],
                                    start=(vc == 0), stop=(vc == 1)),
                                    waits=([tvu, tevv, pe_prev.get(("u", hc - 2))]
                                           if vc == 0 else ()),
                                    sem=(T if vc == 1 else None))
                            mx = T.tok()
                            tevu = b.op(
                                "vector",
                                lambda e, pw=pw, hc=hc, tt=tt: e.tensor_tensor(
                                    v2st[:, hc * 512:(hc + 1) * 512],
                                    ps_x[pw][:],
                                    G[:, tt * 512:(tt + 1) * 512],
                                    ALU.mult),
                                waits=[mx, g_done], sem=V)
                            pe_prev[("u", hc)] = tevu
                        st_prev = b.dma(
                            "scalar",
                            v2b.rearrange("(c p) t -> p c t", p=128)[
                                :, :, tt * 512:(tt + 1) * 512],
                            v2st[:].rearrange("p (c w) -> p c w", w=512),
                            ST, waits=[tevu])
                        for hc in range(HC):
                            pe_prev[("a", hc)] = st_prev
                            pe_prev[("u", hc)] = st_prev
            STAGE_MARKS.append(('_st_f3', {e: len(b.q[e]) for e in ENGINES}))
            _st_f3()
            b.barrier()

        def _st_f4():
            with ExitStack() as sctx:
                Es = sctx.enter_context
                TT4 = 512
                nt4 = nt // TT4
                v2t = Es(nc.sbuf_tensor("v2t", [128, HC * TT4], BF16))
                h2t4 = Es(nc.sbuf_tensor("h2t4", [128, HC * TT4], BF16))
                ft_ = [Es(nc.sbuf_tensor(f"ft{i}", [128, HC * TT4], F32))
                       for i in range(2)]
                fqt = Es(nc.sbuf_tensor("fqt", [128, 2 * TT4], BF16))
                rows = Es(nc.sbuf_tensor("rows4", [1, 2 * TT4], F32))
                ps_sum = Es(nc.psum_tensor("ps_sum4", [1, TT4], F32))
                ps_sq4 = Es(nc.psum_tensor("ps_sq4", [1, TT4], F32))
                ps_A = Es(nc.psum_tensor("ps_A4", [128, TT4], F32))
                ps_B = Es(nc.psum_tensor("ps_B4", [128, TT4], F32))
                prev_out = {}
                prev_t4 = None
                for it in range(nt4):
                    sl = slice(it * TT4, (it + 1) * TT4)
                    ft = ft_[it % 2]
                    lv = b.dma("sync",
                               v2t[:].rearrange("p (c w) -> p c w", w=TT4),
                               v2b.rearrange("(c p) t -> p c t", p=128)[
                                   :, :, sl],
                               LA if it % 2 == 0 else LB, waits=[prev_t4])
                    lh4 = b.dma("sync",
                                h2t4[:].rearrange("p (c w) -> p c w", w=TT4),
                                h2b.rearrange("(c p) t -> p c t", p=128)[
                                    :, :, sl],
                                LC, waits=[prev_t4])
                    fv = ft[:].rearrange("p (c w) -> p c w", w=TT4)
                    hv = h2t4[:].rearrange("p (c w) -> p c w", w=TT4)
                    vv = v2t[:].rearrange("p (c w) -> p c w", w=TT4)
                    t4 = b.op("vector",
                              lambda e, fv=fv, hv=hv, vv=vv: e.tensor_tensor(
                                  fv, hv, vv, ALU.add),
                              waits=[lv, lh4, prev_out.get(it - 2)], sem=V)
                    prev_t4 = t4
                    mm_toks = {}
                    stoks = stats_pass(ft, TT4, HC, None, fqt, ps_sum, ps_sq4,
                                       lambda c: t4, mm_toks)
                    rt = ln_rows(ps_sum, ps_sq4, rows, stoks, TT4)
                    bt = bcast2(ps_A, ps_B, rows, rt, TT4)
                    b.op("vector", lambda e, fv=fv: e.tensor_tensor(
                        fv, fv, rep(ps_A[:], HC, TT4), ALU.mult),
                        waits=bt, sem=V)
                    t7 = b.op("vector", lambda e, fv=fv: e.tensor_tensor(
                        fv, fv, rep(ps_B[:], HC, TT4), ALU.add), sem=V)
                    prev_out[it] = b.op(
                        "gpsimd",
                        lambda e, fv=fv, sl=sl: e.dma_start(
                            out=out_t.rearrange("(c p) t -> p c t", p=128)[
                                :, :, sl],
                            in_=fv, accum_op=ALU.add),
                        waits=[t7], sem=AC, inc=16, serial=False)

        STAGE_MARKS.append(('_st_f4', {e: len(b.q[e]) for e in ENGINES}))
        _st_f4()
        STAGE_MARKS.append(('end', {e: len(b.q[e]) for e in ENGINES}))
        b.wait("sync", ST.tok())
        b.wait("sync", CS.tok())
        b.wait("sync", CC.tok())
        b.wait("sync", CC2.tok())
        b.wait("sync", AC.tok())
        b.wait("sync", SX[0].tok())
        b.wait("sync", SX[1].tok())

        with nc.Block() as block:
            b.finalize(block)
    return nc



def _prep_shared(inp):
    g = lambda k: np.ascontiguousarray(np.asarray(inp[k], dtype=np.float32))
    C = np.ascontiguousarray
    lao = g("L0_attn_local_approx_output")
    fuse = g("L0_attn_fuse_weight")
    d = {}
    d["upc"] = C(g("L0_attn_up_w").reshape(8, 128, HC, 128)
                 .transpose(0, 3, 2, 1))
    d["gatec"] = C(g("L0_attn_gate_w").reshape(8, 128, HC, 128)
                   .transpose(0, 3, 2, 1))
    WC = np.concatenate(
        [lao.T[None], fuse.transpose(2, 1, 0)], axis=0).reshape(NP, D)
    d["wcc"] = C(WC.reshape(2, 28, 128, HC, 128).transpose(0, 3, 2, 1, 4))
    d["fgc"] = C(g("L0_attn_ffn_gate_w").reshape(NF // 256, 256, HC, 128)
                 .transpose(0, 3, 2, 1))
    d["fuc"] = C(g("L0_attn_ffn_up_w").reshape(NF // 256, 256, HC, 128)
                 .transpose(0, 3, 2, 1))
    fdT = np.asarray(inp["L0_attn_ffn_down_w"], dtype=np.float32).T
    d["fdc"] = C(fdT.reshape(2, 39, 128, HC, 128).transpose(0, 3, 2, 1, 4))
    d["wqc"] = C(g("L0_ffn_Wq").reshape(BOT, HC, 128).transpose(2, 1, 0))
    d["WrT"] = C(g("L0_ffn_Wr").T)
    d["akT"] = C(g("L0_ffn_aux_keys").T)
    d["avc"] = C(g("L0_ffn_aux_values").reshape(8, 128, D).transpose(1, 0, 2))
    d["wvdc"] = C(g("L0_ffn_Wvd").reshape(VB, HC, 128).transpose(2, 1, 0))
    d["wvuc"] = C(g("L0_ffn_Wvu").reshape(D, 2, 128).transpose(2, 1, 0))
    d["wu1c"] = C(g("L0_ffn_Wu1").reshape(8, 128, HC, 128)
                  .transpose(0, 3, 2, 1))
    d["wu2c"] = C(g("L0_ffn_Wu2").reshape(8, 128).T)
    d["lp_c"] = C(g("L0_attn_local_point").reshape(8, 128).T)
    d["bu1_c"] = C(g("L0_ffn_bu1").reshape(8, 128).T)
    d["lr_row"] = g("L0_ffn_log_rel").reshape(1, NSLOT)
    d["d_row"] = g("L0_attn_discount_factor").reshape(1, 6)
    d["bu2_s"] = g("L0_ffn_bu2").reshape(1, 1)
    d["gw1_s"] = np.asarray(
        inp["L0_ffn_gate_w1"], dtype=np.float32).reshape(1, 1)
    d["gb_s"] = np.asarray(
        inp["L0_ffn_gate_bias"], dtype=np.float32).reshape(1, 1)
    return d


def kernel(**inputs):
    x = np.ascontiguousarray(
        np.asarray(inputs["L0_attn_hidden_states"], dtype=np.float32))
    _, seq, _ = x.shape
    nt = seq // NCORE
    shared = _prep_shared(inputs)
    xt = x[0].T
    nc = build_program(nt=nt)
    in_maps = []
    for c in range(NCORE):
        m = dict(shared)
        m["x_in"] = np.ascontiguousarray(xt[:, c * nt:(c + 1) * nt])
        in_maps.append(m)
    res = run_bass_kernel_spmd(nc, in_maps, core_ids=list(range(NCORE)))
    out = np.concatenate(
        [res.results[c]["out_t"].T for c in range(NCORE)], axis=0)
    return out.reshape(1, seq, D).astype(np.float32)


# version 13
# speedup vs baseline: 1.0027x; 1.0027x over previous
import numpy as np
from contextlib import ExitStack

import concourse.bass as bass
import concourse.mybir as mybir
import concourse.bass_utils as _bass_utils
from concourse.bass_utils import run_bass_kernel_spmd

if not getattr(_bass_utils, "_ldw_opt_patched", False):
    _orig_run_command = _bass_utils.run_command

    def _run_command_ldw(argv, **kwargs):
        argv = ["--enable-ldw-opt=true" if a == "--enable-ldw-opt=false"
                else a for a in argv]
        return _orig_run_command(argv, **kwargs)

    _bass_utils.run_command = _run_command_ldw
    _bass_utils._ldw_opt_patched = True

F32 = mybir.dt.float32
BF16 = mybir.dt.bfloat16
AF = mybir.ActivationFunctionType
ALU = mybir.AluOpType
AX = mybir.AxisListType

NCORE = 8
D = 4096
S = 1024
NF = 9984
NP = 7 * S
BOT = 64
RDIM = 48
NSLOT = 1024
VB = 256
HC = D // 128
EPS = 1e-5

ENGINES = ("tensor", "scalar", "vector", "gpsimd", "sync")
STAGE_MARKS = []
SERIAL = {"vector", "scalar", "gpsimd"}


class Sem:
    __slots__ = ("h", "n")

    def __init__(self, h):
        self.h = h
        self.n = 0

    def tok(self):
        return (self, self.n)


class Build:
    def __init__(self, nc, ctx):
        self.nc = nc
        self.q = {e: [] for e in ENGINES}
        self.last_wait = {e: {} for e in ENGINES}
        self.esem = {}
        self.all_sems = []
        self.ctx = ctx

    def sem(self, name, barrier=True):
        s = Sem(self.ctx.enter_context(self.nc.semaphore(name)))
        if barrier:
            self.all_sems.append(s)
        return s

    def wait(self, eng, tok):
        if tok is None:
            return
        sem, val = tok
        if val <= 0:
            return
        lw = self.last_wait[eng]
        if lw.get(id(sem), 0) >= val:
            return
        lw[id(sem)] = val
        self.q[eng].append(lambda e, s=sem.h, v=val: e.wait_ge(s, v))

    def op(self, eng, fn, waits=(), sem=None, inc=1, serial=True):
        if eng in SERIAL and serial:
            self.wait(eng, self.esem[eng].tok())
        for t in waits:
            self.wait(eng, t)
        if sem is not None:
            sem.n += inc
            self.q[eng].append(
                lambda e, f=fn, s=sem.h, k=inc: f(e).then_inc(s, k))
            return sem.tok()
        self.q[eng].append(fn)
        return None

    def dma(self, eng, out, in_, sem, waits=()):
        return self.op(eng, lambda e, o=out, i=in_: e.dma_start(out=o, in_=i),
                       waits=waits, sem=sem, inc=16, serial=False)

    def barrier(self, excl=()):
        toks = [s.tok() for s in self.all_sems if s.n > 0]
        for e in ENGINES:
            if e in excl:
                continue
            for t in toks:
                self.wait(e, t)

    def finalize(self, block):
        for name in ENGINES:
            fns = self.q[name]
            getattr(block, name)(lambda eng, fns=fns: [f(eng) for f in fns])


def rep(ap2d, n, w):
    return ap2d.rearrange("p (o w) -> p o w", o=1).to_broadcast(
        [ap2d.shape[0], n, w])


def build_program(nt=2048, dbg=()):
    assert nt % 512 == 0
    ntt = nt // 512
    nc = bass.Bass("TRN2", target_bir_lowering=False)
    dbg = set(dbg)
    STAGE_MARKS.clear()

    def dram(name, shape, dtype):
        kind = "ExternalOutput" if name in dbg else "Internal"
        return nc.dram_tensor(name, shape, dtype, kind=kind).ap()

    x_in = nc.dram_tensor("x_in", [D, nt], F32, kind="ExternalInput").ap()
    out_t = nc.dram_tensor("out_t", [D, nt], F32, kind="ExternalOutput").ap()
    wi = {}
    for name, shape in [
        ("upc", [8, 128, HC, 128]), ("gatec", [8, 128, HC, 128]),
        ("wcc", [2, HC, 128, 28, 128]),
        ("fgc", [NF // 256, 128, HC, 256]), ("fuc", [NF // 256, 128, HC, 256]),
        ("fdc", [2, HC, 128, 39, 128]),
        ("wqc", [128, HC, 64]), ("WrT", [BOT, RDIM]), ("akT", [BOT, NSLOT]),
        ("avc", [128, 8, D]), ("wvdc", [128, HC, VB]), ("wvuc", [128, 2, D]),
        ("wu1c", [8, 128, HC, 128]), ("wu2c", [128, 8]),
        ("lp_c", [128, 8]), ("bu1_c", [128, 8]), ("lr_row", [1, NSLOT]),
        ("d_row", [1, 6]), ("bu2_s", [1, 1]), ("gw1_s", [1, 1]),
        ("gb_s", [1, 1]),
    ]:
        wi[name] = nc.dram_tensor(name, shape, F32, kind="ExternalInput").ap()

    Pbuf = dram("Pbuf", [7, S, nt], BF16)
    pbuf = dram("pbuf", [NF, nt], BF16)
    aarr = dram("aarr", [D, nt], F32)
    x2b = dram("x2b", [D, nt], F32)
    h2b = dram("h2b", [D, nt], BF16)
    v2b = dram("v2b", [D, nt], BF16)
    rowscr = dram("rowscr", [8, nt], F32)
    cc_in = dram("cc_in", [1, 128], F32)
    cc_out = dram("cc_out", [1, 128], F32)
    h1d = dram("h1d", [D, nt], BF16) if "h1d" in dbg else None
    wsd = dram("wsd", [NSLOT, nt], BF16) if "wsd" in dbg else None
    g8 = dram("g8", [8, nt], F32) if "g8" in dbg else None

    with ExitStack() as ctx:
        b = Build(nc, ctx)
        E = ctx.enter_context

        T = b.sem("T")
        SS = b.sem("SS")
        V = b.sem("V")
        GP = b.sem("GP")
        b.esem = {"vector": V, "scalar": SS, "gpsimd": GP}
        ST = b.sem("ST")
        CS = b.sem("CS")
        CC = b.sem("CC", barrier=False)
        CC2 = b.sem("CC2", barrier=False)
        WA = b.sem("WA")
        WB = b.sem("WB")
        LA = b.sem("LA")
        LB = b.sem("LB")
        LC = b.sem("LC")
        LD = b.sem("LD")
        G2 = b.sem("G2")
        G3 = b.sem("G3")
        LE = b.sem("LE")
        SA = b.sem("SA")
        SB = b.sem("SB")
        LF = b.sem("LF")
        CS0 = b.sem("CS0")
        CS1 = b.sem("CS1")
        CSU = b.sem("CSU")
        SR = b.sem("SR")
        AC = b.sem("AC")
        AC2 = b.sem("AC2")
        ACX = [AC, AC2]
        SX = [b.sem("SX0"), b.sem("SX1")]
        G4 = b.sem("G4")
        G5 = b.sem("G5")
        G6 = b.sem("G6")
        G7 = b.sem("G7")
        GRP = [LA, LB, G2, G3]
        GRPS = [[LA, LB, G2, G3], [G4, G5, G6, G7]]
        LEF = [LE, LF]

        ones1f = E(nc.sbuf_tensor("ones1f", [1, 128], F32))
        onesk = E(nc.sbuf_tensor("onesk", [128, 1], BF16))
        oneskf = E(nc.sbuf_tensor("oneskf", [128, 1], F32))
        ident = E(nc.sbuf_tensor("ident", [128, 128], BF16))
        lp_n = E(nc.sbuf_tensor("lp_n", [128, 8], F32))
        bu1_t = E(nc.sbuf_tensor("bu1_t", [128, 8], F32))
        wu2_t = E(nc.sbuf_tensor("wu2_t", [128, 8], BF16))
        E6 = E(nc.sbuf_tensor("E6", [128, 6], F32))
        sc_misc = E(nc.sbuf_tensor("sc_misc", [1, 32], F32))
        constt = E(nc.sbuf_tensor("constt", [1, 4], F32))
        b.op("gpsimd", lambda e: e.memset(constt[0:1, 0:1], EPS), sem=GP)
        b.op("gpsimd", lambda e: e.memset(constt[0:1, 1:2], 1.0), sem=GP)
        nc.const_aps.register(F32, EPS, constt[0:1, 0:1])
        nc.const_aps.register(F32, 1.0, constt[0:1, 1:2])

        t_lp = b.dma("sync", lp_n[:], wi["lp_c"], LD)
        t_bu1 = b.dma("sync", bu1_t[:], wi["bu1_c"], LD)
        t_wu2 = b.dma("gpsimd", wu2_t[:], wi["wu2c"], CS)
        t_d = b.dma("sync", sc_misc[0:1, 20:26], wi["d_row"], LD)
        t_bu2 = b.dma("sync", sc_misc[0:1, 0:1], wi["bu2_s"], LD)
        t_gw1 = b.dma("sync", sc_misc[0:1, 1:2], wi["gw1_s"], LD)
        t_gb = b.dma("sync", sc_misc[0:1, 2:3], wi["gb_s"], LD)
        tok_ld = t_gb
        t_o1 = b.op("gpsimd", lambda e: e.memset(ones1f[:], 1.0), sem=GP)
        t_ok = b.op("gpsimd", lambda e: e.memset(onesk[:], 1.0), sem=GP)
        t_okf = b.op("gpsimd", lambda e: e.memset(oneskf[:], 1.0), sem=GP)
        b.op("gpsimd", lambda e: e.memset(ident[:], 0.0), sem=GP)
        t_id = b.op("gpsimd", lambda e: e.affine_select(
            out=ident[:], in_=ident[:], compare_op=ALU.not_equal, fill=1.0,
            base=0, pattern=[[-1, 128]], channel_multiplier=1), sem=GP)
        t_lpn = b.op("vector", lambda e: e.tensor_scalar_mul(
            lp_n[:], lp_n[:], -1.0), waits=[tok_ld], sem=V)
        t_e6a = b.op("scalar", lambda e: e.activation(
            sc_misc[0:1, 8:14], sc_misc[0:1, 20:26], AF.Exp, scale=-1.0),
            waits=[tok_ld], sem=SS)
        with nc.psum_tensor("ps_set", [128, 8], F32) as ps_set:
            t_e6m = b.op("tensor", lambda e: e.matmul(
                ps_set[:, 0:6], ones1f[0:1, :], sc_misc[0:1, 8:14],
                start=True, stop=True), waits=[t_e6a, t_o1], sem=T)
            b.op("scalar", lambda e: e.copy(E6[:], ps_set[:, 0:6]),
                 waits=[t_e6m], sem=SS)
        b.barrier()

        def ln_rows(ps_sum, ps_sq, rows, wtoks, w):
            mean = rows[0:1, 0:w]
            r = rows[0:1, w:2 * w]
            b.op("vector", lambda e: e.tensor_scalar_mul(
                mean, ps_sum[0:1, 0:w], 1.0 / D), waits=wtoks, sem=V)
            b.op("vector", lambda e: e.tensor_scalar_mul(
                r, ps_sq[0:1, 0:w], 1.0 / D), waits=wtoks, sem=V)
            b.op("vector", lambda e: e.tensor_tensor(
                ps_sq[0:1, 0:w], mean, mean, ALU.mult), sem=V)
            tvar = b.op("vector", lambda e: e.tensor_tensor(
                r, r, ps_sq[0:1, 0:w], ALU.subtract), sem=V)
            b.op("scalar", lambda e: e.activation(
                r, r, AF.Sqrt, bias=EPS), waits=[tvar], sem=SS)
            trc = b.op("vector", lambda e: e.reciprocal(r, r),
                       waits=[SS.tok()], sem=V)
            tnm = b.op("vector", lambda e: e.scalar_tensor_tensor(
                mean, mean, -1.0, r, ALU.mult, ALU.mult), sem=V)
            return [trc, tnm, tvar]

        def bcast2(ps_a, ps_b, rows, wtoks, w, dst_a=None, dst_b=None):
            ta = b.op("tensor", lambda e: e.matmul(
                ps_a[:, 0:w], ones1f[0:1, :], rows[0:1, w:2 * w],
                start=True, stop=True), waits=wtoks, sem=T)
            tb = b.op("tensor", lambda e: e.matmul(
                ps_b[:, 0:w], ones1f[0:1, :], rows[0:1, 0:w],
                start=True, stop=True), sem=T)
            return [ta, tb]

        def stats_pass(src_tile, w, HCn, cbt, cqt, ps_sum, ps_sq, ld_tok,
                       mm_toks, extra_w=()):
            for c in range(HCn):
                pc = c % 2
                sq_c = cqt[:, pc * w:(pc + 1) * w]
                tsq = b.op("scalar", lambda e, sq_c=sq_c, c=c: e.activation(
                    sq_c, src_tile[:, c * w:(c + 1) * w], AF.Square),
                    waits=[ld_tok(c), mm_toks.get(("q", c - 2))] + list(extra_w),
                    sem=SS)
                tm1 = b.op("tensor", lambda e, c=c: e.matmul(
                    ps_sum[:, 0:w], oneskf[:, 0:1],
                    src_tile[:, c * w:(c + 1) * w],
                    start=(c == 0), stop=(c == HCn - 1)),
                    waits=[ld_tok(c), t_okf] + list(extra_w),
                    sem=T)
                tm2 = b.op("tensor", lambda e, sq_c=sq_c, c=c: e.matmul(
                    ps_sq[:, 0:w], onesk[:, 0:1], sq_c,
                    start=(c == 0), stop=(c == HCn - 1)),
                    waits=[tsq], sem=T)
                mm_toks[("s", c)] = tm1
                mm_toks[("q", c)] = tm2
            return [tm1, tm2]

        with ExitStack() as hctx:
            h1sb = hctx.enter_context(
                nc.sbuf_tensor("h1sb", [128, HC * nt], BF16))

            def _st_ln1():
                with ExitStack() as sctx:
                    Es = sctx.enter_context
                    TL = 256
                    ntl = nt // TL
                    xt_ = [Es(nc.sbuf_tensor(f"xt1_{i}", [128, HC * TL], F32))
                           for i in range(2)]
                    cq_ = [Es(nc.sbuf_tensor(f"cq1_{i}", [128, 2 * TL], BF16))
                           for i in range(2)]
                    rows0 = Es(nc.sbuf_tensor("rw1", [1, 2 * TL], F32))
                    rows_ = [rows0, rows0]
                    ps_sum_ = [Es(nc.psum_tensor(f"pss1{i}", [1, TL], F32))
                               for i in range(2)]
                    ps_sq_ = [Es(nc.psum_tensor(f"psq1{i}", [1, TL], F32))
                              for i in range(2)]
                    ps_A_ = [Es(nc.psum_tensor(f"psA1{i}", [128, TL], F32))
                             for i in range(2)]
                    ps_B_ = [Es(nc.psum_tensor(f"psB1{i}", [128, TL], F32))
                             for i in range(2)]
                    done = {}
                    for it in range(ntl):
                        j = it % 2
                        gt = {}
                        for c in range(HC):
                            gt[c // 8] = b.dma(
                                "sync", xt_[j][:, c * TL:(c + 1) * TL],
                                x_in[c * 128:(c + 1) * 128,
                                     it * TL:(it + 1) * TL],
                                GRPS[j][c // 8], waits=[done.get(it - 2)])
                        mm_toks = {}
                        stoks = stats_pass(
                            xt_[j], TL, HC, None, cq_[j], ps_sum_[j],
                            ps_sq_[j], lambda c, gt=gt: gt[c // 8], mm_toks,
                            extra_w=[done.get(it - 2)] if it >= 2 else ())
                        rt = ln_rows(ps_sum_[j], ps_sq_[j], rows_[j], stoks, TL)
                        bt = bcast2(ps_A_[j], ps_B_[j], rows_[j], rt, TL)
                        xv = xt_[j][:].rearrange("p (c w) -> p c w", w=TL)
                        h1v = h1sb[:].rearrange("p (c t) -> p c t", t=nt)[
                            :, :, it * TL:(it + 1) * TL]
                        b.op("vector", lambda e, xv=xv, j=j: e.tensor_tensor(
                            xv, xv, rep(ps_A_[j][:], HC, TL), ALU.mult),
                            waits=bt + list(gt.values()), sem=V)
                        tap2 = b.op("vector",
                                    lambda e, xv=xv, h1v=h1v, j=j: e.tensor_tensor(
                                        h1v, xv, rep(ps_B_[j][:], HC, TL),
                                        ALU.add), sem=V)
                        done[it] = tap2
                    prev_apply = tap2
                    if h1d is not None:
                        b.dma("scalar",
                              h1d.rearrange("(c p) t -> p c t", p=128),
                              h1sb[:].rearrange("p (c t) -> p c t", t=nt),
                              ST, waits=[prev_apply])
                    return prev_apply
            STAGE_MARKS.append(('_st_ln1', {e: len(b.q[e]) for e in ENGINES}))
            ln1_done = _st_ln1()
            b.barrier(excl=("gpsimd",))

            def _st_ab():
                with ExitStack() as sctx:
                    Es = sctx.enter_context
                    wup = Es(nc.sbuf_tensor("wup", [128, HC * 128], BF16))
                    wgt = Es(nc.sbuf_tensor("wgt", [128, HC * 128], BF16))
                    Uj = Es(nc.sbuf_tensor("Uj", [128, nt], F32))
                    dj = Es(nc.sbuf_tensor("dj", [128, nt], F32))
                    keepj = Es(nc.sbuf_tensor("keepj", [128, nt], F32))
                    Ra = Es(nc.sbuf_tensor("Ra", [128, nt], F32))
                    Rb = Es(nc.sbuf_tensor("Rb", [128, nt], F32))
                    Pst = [Es(nc.sbuf_tensor(f"Pst{i}", [128, nt], BF16))
                           for i in range(2)]
                    U_bf = Es(nc.sbuf_tensor("U_bf", [128, nt], BF16))
                    ps_U = Es(nc.psum_tensor("ps_U", [128, ntt * 512], F32))
                    ps_d = Es(nc.psum_tensor("ps_d", [128, ntt * 512], F32))
                    pass
                    pass
                    wup_use = wgt_use = ln1_done
                    evU_prev = evd_prev = None
                    uj_free = []
                    dj_free = None
                    pst_tok = [None, None]
                    tab_wait = None
                    st0_prev = None
                    for sj in range(8):
                        wup_tok = b.dma(
                            "gpsimd", wup[:].rearrange("p (c j) -> p c j", j=128),
                            wi["upc"][sj], WA,
                            waits=[wup_use])
                        wgt_tok = b.dma(
                            "gpsimd", wgt[:].rearrange("p (c j) -> p c j", j=128),
                            wi["gatec"][sj], WB,
                            waits=[wgt_use])
                        for tt in range(ntt):
                            for c in range(HC):
                                last = (tt == ntt - 1 and c == HC - 1)
                                b.op("tensor", lambda e, tt=tt, c=c: e.matmul(
                                    ps_U[:, tt * 512:(tt + 1) * 512],
                                    wup[:, c * 128:(c + 1) * 128],
                                    h1sb[:, c * nt + tt * 512:
                                         c * nt + tt * 512 + 512],
                                    start=(c == 0), stop=(c == HC - 1)),
                                    waits=([wup_tok, evU_prev]
                                           if (tt == 0 and c == 0) else ()),
                                    sem=(T if last else None))
                        mmU = T.tok()
                        wup_use = mmU
                        for tt in range(ntt):
                            for c in range(HC):
                                last = (tt == ntt - 1 and c == HC - 1)
                                b.op("tensor", lambda e, tt=tt, c=c: e.matmul(
                                    ps_d[:, tt * 512:(tt + 1) * 512],
                                    wgt[:, c * 128:(c + 1) * 128],
                                    h1sb[:, c * nt + tt * 512:
                                         c * nt + tt * 512 + 512],
                                    start=(c == 0), stop=(c == HC - 1)),
                                    waits=([wgt_tok, evd_prev]
                                           if (tt == 0 and c == 0) else ()),
                                    sem=(T if last else None))
                        mmD = T.tok()
                        wgt_use = mmD
                        evU = b.op("scalar", lambda e: e.copy(Uj[:], ps_U[:]),
                                   waits=[mmU] + uj_free, sem=SS)
                        evU_prev = evU
                        evd = b.op("scalar", lambda e, sj=sj: e.activation(
                            dj[:], ps_d[:], AF.Identity, bias=lp_n[:, sj:sj + 1]),
                            waits=[mmD, dj_free, t_lpn], sem=SS)
                        evd_prev = evd
                        tab = b.op("scalar", lambda e: e.activation(
                            Ra[:], dj[:], AF.Abs), waits=[tab_wait], sem=SS)
                        tke = b.op("vector", lambda e: e.tensor_scalar(
                            keepj[:], Ra[:], 2.5, None, ALU.is_le),
                            waits=[tab], sem=V)
                        evU2 = b.op("scalar", lambda e: e.copy(
                            U_bf[:], Uj[:]), waits=[st0_prev], sem=SS)
                        st0 = b.dma("sync",
                                    Pbuf[0, sj * 128:(sj + 1) * 128, :],
                                    U_bf[:], CSU, waits=[evU2])
                        st0_prev = st0
                        src = Uj
                        Rd = [Ra, Rb]
                        tr1 = None
                        for m in range(1, 7):
                            dst = Rd[m % 2]
                            tr = b.op("vector", lambda e, s=src, d=dst: e.tensor_tensor(
                                d[:], s[:], dj[:], ALU.mult),
                                waits=[evU, evd], sem=V)
                            if m == 1:
                                tr1 = tr
                            pc = m % 2
                            tp = b.op("vector", lambda e, d=dst, m=m, pc=pc: e.tensor_scalar(
                                Pst[pc][:], d[:], E6[:, m - 1:m], None, ALU.mult),
                                waits=[pst_tok[pc]], sem=V)
                            if m >= 5:
                                tp = b.op("vector", lambda e, pc=pc: e.tensor_tensor(
                                    Pst[pc][:], Pst[pc][:], keepj[:], ALU.mult),
                                    sem=V)
                            if m == 6:
                                tab_wait = tp
                            pst_tok[pc] = b.dma(
                                "sync", Pbuf[m, sj * 128:(sj + 1) * 128, :],
                                Pst[pc][:], (CS0 if pc == 0 else CS1),
                                waits=[tp])
                            src = dst
                        uj_free = [tr1]
                        dj_free = tp
            STAGE_MARKS.append(('_st_ab', {e: len(b.q[e]) for e in ENGINES}))
            _st_ab()
            b.barrier()

            def _st_d():
                with ExitStack() as sctx:
                    Es = sctx.enter_context
                    wg_ = [Es(nc.sbuf_tensor(f"wgD{i}", [128, HC * 256], BF16))
                           for i in range(2)]
                    wu_ = [Es(nc.sbuf_tensor(f"wuD{i}", [128, HC * 256], BF16))
                           for i in range(2)]
                    sg_ = [Es(nc.sbuf_tensor(f"sgD{i}", [128, 512], F32))
                           for i in range(2)]
                    pb_ = [Es(nc.sbuf_tensor(f"pbD{i}", [128, 512], BF16))
                           for i in range(2)]
                    ps_g = [Es(nc.psum_tensor(f"ps_g{i}", [128, 512], F32))
                            for i in range(2)]
                    ps_u = [Es(nc.psum_tensor(f"ps_u{i}", [128, 512], F32))
                            for i in range(2)]
                    pass
                    pass
                    NFG = NF // 256
                    wuse = {}
                    ev_toks = {}
                    st_toks = {}
                    for fg in range(NFG):
                        pw = fg % 2
                        wsem = WA if pw == 0 else WB
                        wg_tok = b.dma(
                            "gpsimd",
                            wg_[pw][:].rearrange("p (c j) -> p c j", j=256),
                            wi["fgc"][fg], wsem,
                            waits=[wuse.get(fg - 2)])
                        wu_tok = b.dma(
                            "gpsimd",
                            wu_[pw][:].rearrange("p (c j) -> p c j", j=256),
                            wi["fuc"][fg], wsem,
                            waits=[wuse.get(fg - 2)])
                        for j in range(2):
                            for tt in range(ntt):
                                un = (fg * 2 + j) * ntt + tt
                                ui = un % 2
                                prev_ev = ev_toks.get(un - 2)
                                for c in range(HC):
                                    b.op("tensor", lambda e, pw=pw, j=j, tt=tt, c=c, ui=ui: e.matmul(
                                        ps_g[ui][:],
                                        wg_[pw][:, c * 256 + j * 128:
                                                c * 256 + j * 128 + 128],
                                        h1sb[:, c * nt + tt * 512:
                                             c * nt + tt * 512 + 512],
                                        start=(c == 0), stop=(c == HC - 1)),
                                        waits=([wu_tok, prev_ev] if c == 0 else ()))
                                for c in range(HC):
                                    last = c == HC - 1
                                    b.op("tensor", lambda e, pw=pw, j=j, tt=tt, c=c, ui=ui: e.matmul(
                                        ps_u[ui][:],
                                        wu_[pw][:, c * 256 + j * 128:
                                                c * 256 + j * 128 + 128],
                                        h1sb[:, c * nt + tt * 512:
                                             c * nt + tt * 512 + 512],
                                        start=(c == 0), stop=(c == HC - 1)),
                                        waits=([wu_tok, prev_ev] if c == 0 else ()),
                                        sem=(T if last else None))
                                mmt = T.tok()
                                if j == 1 and tt == ntt - 1:
                                    wuse[fg] = mmt
                                tsg = b.op("scalar", lambda e, ui=ui: e.activation(
                                    sg_[ui][:], ps_g[ui][:], AF.Silu),
                                    waits=[mmt, st_toks.get(un - 2)], sem=SS)
                                tpb = b.op("vector", lambda e, ui=ui: e.tensor_tensor(
                                    pb_[ui][:], sg_[ui][:], ps_u[ui][:], ALU.mult),
                                    waits=[tsg], sem=V)
                                ev_toks[un] = tpb
                                frow = fg * 256 + j * 128
                                st_toks[un] = b.dma(
                                    "scalar",
                                    pbuf[frow:frow + 128,
                                         tt * 512:(tt + 1) * 512],
                                    pb_[ui][:], (SA if ui == 0 else SB),
                                    waits=[tpb])
            STAGE_MARKS.append(('_st_d', {e: len(b.q[e]) for e in ENGINES}))
            _st_d()
            b.barrier()

        def _st_c():
            with ExitStack() as sctx:
                Es = sctx.enter_context
                Ptile = Es(nc.sbuf_tensor("Ptile", [128, 28 * nt], BF16))
                wc_ = [Es(nc.sbuf_tensor(f"wcC{i}", [128, 28 * 128], BF16))
                       for i in range(2)]
                evb = [Es(nc.sbuf_tensor(f"evC{i}", [128, nt], F32))
                       for i in range(2)]
                ain = [Es(nc.sbuf_tensor(f"ainC{i}", [128, nt], F32))
                       for i in range(2)]
                ps_C = [Es(nc.psum_tensor(f"ps_C{i}", [128, ntt * 512], F32))
                        for i in range(2)]
                pass
                carry = {}
                for ph in range(2):
                    koff = ph * 28
                    gtok = {}
                    for ci in range(28):
                        cg = koff + ci
                        m, sc = divmod(cg, 8)
                        gtok[ci // 7] = b.dma(
                            "sync", Ptile[:, ci * nt:(ci + 1) * nt],
                            Pbuf[m, sc * 128:(sc + 1) * 128, :],
                            GRP[ci // 7],
                            waits=[carry.get("mm")])
                    wtok = {}
                    wuse = {-2: carry.get("mm"), -1: carry.get("mm")}
                    tevs = {-2: carry.get("ev0"), -1: carry.get("ev1")}
                    stp = {-2: carry.get("st0"), -1: carry.get("st1")}
                    for h in range(HC):
                        pw = h % 2
                        wtok[h] = b.dma(
                            "gpsimd",
                            wc_[pw][:].rearrange("p (c j) -> p c j", j=128),
                            wi["wcc"][ph, h],
                            WA if pw == 0 else WB, waits=[wuse.get(h - 2)])
                        if ph == 1:
                            ld_a = b.dma("sync", ain[pw][:],
                                         aarr[h * 128:(h + 1) * 128, :],
                                         (LC if pw == 0 else LF),
                                         waits=[tevs.get(h - 2)])
                        for ci in range(28):
                            for tt in range(ntt):
                                last = (ci == 27 and tt == ntt - 1)
                                b.op("tensor", lambda e, pw=pw, ci=ci, tt=tt: e.matmul(
                                    ps_C[pw][:, tt * 512:(tt + 1) * 512],
                                    wc_[pw][:, ci * 128:(ci + 1) * 128],
                                    Ptile[:, ci * nt + tt * 512:
                                          ci * nt + tt * 512 + 512],
                                    start=(ci == 0), stop=(ci == 27)),
                                    waits=(([wtok[h], tevs.get(h - 2),
                                             gtok[ci // 7]]
                                            if h < 2 else
                                            [wtok[h], tevs.get(h - 2)])
                                           if (ci % 7 == 0 and tt == 0)
                                           else ()),
                                    sem=(T if last else None))
                        mmt = T.tok()
                        wuse[h] = mmt
                        if ph == 0:
                            tev = b.op("scalar", lambda e, pw=pw: e.copy(
                                evb[pw][:], ps_C[pw][:]),
                                waits=[mmt, stp.get(h - 2)], sem=SS)
                        else:
                            tev = b.op("vector", lambda e, pw=pw: e.tensor_tensor(
                                evb[pw][:], ain[pw][:], ps_C[pw][:], ALU.add),
                                waits=[mmt, ld_a, stp.get(h - 2)], sem=V)
                        tevs[h] = tev
                        stp[h] = b.dma(
                            "scalar", aarr[h * 128:(h + 1) * 128, :],
                            evb[pw][:], (SA if pw == 0 else SB), waits=[tev])
                    carry = {"mm": wuse[HC - 1], "ev0": tevs[HC - 2],
                             "ev1": tevs[HC - 1], "st0": stp[HC - 2],
                             "st1": stp[HC - 1]}
        STAGE_MARKS.append(('_st_c', {e: len(b.q[e]) for e in ENGINES}))
        _st_c()
        b.barrier()

        def _st_e():
            with ExitStack() as sctx:
                Es = sctx.enter_context
                ptile = Es(nc.sbuf_tensor("ptile", [128, 39 * nt], BF16))
                wd_ = [Es(nc.sbuf_tensor(f"wdE{i}", [128, 39 * 128], BF16))
                       for i in range(2)]
                evb = [Es(nc.sbuf_tensor(f"evE{i}", [128, nt], F32))
                       for i in range(2)]
                ain0 = Es(nc.sbuf_tensor("ainE0", [128, nt], F32))
                ain = [ain0, ain0]
                ps_E = [Es(nc.psum_tensor(f"ps_E{i}", [128, ntt * 512], F32))
                        for i in range(2)]
                pass
                carry = {}
                for ph in range(2):
                    koff = ph * 39
                    gtok = {}
                    for ci in range(39):
                        fr = (koff + ci) * 128
                        gtok[min(ci // 10, 3)] = b.dma(
                            "sync", ptile[:, ci * nt:(ci + 1) * nt],
                            pbuf[fr:fr + 128, :],
                            GRP[min(ci // 10, 3)],
                            waits=[carry.get("mm")])
                    wtok = {}
                    wuse = {-2: carry.get("mm"), -1: carry.get("mm")}
                    tevs = {-2: carry.get("ev0"), -1: carry.get("ev1")}
                    stp = {-2: carry.get("st0"), -1: carry.get("st1")}
                    tev = carry.get("ev1")
                    for h in range(HC):
                        pw = h % 2
                        wtok[h] = b.dma(
                            "gpsimd",
                            wd_[pw][:].rearrange("p (c j) -> p c j", j=128),
                            wi["fdc"][ph, h],
                            WA if pw == 0 else WB, waits=[wuse.get(h - 2)])
                        ld_a = b.dma("sync", ain[pw][:],
                                     aarr[h * 128:(h + 1) * 128, :], LC,
                                     waits=[tev])
                        for ci in range(39):
                            for tt in range(ntt):
                                last = (ci == 38 and tt == ntt - 1)
                                b.op("tensor", lambda e, pw=pw, ci=ci, tt=tt: e.matmul(
                                    ps_E[pw][:, tt * 512:(tt + 1) * 512],
                                    wd_[pw][:, ci * 128:(ci + 1) * 128],
                                    ptile[:, ci * nt + tt * 512:
                                          ci * nt + tt * 512 + 512],
                                    start=(ci == 0), stop=(ci == 38)),
                                    waits=(([wtok[h], tevs.get(h - 2),
                                             gtok[min(ci // 10, 3)]]
                                            if h < 2 else
                                            [wtok[h], tevs.get(h - 2)])
                                           if (ci % 10 == 0 and tt == 0)
                                           else ()),
                                    sem=(T if last else None))
                        mmt = T.tok()
                        wuse[h] = mmt
                        tev = b.op("vector", lambda e, pw=pw: e.tensor_tensor(
                            evb[pw][:], ain[pw][:], ps_E[pw][:], ALU.add),
                            waits=[mmt, ld_a, stp.get(h - 2)], sem=V)
                        tevs[h] = tev
                        stp[h] = b.dma(
                            "scalar", aarr[h * 128:(h + 1) * 128, :],
                            evb[pw][:], (SA if pw == 0 else SB), waits=[tev])
                    carry = {"mm": wuse[HC - 1], "ev0": tevs[HC - 2],
                             "ev1": tevs[HC - 1], "st0": stp[HC - 2],
                             "st1": stp[HC - 1]}
        STAGE_MARKS.append(('_st_e', {e: len(b.q[e]) for e in ENGINES}))
        _st_e()
        b.barrier()

        def _st_ln23():
            with ExitStack() as sctx:
                Es = sctx.enter_context
                TL = 256
                ntl = nt // TL
                at_ = [Es(nc.sbuf_tensor(f"at2_{i}", [128, HC * TL], F32))
                       for i in range(2)]
                xt_ = [Es(nc.sbuf_tensor(f"xt2_{i}", [128, HC * TL], F32))
                       for i in range(2)]
                h2_ = [Es(nc.sbuf_tensor(f"h2t_{i}", [128, HC * TL], BF16))
                       for i in range(2)]
                cq_ = [Es(nc.sbuf_tensor(f"cq2_{i}", [128, 2 * TL], BF16))
                       for i in range(2)]
                rows0 = Es(nc.sbuf_tensor("rw2", [1, 2 * TL], F32))
                rows_ = [rows0, rows0]
                ps_sum_ = [Es(nc.psum_tensor(f"pss2{i}", [1, TL], F32))
                           for i in range(2)]
                ps_sq_ = [Es(nc.psum_tensor(f"psq2{i}", [1, TL], F32))
                          for i in range(2)]
                ps_A_ = [Es(nc.psum_tensor(f"psA2{i}", [128, TL], F32))
                         for i in range(2)]
                ps_B_ = [Es(nc.psum_tensor(f"psB2{i}", [128, TL], F32))
                         for i in range(2)]
                done = {}
                h2done = {}
                prev_str = None
                for it in range(ntl):
                    j = it % 2
                    gt = {}
                    lxl = None
                    for c in range(HC):
                        gt[c // 8] = b.dma(
                            "sync", at_[j][:, c * TL:(c + 1) * TL],
                            aarr[c * 128:(c + 1) * 128,
                                 it * TL:(it + 1) * TL],
                            GRPS[j][c // 8], waits=[done.get(it - 2)])
                        lxl = b.dma(
                            "sync", xt_[j][:, c * TL:(c + 1) * TL],
                            x_in[c * 128:(c + 1) * 128,
                                 it * TL:(it + 1) * TL],
                            LEF[j], waits=[done.get(it - 2)])
                    mm_toks = {}
                    stoks = stats_pass(
                        at_[j], TL, HC, None, cq_[j], ps_sum_[j], ps_sq_[j],
                        lambda c, gt=gt: gt[c // 8], mm_toks,
                        extra_w=[done.get(it - 2)] if it >= 2 else ())
                    rt = ln_rows(ps_sum_[j], ps_sq_[j], rows_[j], stoks, TL)
                    bt = bcast2(ps_A_[j], ps_B_[j], rows_[j], rt, TL)
                    av = at_[j][:].rearrange("p (c w) -> p c w", w=TL)
                    xv = xt_[j][:].rearrange("p (c w) -> p c w", w=TL)
                    b.op("vector", lambda e, av=av, j=j: e.tensor_tensor(
                        av, av, rep(ps_A_[j][:], HC, TL), ALU.mult),
                        waits=bt + list(gt.values()), sem=V)
                    b.op("vector", lambda e, av=av, j=j: e.tensor_tensor(
                        av, av, rep(ps_B_[j][:], HC, TL), ALU.add), sem=V)
                    t3 = b.op("vector", lambda e, av=av, xv=xv: e.tensor_tensor(
                        xv, xv, av, ALU.add),
                        waits=[lxl], sem=V)
                    stx2 = b.dma(
                        "scalar",
                        out_t.rearrange("(c p) t -> p c t", p=128)[
                            :, :, it * TL:(it + 1) * TL],
                        xv, SX[j], waits=[t3])
                    mm_toks2 = {}
                    stoks = stats_pass(xt_[j], TL, HC, None, cq_[j],
                                       ps_sum_[j], ps_sq_[j],
                                       lambda c: t3, mm_toks2)
                    rt = ln_rows(ps_sum_[j], ps_sq_[j], rows_[j], stoks, TL)
                    bt = bcast2(ps_A_[j], ps_B_[j], rows_[j], rt, TL)
                    b.op("vector", lambda e, av=av, xv=xv, j=j: e.tensor_tensor(
                        av, xv, rep(ps_A_[j][:], HC, TL), ALU.mult),
                        waits=bt, sem=V)
                    h2v = h2_[j][:].rearrange("p (c w) -> p c w", w=TL)
                    t5 = b.op("vector",
                              lambda e, av=av, h2v=h2v, j=j: e.tensor_tensor(
                                  h2v, av, rep(ps_B_[j][:], HC, TL), ALU.add),
                              waits=[h2done.get(it - 2)], sem=V)
                    sth2 = b.dma(
                        "scalar",
                        h2b.rearrange("(c p) t -> p c t", p=128)[
                            :, :, it * TL:(it + 1) * TL],
                        h2v, SX[j], waits=[t5])
                    h2done[it] = sth2
                    done[it] = sth2
        STAGE_MARKS.append(('_st_ln23', {e: len(b.q[e]) for e in ENGINES}))
        _st_ln23()
        b.barrier()

        with ExitStack() as fctx:
            wT = fctx.enter_context(
                nc.sbuf_tensor("wT", [128, 8 * nt], BF16))
            av_sb = fctx.enter_context(
                nc.sbuf_tensor("av_sb", [128, 8 * D], BF16))
            wvd_sb = fctx.enter_context(
                nc.sbuf_tensor("wvd_sb", [128, HC * VB], BF16))
            wvu_sb = fctx.enter_context(
                nc.sbuf_tensor("wvu_sb", [128, 2 * D], BF16))
            pass
            pass
            pass
            tav = b.dma("gpsimd",
                        av_sb[:].rearrange("p (s h) -> p s h", h=D),
                        wi["avc"], CS)
            tvd = b.dma("gpsimd",
                        wvd_sb[:].rearrange("p (c m) -> p c m", m=VB),
                        wi["wvdc"], CS)
            tvu = b.dma("gpsimd",
                        wvu_sb[:].rearrange("p (c m) -> p c m", m=D),
                        wi["wvuc"], CS)
            with ExitStack() as qctx:
                q_bf = qctx.enter_context(
                    nc.sbuf_tensor("q_bf", [BOT, nt], BF16))
                q_bf8 = qctx.enter_context(
                    nc.sbuf_tensor("q_bf8", [BOT, nt], BF16))

                def _st_f1():
                    with ExitStack() as sctx:
                        Es = sctx.enter_context
                        h2tile = Es(nc.sbuf_tensor("h2tile", [128, HC * 512], BF16))
                        wq_t = Es(nc.sbuf_tensor("wq_t", [128, HC * 64], BF16))
                        wu1_ = [Es(nc.sbuf_tensor(f"wu1_{i}", [128, HC * 128], BF16))
                                for i in range(2)]
                        uub = Es(nc.sbuf_tensor("uub", [128, 512], BF16))
                        sqf = Es(nc.sbuf_tensor("sqf", [128, 2 * 512], BF16))
                        rows = Es(nc.sbuf_tensor("rows3", [1, 1024], F32))
                        lsl = Es(nc.sbuf_tensor("lsl", [1, 512], F32))
                        ccrow = Es(nc.sbuf_tensor("ccrow", [1, 128], F32))
                        ps_q = Es(nc.psum_tensor("ps_q", [64, 512], F32))
                        ps_uu = [Es(nc.psum_tensor(f"ps_uu{i}", [128, 512], F32))
                                 for i in range(2)]
                        ps_le = Es(nc.psum_tensor("ps_le", [1, 512], F32))
                        ps_sum = Es(nc.psum_tensor("ps_sum3", [1, 512], F32))
                        ps_sq3 = Es(nc.psum_tensor("ps_sq3", [1, 512], F32))
                        pass
                        pass
                        twq = b.dma("gpsimd",
                                    wq_t[:].rearrange("p (c j) -> p c j", j=64),
                                    wi["wqc"], CS)
                        wu1_use = {}
                        uu_prev = None
                        tl_prev = None
                        prev_lvst = None
                        prev_tv = None
                        prev_sq = None
                        tz = b.op("gpsimd", lambda e: e.memset(
                            ccrow[:], 0.0), sem=GP)
                        prev_str = None
                        for it in range(ntt):
                            lh = b.dma(
                                "sync",
                                h2tile[:].rearrange("p (c w) -> p c w", w=512),
                                h2b.rearrange("(c p) t -> p c t", p=128)[
                                    :, :, it * 512:(it + 1) * 512],
                                LA if it % 2 == 0 else LB,
                                waits=[wu1_use.get((it - 1, 7))])
                            for c in range(HC):
                                b.op("tensor", lambda e, c=c: e.matmul(
                                    ps_q[:], wq_t[:, c * 64:(c + 1) * 64],
                                    h2tile[:, c * 512:(c + 1) * 512],
                                    start=(c == 0), stop=(c == HC - 1)),
                                    waits=([twq, lh, prev_sq] if c == 0 else ()),
                                    sem=(T if c == HC - 1 else None))
                            mq = T.tok()
                            tq1 = b.op("scalar", lambda e, it=it: e.activation(
                                q_bf[:, it * 512:(it + 1) * 512], ps_q[:],
                                AF.Identity), waits=[mq], sem=SS)
                            prev_sq = b.op("scalar", lambda e, it=it: e.activation(
                                q_bf8[:, it * 512:(it + 1) * 512], ps_q[:],
                                AF.Identity, scale=0.125), waits=[mq], sem=SS)
                            sqp = None
                            for c in range(HC):
                                pc = c % 2
                                sq_c = sqf[:, pc * 512:(pc + 1) * 512]
                                tsq = b.op("scalar", lambda e, sq_c=sq_c, c=c: e.activation(
                                    sq_c, h2tile[:, c * 512:(c + 1) * 512],
                                    AF.Square),
                                    waits=[lh, sqp], sem=SS)
                                b.op("tensor", lambda e, c=c: e.matmul(
                                    ps_sum[:], onesk[:, 0:1],
                                    h2tile[:, c * 512:(c + 1) * 512],
                                    start=(c == 0), stop=(c == HC - 1)),
                                    waits=[lh, prev_tv] if c == 0 else (),
                                    sem=(T if c == HC - 1 else None))
                                tm2 = b.op("tensor", lambda e, sq_c=sq_c, c=c: e.matmul(
                                    ps_sq3[:], onesk[:, 0:1], sq_c,
                                    start=(c == 0), stop=(c == HC - 1)),
                                    waits=[tsq], sem=T)
                                sqp = tm2
                            mstat = T.tok()
                            tv1 = b.op("vector", lambda e: e.tensor_scalar_mul(
                                rows[0:1, 0:512], ps_sum[0:1, 0:512], 1.0 / D),
                                waits=[mstat, prev_lvst], sem=V)
                            b.op("vector", lambda e: e.tensor_tensor(
                                rows[0:1, 0:512], rows[0:1, 0:512],
                                rows[0:1, 0:512], ALU.mult), sem=V)
                            b.op("vector", lambda e: e.tensor_scalar_mul(
                                rows[0:1, 512:1024], ps_sq3[0:1, 0:512], 1.0 / D),
                                sem=V)
                            tvv = b.op("vector", lambda e: e.tensor_tensor(
                                rows[0:1, 512:1024], rows[0:1, 512:1024],
                                rows[0:1, 0:512], ALU.subtract), sem=V)
                            tlnv = b.op("scalar", lambda e: e.activation(
                                rows[0:1, 512:1024], rows[0:1, 512:1024],
                                AF.Ln, bias=1.0), waits=[tvv], sem=SS)
                            prev_lvst = b.dma(
                                "scalar",
                                rowscr[2:3, it * 512:(it + 1) * 512],
                                rows[0:1, 512:1024], SR, waits=[tlnv])
                            tps = b.op("vector", lambda e: e.reduce_sum(
                                ccrow[0:1, 1:2], rows[0:1, 512:1024], AX.X),
                                waits=[tlnv, tz], sem=V)
                            prev_tv = b.op("vector", lambda e: e.tensor_tensor(
                                ccrow[0:1, 0:1], ccrow[0:1, 0:1],
                                ccrow[0:1, 1:2], ALU.add), sem=V)
                            for sj in range(8):
                                pw = sj % 2
                                wpr = wu1_use.get((it, sj - 2),
                                                  wu1_use.get((it - 1, sj + 6)))
                                wtk = b.dma(
                                    "gpsimd",
                                    wu1_[pw][:].rearrange("p (c j) -> p c j", j=128),
                                    wi["wu1c"][sj],
                                    WA if pw == 0 else WB, waits=[wpr])
                                for c in range(HC):
                                    b.op("tensor", lambda e, pw=pw, c=c: e.matmul(
                                        ps_uu[pw][:],
                                        wu1_[pw][:, c * 128:(c + 1) * 128],
                                        h2tile[:, c * 512:(c + 1) * 512],
                                        start=(c == 0), stop=(c == HC - 1)),
                                        waits=([wtk, uu_prev] if c == 0 else ()),
                                        sem=(T if c == HC - 1 else None))
                                muu = T.tok()
                                wu1_use[(it, sj)] = muu
                                tg = b.op("scalar", lambda e, pw=pw, sj=sj: e.activation(
                                    uub[:], ps_uu[pw][:], AF.Gelu_apprx_tanh,
                                    bias=bu1_t[:, sj:sj + 1]),
                                    waits=[muu, t_bu1, tl_prev], sem=SS)
                                tl_prev = b.op("tensor", lambda e, sj=sj: e.matmul(
                                    ps_le[:], wu2_t[:, sj:sj + 1], uub[:],
                                    start=(sj == 0), stop=(sj == 7)),
                                    waits=[tg, t_wu2], sem=T)
                                uu_prev = tg
                            tle = b.op("scalar", lambda e: e.activation(
                                lsl[0:1, :], ps_le[:],
                                AF.Identity, bias=sc_misc[0:1, 0:1]),
                                waits=[tl_prev, tok_ld, prev_str], sem=SS)
                            prev_str = b.dma(
                                "scalar",
                                rowscr[3:4, it * 512:(it + 1) * 512],
                                lsl[0:1, :], ST, waits=[tle])
                            tl_prev = tle
                        tzz = b.op("vector", lambda e: e.memset(
                            ccrow[0:1, 1:128], 0.0), waits=[prev_tv], sem=V)
                        tst = b.dma("scalar", cc_in, ccrow[:], ST,
                                    waits=[prev_tv, tzz])
                        tcc = b.op("gpsimd", lambda e: e.collective_compute(
                            "AllReduce", ALU.add,
                            replica_groups=[list(range(NCORE))],
                            ins=[cc_in], outs=[cc_out]),
                            waits=[tst], sem=CC, serial=False)
                    return tcc
                STAGE_MARKS.append(('_st_f1', {e: len(b.q[e]) for e in ENGINES}))
                tcc = _st_f1()
                b.barrier()

                def _st_f2():
                    with ExitStack() as sctx:
                        Es = sctx.enter_context
                        ak_sb = Es(nc.sbuf_tensor("ak_sb", [64, NSLOT], BF16))
                        wr_sb = Es(nc.sbuf_tensor("wr_sb", [64, RDIM], BF16))
                        rk_bf = Es(nc.sbuf_tensor("rk_bf", [RDIM, NSLOT], BF16))
                        rq_bf = Es(nc.sbuf_tensor("rq_bf", [RDIM, nt], BF16))
                        LRb = Es(nc.sbuf_tensor("LRb", [128, NSLOT], F32))
                        lrr = Es(nc.sbuf_tensor("lrr", [1, 1024], F32))
                        rout = Es(nc.sbuf_tensor("rout", [128, NSLOT], F32))
                        attn = Es(nc.sbuf_tensor("attn", [128, NSLOT], F32))
                        am_t = Es(nc.sbuf_tensor("am_t", [128, NSLOT], F32))
                        e_t = Es(nc.sbuf_tensor("e_t", [128, NSLOT], F32))
                        msk = Es(nc.sbuf_tensor("msk", [128, NSLOT], F32))
                        t8 = Es(nc.sbuf_tensor("t8", [128, 8], F32))
                        thr = Es(nc.sbuf_tensor("thr", [128, 1], F32))
                        nmx = Es(nc.sbuf_tensor("nmx", [128, 1], F32))
                        ssm = Es(nc.sbuf_tensor("ssm", [128, 1], F32))
                        wbf = Es(nc.sbuf_tensor("wbf", [128, NSLOT], BF16))
                        ps_r = Es(nc.psum_tensor("ps_r", [128, NSLOT], F32))
                        ps_a2 = Es(nc.psum_tensor("ps_a2", [128, NSLOT], F32))
                        ps_t = Es(nc.psum_tensor("ps_t", [128, 128], BF16))
                        tak = b.dma("gpsimd", ak_sb[:], wi["akT"], CS)
                        twr = b.dma("gpsimd", wr_sb[:], wi["WrT"], CS)
                        tl1 = b.dma("sync", lrr[0:1, 0:1024],
                                    wi["lr_row"][:, 0:1024], LD)
                        tl2 = tl1
                        with nc.psum_tensor("ps_rk", [RDIM, NSLOT], F32) as ps_rk:
                            for hh in range(2):
                                tmb = b.op("tensor", lambda e, hh=hh: e.matmul(
                                    ps_r[:, hh * 512:(hh + 1) * 512],
                                    ones1f[0:1, :],
                                    lrr[0:1, hh * 512:(hh + 1) * 512],
                                    start=True, stop=True),
                                    waits=[tl1, tl2], sem=T)
                            tlrb = b.op("scalar", lambda e: e.copy(
                                LRb[:], ps_r[:]), waits=[tmb], sem=SS)
                            for hh in range(2):
                                b.op("tensor", lambda e, hh=hh: e.matmul(
                                    ps_rk[:, hh * 512:(hh + 1) * 512], wr_sb[:],
                                    ak_sb[:, hh * 512:(hh + 1) * 512],
                                    start=True, stop=True),
                                    waits=[twr],
                                    sem=(T if hh == 1 else None))
                            mrk = T.tok()
                            trk = b.op("scalar", lambda e: e.copy(
                                rk_bf[:], ps_rk[:]), waits=[mrk], sem=SS)
                            rq_prev = None
                            for tt in range(ntt):
                                mrq = b.op("tensor", lambda e, tt=tt: e.matmul(
                                    ps_rk[:, 0:512], wr_sb[:],
                                    q_bf[:, tt * 512:(tt + 1) * 512],
                                    start=True, stop=True),
                                    waits=[trk, rq_prev], sem=T)
                                rq_prev = b.op("scalar", lambda e, tt=tt: e.activation(
                                    rq_bf[:, tt * 512:(tt + 1) * 512],
                                    ps_rk[:, 0:512], AF.Identity,
                                    scale=float(1.0 / np.sqrt(RDIM))),
                                    waits=[mrq], sem=SS)
                        trq = rq_prev
                        prev = {}
                        tpp = None
                        for tc in range(nt // 128):
                            for hh in range(2):
                                b.op("tensor", lambda e, tc=tc, hh=hh: e.matmul(
                                    ps_r[:, hh * 512:(hh + 1) * 512],
                                    rq_bf[:, tc * 128:(tc + 1) * 128],
                                    rk_bf[:, hh * 512:(hh + 1) * 512],
                                    start=True, stop=True),
                                    waits=([trq, tlrb, prev.get("rout")]
                                           if hh == 0 else ()),
                                    sem=(T if hh == 1 else None))
                            mr = T.tok()
                            tro = b.op("vector", lambda e: e.tensor_tensor(
                                rout[:], ps_r[:], LRb[:], ALU.add),
                                waits=[mr], sem=V)
                            for hh in range(2):
                                b.op("tensor", lambda e, tc=tc, hh=hh: e.matmul(
                                    ps_a2[:, hh * 512:(hh + 1) * 512],
                                    q_bf8[:, tc * 128:(tc + 1) * 128],
                                    ak_sb[:, hh * 512:(hh + 1) * 512],
                                    start=True, stop=True),
                                    waits=[prev.get("attn")] if hh == 0 else (),
                                    sem=(T if hh == 1 else None))
                            ma = T.tok()
                            tat = b.op("vector", lambda e: e.tensor_tensor(
                                attn[:], ps_a2[:], LRb[:], ALU.add),
                                waits=[ma], sem=V)
                            tt8 = b.op("vector", lambda e: e.max(
                                out=t8[:], in_=rout[:]), waits=[tro], sem=V)
                            b.op("vector", lambda e: e.tensor_reduce(
                                thr[:], t8[:], AX.X, ALU.min), sem=V)
                            tmk = b.op("vector", lambda e: e.tensor_scalar(
                                msk[:], rout[:], thr[:, 0:1], None, ALU.is_ge),
                                sem=V)
                            tam = b.op("vector", lambda e: e.scalar_tensor_tensor(
                                am_t[:], attn[:], 30000.0, msk[:], ALU.add,
                                ALU.mult), waits=[tat, prev.get("am")], sem=V)
                            tnx = b.op("vector", lambda e: e.reduce_max(
                                nmx[:], am_t[:], AX.X, negate=True), sem=V)
                            tex = b.op("scalar", lambda e: e.activation(
                                e_t[:], am_t[:], AF.Exp, bias=nmx[:, 0:1]),
                                waits=[tnx, prev.get("e")], sem=SS)
                            tsm = b.op("vector", lambda e: e.reduce_sum(
                                ssm[:], e_t[:], AX.X), waits=[tex], sem=V)
                            b.op("vector", lambda e: e.reciprocal(
                                ssm[:], ssm[:]), sem=V)
                            twb = b.op("vector", lambda e: e.tensor_scalar(
                                wbf[:], e_t[:], ssm[:, 0:1], None, ALU.mult),
                                waits=[prev.get("wbf")], sem=V)
                            prev["rout"] = tmk
                            prev["attn"] = tam
                            prev["am"] = tex
                            prev["e"] = twb
                            for sb in range(8):
                                ttr = b.op("tensor", lambda e, sb=sb: e.transpose(
                                    ps_t[:], wbf[:, sb * 128:(sb + 1) * 128],
                                    ident[:]), waits=[twb, t_id, tpp], sem=T)
                                tpp = b.op("vector", lambda e, sb=sb, tc=tc: e.tensor_copy(
                                    wT[:, sb * nt + tc * 128:
                                       sb * nt + tc * 128 + 128], ps_t[:]),
                                    waits=[ttr], sem=V)
                            prev["wbf"] = tpp
                        if wsd is not None:
                            b.dma("scalar",
                                  wsd.rearrange("(s p) t -> p s t", p=128),
                                  wT[:].rearrange("p (s t) -> p s t", t=nt),
                                  ST, waits=[tpp])
                STAGE_MARKS.append(('_st_f2', {e: len(b.q[e]) for e in ENGINES}))
                _st_f2()
                b.barrier()

            G = fctx.enter_context(nc.sbuf_tensor("G", [128, nt], F32))
            def _st_gate():
                with ExitStack() as gctx:
                    Eg = gctx.enter_context
                    grow = Eg(nc.sbuf_tensor("grow", [1, nt], F32))
                    lvrow = Eg(nc.sbuf_tensor("lvrow", [1, nt], F32))
                    lnrow = Eg(nc.sbuf_tensor("lnrow", [1, nt], F32))
                    ccrow2 = Eg(nc.sbuf_tensor("ccrow2", [1, 128], F32))
                    ps_G = Eg(nc.psum_tensor("ps_G", [128, 512], F32))
                    b.dma("sync", lvrow[0:1, :], rowscr[2:3, :], LD)
                    trw = b.dma("sync", lnrow[0:1, :], rowscr[3:4, :], LD)
                    tcr = b.dma("gpsimd", ccrow2[:], cc_out, CC2, waits=[tcc])
                    tmean = b.op("vector", lambda e: e.tensor_scalar(
                        sc_misc[0:1, 3:4], ccrow2[0:1, 0:1],
                        1.0 / (NCORE * nt), 1e-6, ALU.mult, ALU.add),
                        waits=[tcr, trw], sem=V)
                    tmr = b.op("vector", lambda e: e.reciprocal(
                        sc_misc[0:1, 3:4], sc_misc[0:1, 3:4]),
                        waits=[tmean], sem=V)
                    tnv = b.op("vector", lambda e: e.tensor_scalar(
                        lvrow[0:1, :], lvrow[0:1, :], sc_misc[0:1, 3:4], 0.5,
                        ALU.mult, ALU.mult), waits=[tmr, trw], sem=V)
                    tsg = b.op("scalar", lambda e: e.activation(
                        lnrow[0:1, :], lnrow[0:1, :], AF.Sigmoid),
                        waits=[trw], sem=SS)
                    b.op("vector", lambda e: e.scalar_tensor_tensor(
                        grow[0:1, :], lnrow[0:1, :], 2.5, lvrow[0:1, :],
                        ALU.mult, ALU.add), waits=[tsg, tnv], sem=V)
                    b.op("vector", lambda e: e.tensor_scalar(
                        grow[0:1, :], grow[0:1, :], 0.0, 5.0,
                        ALU.max, ALU.min), sem=V)
                    b.op("vector", lambda e: e.tensor_scalar(
                        grow[0:1, :], grow[0:1, :], 0.5, float(1.0 / 1.5),
                        ALU.subtract, ALU.mult), sem=V)
                    b.op("vector", lambda e: e.tensor_scalar(
                        grow[0:1, :], grow[0:1, :], 0.0, 1.0,
                        ALU.max, ALU.min), sem=V)
                    tga = b.op("vector", lambda e: e.tensor_scalar(
                        grow[0:1, :], grow[0:1, :], sc_misc[0:1, 1:2],
                        sc_misc[0:1, 2:3], ALU.mult, ALU.add),
                        waits=[t_gw1, t_gb], sem=V)
                    tgs = b.op("scalar", lambda e: e.activation(
                        grow[0:1, :], grow[0:1, :], AF.Sigmoid),
                        waits=[tga], sem=SS)
                    b.op("vector", lambda e: e.tensor_scalar(
                        lvrow[0:1, :], grow[0:1, :], 0.05, None, ALU.is_ge),
                        waits=[tgs], sem=V)
                    tgf = b.op("vector", lambda e: e.tensor_tensor(
                        grow[0:1, :], grow[0:1, :], lvrow[0:1, :], ALU.mult),
                        sem=V)
                    if g8 is not None:
                        b.dma("scalar", g8[0:1, :], grow[0:1, :], ST,
                              waits=[tgf])
                    gd = None
                    for k in range(nt // 512):
                        tbg = b.op("tensor", lambda e, k=k: e.matmul(
                            ps_G[:], ones1f[0:1, :],
                            grow[0:1, k * 512:(k + 1) * 512],
                            start=True, stop=True), waits=[tgf, gd], sem=T)
                        gd = b.op("scalar", lambda e, k=k: e.copy(
                            G[:, k * 512:(k + 1) * 512], ps_G[:]),
                            waits=[tbg], sem=SS)
                    return gd
            g_done = _st_gate()


            def _st_f3():
                with ExitStack() as sctx:
                    Es = sctx.enter_context
                    aux1 = Es(nc.sbuf_tensor("aux1", [128, HC * 512], BF16))
                    v1b = Es(nc.sbuf_tensor("v1b", [128, 2 * 512], BF16))
                    v2st = Es(nc.sbuf_tensor("v2st", [128, HC * 512], BF16))
                    ps_x = [Es(nc.psum_tensor(f"ps_x{i}", [128, 512], F32))
                            for i in range(2)]
                    pe_prev = {}
                    st_prev = None
                    for tt in range(ntt):
                        for hc in range(HC):
                            pw = hc % 2
                            for sc in range(8):
                                b.op("tensor", lambda e, pw=pw, hc=hc, sc=sc, tt=tt: e.matmul(
                                    ps_x[pw][:],
                                    av_sb[:, sc * D + hc * 128:
                                          sc * D + hc * 128 + 128],
                                    wT[:, sc * nt + tt * 512:
                                       sc * nt + tt * 512 + 512],
                                    start=(sc == 0), stop=(sc == 7)),
                                    waits=([tvu, g_done,
                                        pe_prev.get(("a", hc - 2)),
                                            st_prev] if sc == 0 else ()),
                                    sem=(T if sc == 7 else None))
                            mx = T.tok()
                            teva = b.op("scalar", lambda e, pw=pw, hc=hc: e.copy(
                                aux1[:, hc * 512:(hc + 1) * 512], ps_x[pw][:]),
                                waits=[mx], sem=SS)
                            pe_prev[("a", hc)] = teva
                        for vc in range(2):
                            pw = vc % 2
                            for hc in range(HC):
                                b.op("tensor", lambda e, pw=pw, vc=vc, hc=hc: e.matmul(
                                    ps_x[pw][:],
                                    wvd_sb[:, hc * VB + vc * 128:
                                           hc * VB + vc * 128 + 128],
                                    aux1[:, hc * 512:(hc + 1) * 512],
                                    start=(hc == 0), stop=(hc == HC - 1)),
                                    waits=([tvd, teva, pe_prev.get(("v", vc - 2))]
                                           if hc == 0 else ()),
                                    sem=(T if hc == HC - 1 else None))
                            mx = T.tok()
                            tevv = b.op("scalar", lambda e, pw=pw, vc=vc: e.copy(
                                v1b[:, vc * 512:(vc + 1) * 512], ps_x[pw][:]),
                                waits=[mx], sem=SS)
                            pe_prev[("v", vc)] = tevv
                        for hc in range(HC):
                            pw = hc % 2
                            for vc in range(2):
                                b.op("tensor", lambda e, pw=pw, hc=hc, vc=vc: e.matmul(
                                    ps_x[pw][:],
                                    wvu_sb[:, vc * D + hc * 128:
                                           vc * D + hc * 128 + 128],
                                    v1b[:, vc * 512:(vc + 1) * 512],
                                    start=(vc == 0), stop=(vc == 1)),
                                    waits=([tvu, tevv, pe_prev.get(("u", hc - 2))]
                                           if vc == 0 else ()),
                                    sem=(T if vc == 1 else None))
                            mx = T.tok()
                            tevu = b.op(
                                "vector",
                                lambda e, pw=pw, hc=hc, tt=tt: e.tensor_tensor(
                                    v2st[:, hc * 512:(hc + 1) * 512],
                                    ps_x[pw][:],
                                    G[:, tt * 512:(tt + 1) * 512],
                                    ALU.mult),
                                waits=[mx, g_done], sem=V)
                            pe_prev[("u", hc)] = tevu
                        st_prev = b.dma(
                            "scalar",
                            v2b.rearrange("(c p) t -> p c t", p=128)[
                                :, :, tt * 512:(tt + 1) * 512],
                            v2st[:].rearrange("p (c w) -> p c w", w=512),
                            ST, waits=[tevu])
                        for hc in range(HC):
                            pe_prev[("a", hc)] = st_prev
                            pe_prev[("u", hc)] = st_prev
            STAGE_MARKS.append(('_st_f3', {e: len(b.q[e]) for e in ENGINES}))
            _st_f3()
            b.barrier()

        def _st_f4():
            with ExitStack() as sctx:
                Es = sctx.enter_context
                TT4 = 512
                nt4 = nt // TT4
                v2t = Es(nc.sbuf_tensor("v2t", [128, HC * TT4], BF16))
                h2t4 = Es(nc.sbuf_tensor("h2t4", [128, HC * TT4], BF16))
                ft_ = [Es(nc.sbuf_tensor(f"ft{i}", [128, HC * TT4], F32))
                       for i in range(2)]
                fqt = Es(nc.sbuf_tensor("fqt", [128, 2 * TT4], BF16))
                rows = Es(nc.sbuf_tensor("rows4", [1, 2 * TT4], F32))
                ps_sum = Es(nc.psum_tensor("ps_sum4", [1, TT4], F32))
                ps_sq4 = Es(nc.psum_tensor("ps_sq4", [1, TT4], F32))
                ps_A = Es(nc.psum_tensor("ps_A4", [128, TT4], F32))
                ps_B = Es(nc.psum_tensor("ps_B4", [128, TT4], F32))
                prev_out = {}
                prev_t4 = None
                for it in range(nt4):
                    sl = slice(it * TT4, (it + 1) * TT4)
                    ft = ft_[it % 2]
                    lv = b.dma("sync",
                               v2t[:].rearrange("p (c w) -> p c w", w=TT4),
                               v2b.rearrange("(c p) t -> p c t", p=128)[
                                   :, :, sl],
                               LA if it % 2 == 0 else LB, waits=[prev_t4])
                    lh4 = b.dma("sync",
                                h2t4[:].rearrange("p (c w) -> p c w", w=TT4),
                                h2b.rearrange("(c p) t -> p c t", p=128)[
                                    :, :, sl],
                                LC, waits=[prev_t4])
                    fv = ft[:].rearrange("p (c w) -> p c w", w=TT4)
                    hv = h2t4[:].rearrange("p (c w) -> p c w", w=TT4)
                    vv = v2t[:].rearrange("p (c w) -> p c w", w=TT4)
                    t4 = b.op("vector",
                              lambda e, fv=fv, hv=hv, vv=vv: e.tensor_tensor(
                                  fv, hv, vv, ALU.add),
                              waits=[lv, lh4, prev_out.get(it - 2)], sem=V)
                    prev_t4 = t4
                    mm_toks = {}
                    stoks = stats_pass(ft, TT4, HC, None, fqt, ps_sum, ps_sq4,
                                       lambda c: t4, mm_toks)
                    rt = ln_rows(ps_sum, ps_sq4, rows, stoks, TT4)
                    bt = bcast2(ps_A, ps_B, rows, rt, TT4)
                    b.op("vector", lambda e, fv=fv: e.tensor_tensor(
                        fv, fv, rep(ps_A[:], HC, TT4), ALU.mult),
                        waits=bt, sem=V)
                    t7 = b.op("vector", lambda e, fv=fv: e.tensor_tensor(
                        fv, fv, rep(ps_B[:], HC, TT4), ALU.add), sem=V)
                    prev_out[it] = b.op(
                        "gpsimd",
                        lambda e, fv=fv, sl=sl: e.dma_start(
                            out=out_t.rearrange("(c p) t -> p c t", p=128)[
                                :, :, sl],
                            in_=fv, accum_op=ALU.add),
                        waits=[t7], sem=ACX[it % 2], inc=16, serial=False)

        STAGE_MARKS.append(('_st_f4', {e: len(b.q[e]) for e in ENGINES}))
        _st_f4()
        STAGE_MARKS.append(('end', {e: len(b.q[e]) for e in ENGINES}))
        b.wait("sync", ST.tok())
        b.wait("sync", CS.tok())
        b.wait("sync", CC.tok())
        b.wait("sync", CC2.tok())
        b.wait("sync", AC.tok())
        b.wait("sync", AC2.tok())
        b.wait("sync", SX[0].tok())
        b.wait("sync", SX[1].tok())

        with nc.Block() as block:
            b.finalize(block)
    return nc



def _prep_shared(inp):
    g = lambda k: np.ascontiguousarray(np.asarray(inp[k], dtype=np.float32))
    C = np.ascontiguousarray
    lao = g("L0_attn_local_approx_output")
    fuse = g("L0_attn_fuse_weight")
    d = {}
    d["upc"] = C(g("L0_attn_up_w").reshape(8, 128, HC, 128)
                 .transpose(0, 3, 2, 1))
    d["gatec"] = C(g("L0_attn_gate_w").reshape(8, 128, HC, 128)
                   .transpose(0, 3, 2, 1))
    WC = np.concatenate(
        [lao.T[None], fuse.transpose(2, 1, 0)], axis=0).reshape(NP, D)
    d["wcc"] = C(WC.reshape(2, 28, 128, HC, 128).transpose(0, 3, 2, 1, 4))
    d["fgc"] = C(g("L0_attn_ffn_gate_w").reshape(NF // 256, 256, HC, 128)
                 .transpose(0, 3, 2, 1))
    d["fuc"] = C(g("L0_attn_ffn_up_w").reshape(NF // 256, 256, HC, 128)
                 .transpose(0, 3, 2, 1))
    fdT = np.asarray(inp["L0_attn_ffn_down_w"], dtype=np.float32).T
    d["fdc"] = C(fdT.reshape(2, 39, 128, HC, 128).transpose(0, 3, 2, 1, 4))
    d["wqc"] = C(g("L0_ffn_Wq").reshape(BOT, HC, 128).transpose(2, 1, 0))
    d["WrT"] = C(g("L0_ffn_Wr").T)
    d["akT"] = C(g("L0_ffn_aux_keys").T)
    d["avc"] = C(g("L0_ffn_aux_values").reshape(8, 128, D).transpose(1, 0, 2))
    d["wvdc"] = C(g("L0_ffn_Wvd").reshape(VB, HC, 128).transpose(2, 1, 0))
    d["wvuc"] = C(g("L0_ffn_Wvu").reshape(D, 2, 128).transpose(2, 1, 0))
    d["wu1c"] = C(g("L0_ffn_Wu1").reshape(8, 128, HC, 128)
                  .transpose(0, 3, 2, 1))
    d["wu2c"] = C(g("L0_ffn_Wu2").reshape(8, 128).T)
    d["lp_c"] = C(g("L0_attn_local_point").reshape(8, 128).T)
    d["bu1_c"] = C(g("L0_ffn_bu1").reshape(8, 128).T)
    d["lr_row"] = g("L0_ffn_log_rel").reshape(1, NSLOT)
    d["d_row"] = g("L0_attn_discount_factor").reshape(1, 6)
    d["bu2_s"] = g("L0_ffn_bu2").reshape(1, 1)
    d["gw1_s"] = np.asarray(
        inp["L0_ffn_gate_w1"], dtype=np.float32).reshape(1, 1)
    d["gb_s"] = np.asarray(
        inp["L0_ffn_gate_bias"], dtype=np.float32).reshape(1, 1)
    return d


def kernel(**inputs):
    x = np.ascontiguousarray(
        np.asarray(inputs["L0_attn_hidden_states"], dtype=np.float32))
    _, seq, _ = x.shape
    nt = seq // NCORE
    shared = _prep_shared(inputs)
    xt = x[0].T
    nc = build_program(nt=nt)
    in_maps = []
    for c in range(NCORE):
        m = dict(shared)
        m["x_in"] = np.ascontiguousarray(xt[:, c * nt:(c + 1) * nt])
        in_maps.append(m)
    res = run_bass_kernel_spmd(nc, in_maps, core_ids=list(range(NCORE)))
    out = np.concatenate(
        [res.results[c]["out_t"].T for c in range(NCORE)], axis=0)
    return out.reshape(1, seq, D).astype(np.float32)


# version 14
# speedup vs baseline: 1.0095x; 1.0067x over previous
import numpy as np
from contextlib import ExitStack

import concourse.bass as bass
import concourse.mybir as mybir
import concourse.bass_utils as _bass_utils
from concourse.bass_utils import run_bass_kernel_spmd

if not getattr(_bass_utils, "_ldw_opt_patched", False):
    _orig_run_command = _bass_utils.run_command

    def _run_command_ldw(argv, **kwargs):
        argv = ["--enable-ldw-opt=true" if a == "--enable-ldw-opt=false"
                else a for a in argv]
        return _orig_run_command(argv, **kwargs)

    _bass_utils.run_command = _run_command_ldw
    _bass_utils._ldw_opt_patched = True

F32 = mybir.dt.float32
BF16 = mybir.dt.bfloat16
AF = mybir.ActivationFunctionType
ALU = mybir.AluOpType
AX = mybir.AxisListType

NCORE = 8
D = 4096
S = 1024
NF = 9984
NP = 7 * S
BOT = 64
RDIM = 48
NSLOT = 1024
VB = 256
HC = D // 128
EPS = 1e-5

ENGINES = ("tensor", "scalar", "vector", "gpsimd", "sync")
STAGE_MARKS = []
SERIAL = {"vector", "scalar", "gpsimd"}


class Sem:
    __slots__ = ("h", "n")

    def __init__(self, h):
        self.h = h
        self.n = 0

    def tok(self):
        return (self, self.n)


class Build:
    def __init__(self, nc, ctx):
        self.nc = nc
        self.q = {e: [] for e in ENGINES}
        self.last_wait = {e: {} for e in ENGINES}
        self.esem = {}
        self.all_sems = []
        self.ctx = ctx

    def sem(self, name, barrier=True):
        s = Sem(self.ctx.enter_context(self.nc.semaphore(name)))
        if barrier:
            self.all_sems.append(s)
        return s

    def wait(self, eng, tok):
        if tok is None:
            return
        sem, val = tok
        if val <= 0:
            return
        lw = self.last_wait[eng]
        if lw.get(id(sem), 0) >= val:
            return
        lw[id(sem)] = val
        self.q[eng].append(lambda e, s=sem.h, v=val: e.wait_ge(s, v))

    def op(self, eng, fn, waits=(), sem=None, inc=1, serial=True):
        if eng in SERIAL and serial:
            self.wait(eng, self.esem[eng].tok())
        for t in waits:
            self.wait(eng, t)
        if sem is not None:
            sem.n += inc
            self.q[eng].append(
                lambda e, f=fn, s=sem.h, k=inc: f(e).then_inc(s, k))
            return sem.tok()
        self.q[eng].append(fn)
        return None

    def dma(self, eng, out, in_, sem, waits=()):
        return self.op(eng, lambda e, o=out, i=in_: e.dma_start(out=o, in_=i),
                       waits=waits, sem=sem, inc=16, serial=False)

    def barrier(self, excl=()):
        toks = [s.tok() for s in self.all_sems if s.n > 0]
        for e in ENGINES:
            if e in excl:
                continue
            for t in toks:
                self.wait(e, t)

    def finalize(self, block):
        for name in ENGINES:
            fns = self.q[name]
            getattr(block, name)(lambda eng, fns=fns: [f(eng) for f in fns])


def rep(ap2d, n, w):
    return ap2d.rearrange("p (o w) -> p o w", o=1).to_broadcast(
        [ap2d.shape[0], n, w])


def build_program(nt=2048, dbg=()):
    assert nt % 512 == 0
    ntt = nt // 512
    nc = bass.Bass("TRN2", target_bir_lowering=False)
    dbg = set(dbg)
    STAGE_MARKS.clear()

    def dram(name, shape, dtype):
        kind = "ExternalOutput" if name in dbg else "Internal"
        return nc.dram_tensor(name, shape, dtype, kind=kind).ap()

    x_in = nc.dram_tensor("x_in", [D, nt], F32, kind="ExternalInput").ap()
    out_t = nc.dram_tensor("out_t", [D, nt], F32, kind="ExternalOutput").ap()
    wi = {}
    for name, shape in [
        ("upc", [8, 128, HC, 128]), ("gatec", [8, 128, HC, 128]),
        ("wcc", [2, HC, 128, 28, 128]),
        ("fgc", [NF // 256, 128, HC, 256]), ("fuc", [NF // 256, 128, HC, 256]),
        ("fdc", [2, HC, 128, 39, 128]),
        ("wqc", [128, HC, 64]), ("WrT", [BOT, RDIM]), ("akT", [BOT, NSLOT]),
        ("avc", [128, 8, D]), ("wvdc", [128, HC, VB]), ("wvuc", [128, 2, D]),
        ("wu1c", [8, 128, HC, 128]), ("wu2c", [128, 8]),
        ("lp_c", [128, 8]), ("bu1_c", [128, 8]), ("lr_row", [1, NSLOT]),
        ("d_row", [1, 6]), ("bu2_s", [1, 1]), ("gw1_s", [1, 1]),
        ("gb_s", [1, 1]),
    ]:
        wi[name] = nc.dram_tensor(name, shape, F32, kind="ExternalInput").ap()

    Pbuf = dram("Pbuf", [7, S, nt], BF16)
    pbuf = dram("pbuf", [NF, nt], BF16)
    aarr = dram("aarr", [D, nt], BF16)
    x2b = dram("x2b", [D, nt], F32)
    h2b = dram("h2b", [D, nt], BF16)
    v2b = dram("v2b", [D, nt], BF16)
    rowscr = dram("rowscr", [8, nt], F32)
    cc_in = dram("cc_in", [1, 128], F32)
    cc_out = dram("cc_out", [1, 128], F32)
    h1d = dram("h1d", [D, nt], BF16) if "h1d" in dbg else None
    wsd = dram("wsd", [NSLOT, nt], BF16) if "wsd" in dbg else None
    g8 = dram("g8", [8, nt], F32) if "g8" in dbg else None

    with ExitStack() as ctx:
        b = Build(nc, ctx)
        E = ctx.enter_context

        T = b.sem("T")
        SS = b.sem("SS")
        V = b.sem("V")
        GP = b.sem("GP")
        b.esem = {"vector": V, "scalar": SS, "gpsimd": GP}
        ST = b.sem("ST")
        CS = b.sem("CS")
        CC = b.sem("CC", barrier=False)
        CC2 = b.sem("CC2", barrier=False)
        WA = b.sem("WA")
        WB = b.sem("WB")
        LA = b.sem("LA")
        LB = b.sem("LB")
        LC = b.sem("LC")
        LD = b.sem("LD")
        G2 = b.sem("G2")
        G3 = b.sem("G3")
        LE = b.sem("LE")
        SA = b.sem("SA")
        SB = b.sem("SB")
        LF = b.sem("LF")
        CS0 = b.sem("CS0")
        CS1 = b.sem("CS1")
        CSU = b.sem("CSU")
        SR = b.sem("SR")
        AC = b.sem("AC")
        AC2 = b.sem("AC2")
        ACX = [AC, AC2]
        SX = [b.sem("SX0"), b.sem("SX1")]
        G4 = b.sem("G4")
        G5 = b.sem("G5")
        G6 = b.sem("G6")
        G7 = b.sem("G7")
        GRP = [LA, LB, G2, G3]
        GRPS = [[LA, LB, G2, G3], [G4, G5, G6, G7]]
        LEF = [LE, LF]

        ones1f = E(nc.sbuf_tensor("ones1f", [1, 128], F32))
        onesk = E(nc.sbuf_tensor("onesk", [128, 1], BF16))
        oneskf = E(nc.sbuf_tensor("oneskf", [128, 1], F32))
        ident = E(nc.sbuf_tensor("ident", [128, 128], BF16))
        lp_n = E(nc.sbuf_tensor("lp_n", [128, 8], F32))
        bu1_t = E(nc.sbuf_tensor("bu1_t", [128, 8], F32))
        wu2_t = E(nc.sbuf_tensor("wu2_t", [128, 8], BF16))
        E6 = E(nc.sbuf_tensor("E6", [128, 6], F32))
        sc_misc = E(nc.sbuf_tensor("sc_misc", [1, 32], F32))
        constt = E(nc.sbuf_tensor("constt", [1, 4], F32))
        b.op("gpsimd", lambda e: e.memset(constt[0:1, 0:1], EPS), sem=GP)
        b.op("gpsimd", lambda e: e.memset(constt[0:1, 1:2], 1.0), sem=GP)
        nc.const_aps.register(F32, EPS, constt[0:1, 0:1])
        nc.const_aps.register(F32, 1.0, constt[0:1, 1:2])

        t_lp = b.dma("sync", lp_n[:], wi["lp_c"], LD)
        t_bu1 = b.dma("sync", bu1_t[:], wi["bu1_c"], LD)
        t_wu2 = b.dma("gpsimd", wu2_t[:], wi["wu2c"], CS)
        t_d = b.dma("sync", sc_misc[0:1, 20:26], wi["d_row"], LD)
        t_bu2 = b.dma("sync", sc_misc[0:1, 0:1], wi["bu2_s"], LD)
        t_gw1 = b.dma("sync", sc_misc[0:1, 1:2], wi["gw1_s"], LD)
        t_gb = b.dma("sync", sc_misc[0:1, 2:3], wi["gb_s"], LD)
        tok_ld = t_gb
        t_o1 = b.op("gpsimd", lambda e: e.memset(ones1f[:], 1.0), sem=GP)
        t_ok = b.op("gpsimd", lambda e: e.memset(onesk[:], 1.0), sem=GP)
        t_okf = b.op("gpsimd", lambda e: e.memset(oneskf[:], 1.0), sem=GP)
        b.op("gpsimd", lambda e: e.memset(ident[:], 0.0), sem=GP)
        t_id = b.op("gpsimd", lambda e: e.affine_select(
            out=ident[:], in_=ident[:], compare_op=ALU.not_equal, fill=1.0,
            base=0, pattern=[[-1, 128]], channel_multiplier=1), sem=GP)
        t_lpn = b.op("vector", lambda e: e.tensor_scalar_mul(
            lp_n[:], lp_n[:], -1.0), waits=[tok_ld], sem=V)
        t_e6a = b.op("scalar", lambda e: e.activation(
            sc_misc[0:1, 8:14], sc_misc[0:1, 20:26], AF.Exp, scale=-1.0),
            waits=[tok_ld], sem=SS)
        with nc.psum_tensor("ps_set", [128, 8], F32) as ps_set:
            t_e6m = b.op("tensor", lambda e: e.matmul(
                ps_set[:, 0:6], ones1f[0:1, :], sc_misc[0:1, 8:14],
                start=True, stop=True), waits=[t_e6a, t_o1], sem=T)
            b.op("scalar", lambda e: e.copy(E6[:], ps_set[:, 0:6]),
                 waits=[t_e6m], sem=SS)
        b.barrier()

        def ln_rows(ps_sum, ps_sq, rows, wtoks, w):
            mean = rows[0:1, 0:w]
            r = rows[0:1, w:2 * w]
            b.op("vector", lambda e: e.tensor_scalar_mul(
                mean, ps_sum[0:1, 0:w], 1.0 / D), waits=wtoks, sem=V)
            b.op("vector", lambda e: e.tensor_scalar_mul(
                r, ps_sq[0:1, 0:w], 1.0 / D), waits=wtoks, sem=V)
            b.op("vector", lambda e: e.tensor_tensor(
                ps_sq[0:1, 0:w], mean, mean, ALU.mult), sem=V)
            tvar = b.op("vector", lambda e: e.tensor_tensor(
                r, r, ps_sq[0:1, 0:w], ALU.subtract), sem=V)
            b.op("scalar", lambda e: e.activation(
                r, r, AF.Sqrt, bias=EPS), waits=[tvar], sem=SS)
            trc = b.op("vector", lambda e: e.reciprocal(r, r),
                       waits=[SS.tok()], sem=V)
            tnm = b.op("vector", lambda e: e.scalar_tensor_tensor(
                mean, mean, -1.0, r, ALU.mult, ALU.mult), sem=V)
            return [trc, tnm, tvar]

        def bcast2(ps_a, ps_b, rows, wtoks, w, dst_a=None, dst_b=None):
            ta = b.op("tensor", lambda e: e.matmul(
                ps_a[:, 0:w], ones1f[0:1, :], rows[0:1, w:2 * w],
                start=True, stop=True), waits=wtoks, sem=T)
            tb = b.op("tensor", lambda e: e.matmul(
                ps_b[:, 0:w], ones1f[0:1, :], rows[0:1, 0:w],
                start=True, stop=True), sem=T)
            return [ta, tb]

        def stats_pass(src_tile, w, HCn, cbt, cqt, ps_sum, ps_sq, ld_tok,
                       mm_toks, extra_w=(), src_bf=False):
            for c in range(HCn):
                pc = c % 2
                sq_c = cqt[:, pc * w:(pc + 1) * w]
                tsq = b.op("scalar", lambda e, sq_c=sq_c, c=c: e.activation(
                    sq_c, src_tile[:, c * w:(c + 1) * w], AF.Square),
                    waits=[ld_tok(c), mm_toks.get(("q", c - 2))] + list(extra_w),
                    sem=SS)
                tm1 = b.op("tensor", lambda e, c=c: e.matmul(
                    ps_sum[:, 0:w],
                    (onesk if src_bf else oneskf)[:, 0:1],
                    src_tile[:, c * w:(c + 1) * w],
                    start=(c == 0), stop=(c == HCn - 1)),
                    waits=[ld_tok(c), t_okf, t_ok] + list(extra_w),
                    sem=T)
                tm2 = b.op("tensor", lambda e, sq_c=sq_c, c=c: e.matmul(
                    ps_sq[:, 0:w], onesk[:, 0:1], sq_c,
                    start=(c == 0), stop=(c == HCn - 1)),
                    waits=[tsq], sem=T)
                mm_toks[("s", c)] = tm1
                mm_toks[("q", c)] = tm2
            return [tm1, tm2]

        with ExitStack() as hctx:
            h1sb = hctx.enter_context(
                nc.sbuf_tensor("h1sb", [128, HC * nt], BF16))

            def _st_ln1():
                with ExitStack() as sctx:
                    Es = sctx.enter_context
                    TL = 256
                    ntl = nt // TL
                    xt_ = [Es(nc.sbuf_tensor(f"xt1_{i}", [128, HC * TL], F32))
                           for i in range(2)]
                    cq_ = [Es(nc.sbuf_tensor(f"cq1_{i}", [128, 2 * TL], BF16))
                           for i in range(2)]
                    rows0 = Es(nc.sbuf_tensor("rw1", [1, 2 * TL], F32))
                    rows_ = [rows0, rows0]
                    ps_sum_ = [Es(nc.psum_tensor(f"pss1{i}", [1, TL], F32))
                               for i in range(2)]
                    ps_sq_ = [Es(nc.psum_tensor(f"psq1{i}", [1, TL], F32))
                              for i in range(2)]
                    ps_A_ = [Es(nc.psum_tensor(f"psA1{i}", [128, TL], F32))
                             for i in range(2)]
                    ps_B_ = [Es(nc.psum_tensor(f"psB1{i}", [128, TL], F32))
                             for i in range(2)]
                    done = {}
                    for it in range(ntl):
                        j = it % 2
                        gt = {}
                        for c in range(HC):
                            gt[c // 8] = b.dma(
                                "sync", xt_[j][:, c * TL:(c + 1) * TL],
                                x_in[c * 128:(c + 1) * 128,
                                     it * TL:(it + 1) * TL],
                                GRPS[j][c // 8], waits=[done.get(it - 2)])
                        mm_toks = {}
                        stoks = stats_pass(
                            xt_[j], TL, HC, None, cq_[j], ps_sum_[j],
                            ps_sq_[j], lambda c, gt=gt: gt[c // 8], mm_toks,
                            extra_w=[done.get(it - 2)] if it >= 2 else ())
                        rt = ln_rows(ps_sum_[j], ps_sq_[j], rows_[j], stoks, TL)
                        bt = bcast2(ps_A_[j], ps_B_[j], rows_[j], rt, TL)
                        xv = xt_[j][:].rearrange("p (c w) -> p c w", w=TL)
                        h1v = h1sb[:].rearrange("p (c t) -> p c t", t=nt)[
                            :, :, it * TL:(it + 1) * TL]
                        b.op("vector", lambda e, xv=xv, j=j: e.tensor_tensor(
                            xv, xv, rep(ps_A_[j][:], HC, TL), ALU.mult),
                            waits=bt + list(gt.values()), sem=V)
                        tap2 = b.op("vector",
                                    lambda e, xv=xv, h1v=h1v, j=j: e.tensor_tensor(
                                        h1v, xv, rep(ps_B_[j][:], HC, TL),
                                        ALU.add), sem=V)
                        done[it] = tap2
                    prev_apply = tap2
                    if h1d is not None:
                        b.dma("scalar",
                              h1d.rearrange("(c p) t -> p c t", p=128),
                              h1sb[:].rearrange("p (c t) -> p c t", t=nt),
                              ST, waits=[prev_apply])
                    return prev_apply
            STAGE_MARKS.append(('_st_ln1', {e: len(b.q[e]) for e in ENGINES}))
            ln1_done = _st_ln1()
            b.barrier(excl=("gpsimd",))

            def _st_ab():
                with ExitStack() as sctx:
                    Es = sctx.enter_context
                    wup = Es(nc.sbuf_tensor("wup", [128, HC * 128], BF16))
                    wgt = Es(nc.sbuf_tensor("wgt", [128, HC * 128], BF16))
                    Uj = Es(nc.sbuf_tensor("Uj", [128, nt], F32))
                    dj = Es(nc.sbuf_tensor("dj", [128, nt], F32))
                    keepj = Es(nc.sbuf_tensor("keepj", [128, nt], F32))
                    Ra = Es(nc.sbuf_tensor("Ra", [128, nt], F32))
                    Rb = Es(nc.sbuf_tensor("Rb", [128, nt], F32))
                    Pst = [Es(nc.sbuf_tensor(f"Pst{i}", [128, nt], BF16))
                           for i in range(2)]
                    U_bf = Es(nc.sbuf_tensor("U_bf", [128, nt], BF16))
                    ps_U = Es(nc.psum_tensor("ps_U", [128, ntt * 512], F32))
                    ps_d = Es(nc.psum_tensor("ps_d", [128, ntt * 512], F32))
                    pass
                    pass
                    wup_use = wgt_use = ln1_done
                    evU_prev = evd_prev = None
                    uj_free = []
                    dj_free = None
                    pst_tok = [None, None]
                    tab_wait = None
                    st0_prev = None
                    for sj in range(8):
                        wup_tok = b.dma(
                            "gpsimd", wup[:].rearrange("p (c j) -> p c j", j=128),
                            wi["upc"][sj], WA,
                            waits=[wup_use])
                        wgt_tok = b.dma(
                            "gpsimd", wgt[:].rearrange("p (c j) -> p c j", j=128),
                            wi["gatec"][sj], WB,
                            waits=[wgt_use])
                        for tt in range(ntt):
                            for c in range(HC):
                                last = (tt == ntt - 1 and c == HC - 1)
                                b.op("tensor", lambda e, tt=tt, c=c: e.matmul(
                                    ps_U[:, tt * 512:(tt + 1) * 512],
                                    wup[:, c * 128:(c + 1) * 128],
                                    h1sb[:, c * nt + tt * 512:
                                         c * nt + tt * 512 + 512],
                                    start=(c == 0), stop=(c == HC - 1)),
                                    waits=([wup_tok, evU_prev]
                                           if (tt == 0 and c == 0) else ()),
                                    sem=(T if last else None))
                        mmU = T.tok()
                        wup_use = mmU
                        for tt in range(ntt):
                            for c in range(HC):
                                last = (tt == ntt - 1 and c == HC - 1)
                                b.op("tensor", lambda e, tt=tt, c=c: e.matmul(
                                    ps_d[:, tt * 512:(tt + 1) * 512],
                                    wgt[:, c * 128:(c + 1) * 128],
                                    h1sb[:, c * nt + tt * 512:
                                         c * nt + tt * 512 + 512],
                                    start=(c == 0), stop=(c == HC - 1)),
                                    waits=([wgt_tok, evd_prev]
                                           if (tt == 0 and c == 0) else ()),
                                    sem=(T if last else None))
                        mmD = T.tok()
                        wgt_use = mmD
                        evU = b.op("scalar", lambda e: e.copy(Uj[:], ps_U[:]),
                                   waits=[mmU] + uj_free, sem=SS)
                        evU_prev = evU
                        evd = b.op("scalar", lambda e, sj=sj: e.activation(
                            dj[:], ps_d[:], AF.Identity, bias=lp_n[:, sj:sj + 1]),
                            waits=[mmD, dj_free, t_lpn], sem=SS)
                        evd_prev = evd
                        tab = b.op("scalar", lambda e: e.activation(
                            Ra[:], dj[:], AF.Abs), waits=[tab_wait], sem=SS)
                        tke = b.op("vector", lambda e: e.tensor_scalar(
                            keepj[:], Ra[:], 2.5, None, ALU.is_le),
                            waits=[tab], sem=V)
                        evU2 = b.op("scalar", lambda e: e.copy(
                            U_bf[:], Uj[:]), waits=[st0_prev], sem=SS)
                        st0 = b.dma("sync",
                                    Pbuf[0, sj * 128:(sj + 1) * 128, :],
                                    U_bf[:], CSU, waits=[evU2])
                        st0_prev = st0
                        src = Uj
                        Rd = [Ra, Rb]
                        tr1 = None
                        for m in range(1, 7):
                            dst = Rd[m % 2]
                            tr = b.op("vector", lambda e, s=src, d=dst: e.tensor_tensor(
                                d[:], s[:], dj[:], ALU.mult),
                                waits=[evU, evd], sem=V)
                            if m == 1:
                                tr1 = tr
                            pc = m % 2
                            tp = b.op("vector", lambda e, d=dst, m=m, pc=pc: e.tensor_scalar(
                                Pst[pc][:], d[:], E6[:, m - 1:m], None, ALU.mult),
                                waits=[pst_tok[pc]], sem=V)
                            if m >= 5:
                                tp = b.op("vector", lambda e, pc=pc: e.tensor_tensor(
                                    Pst[pc][:], Pst[pc][:], keepj[:], ALU.mult),
                                    sem=V)
                            if m == 6:
                                tab_wait = tp
                            pst_tok[pc] = b.dma(
                                "sync", Pbuf[m, sj * 128:(sj + 1) * 128, :],
                                Pst[pc][:], (CS0 if pc == 0 else CS1),
                                waits=[tp])
                            src = dst
                        uj_free = [tr1]
                        dj_free = tp
            STAGE_MARKS.append(('_st_ab', {e: len(b.q[e]) for e in ENGINES}))
            _st_ab()
            b.barrier()

            def _st_d():
                with ExitStack() as sctx:
                    Es = sctx.enter_context
                    wg_ = [Es(nc.sbuf_tensor(f"wgD{i}", [128, HC * 256], BF16))
                           for i in range(2)]
                    wu_ = [Es(nc.sbuf_tensor(f"wuD{i}", [128, HC * 256], BF16))
                           for i in range(2)]
                    sg_ = [Es(nc.sbuf_tensor(f"sgD{i}", [128, 512], F32))
                           for i in range(2)]
                    pb_ = [Es(nc.sbuf_tensor(f"pbD{i}", [128, 512], BF16))
                           for i in range(2)]
                    ps_g = [Es(nc.psum_tensor(f"ps_g{i}", [128, 512], F32))
                            for i in range(2)]
                    ps_u = [Es(nc.psum_tensor(f"ps_u{i}", [128, 512], F32))
                            for i in range(2)]
                    pass
                    pass
                    NFG = NF // 256
                    wuse = {}
                    ev_toks = {}
                    st_toks = {}
                    for fg in range(NFG):
                        pw = fg % 2
                        wsem = WA if pw == 0 else WB
                        wg_tok = b.dma(
                            "gpsimd",
                            wg_[pw][:].rearrange("p (c j) -> p c j", j=256),
                            wi["fgc"][fg], wsem,
                            waits=[wuse.get(fg - 2)])
                        wu_tok = b.dma(
                            "gpsimd",
                            wu_[pw][:].rearrange("p (c j) -> p c j", j=256),
                            wi["fuc"][fg], wsem,
                            waits=[wuse.get(fg - 2)])
                        for j in range(2):
                            for tt in range(ntt):
                                un = (fg * 2 + j) * ntt + tt
                                ui = un % 2
                                prev_ev = ev_toks.get(un - 2)
                                for c in range(HC):
                                    b.op("tensor", lambda e, pw=pw, j=j, tt=tt, c=c, ui=ui: e.matmul(
                                        ps_g[ui][:],
                                        wg_[pw][:, c * 256 + j * 128:
                                                c * 256 + j * 128 + 128],
                                        h1sb[:, c * nt + tt * 512:
                                             c * nt + tt * 512 + 512],
                                        start=(c == 0), stop=(c == HC - 1)),
                                        waits=([wu_tok, prev_ev] if c == 0 else ()))
                                for c in range(HC):
                                    last = c == HC - 1
                                    b.op("tensor", lambda e, pw=pw, j=j, tt=tt, c=c, ui=ui: e.matmul(
                                        ps_u[ui][:],
                                        wu_[pw][:, c * 256 + j * 128:
                                                c * 256 + j * 128 + 128],
                                        h1sb[:, c * nt + tt * 512:
                                             c * nt + tt * 512 + 512],
                                        start=(c == 0), stop=(c == HC - 1)),
                                        waits=([wu_tok, prev_ev] if c == 0 else ()),
                                        sem=(T if last else None))
                                mmt = T.tok()
                                if j == 1 and tt == ntt - 1:
                                    wuse[fg] = mmt
                                tsg = b.op("scalar", lambda e, ui=ui: e.activation(
                                    sg_[ui][:], ps_g[ui][:], AF.Silu),
                                    waits=[mmt, st_toks.get(un - 2)], sem=SS)
                                tpb = b.op("vector", lambda e, ui=ui: e.tensor_tensor(
                                    pb_[ui][:], sg_[ui][:], ps_u[ui][:], ALU.mult),
                                    waits=[tsg], sem=V)
                                ev_toks[un] = tpb
                                frow = fg * 256 + j * 128
                                st_toks[un] = b.dma(
                                    "scalar",
                                    pbuf[frow:frow + 128,
                                         tt * 512:(tt + 1) * 512],
                                    pb_[ui][:], (SA if ui == 0 else SB),
                                    waits=[tpb])
            STAGE_MARKS.append(('_st_d', {e: len(b.q[e]) for e in ENGINES}))
            _st_d()
            b.barrier()

        def _st_c():
            with ExitStack() as sctx:
                Es = sctx.enter_context
                Ptile = Es(nc.sbuf_tensor("Ptile", [128, 28 * nt], BF16))
                wc_ = [Es(nc.sbuf_tensor(f"wcC{i}", [128, 28 * 128], BF16))
                       for i in range(2)]
                evb = [Es(nc.sbuf_tensor(f"evC{i}", [128, nt], BF16))
                       for i in range(2)]
                ain = [Es(nc.sbuf_tensor(f"ainC{i}", [128, nt], BF16))
                       for i in range(2)]
                ps_C = [Es(nc.psum_tensor(f"ps_C{i}", [128, ntt * 512], F32))
                        for i in range(2)]
                pass
                carry = {}
                for ph in range(2):
                    koff = ph * 28
                    gtok = {}
                    for ci in range(28):
                        cg = koff + ci
                        m, sc = divmod(cg, 8)
                        gtok[ci // 7] = b.dma(
                            "sync", Ptile[:, ci * nt:(ci + 1) * nt],
                            Pbuf[m, sc * 128:(sc + 1) * 128, :],
                            GRP[ci // 7],
                            waits=[carry.get("mm")])
                    wtok = {}
                    wuse = {-2: carry.get("mm"), -1: carry.get("mm")}
                    tevs = {-2: carry.get("ev0"), -1: carry.get("ev1")}
                    stp = {-2: carry.get("st0"), -1: carry.get("st1")}
                    for h in range(HC):
                        pw = h % 2
                        wtok[h] = b.dma(
                            "gpsimd",
                            wc_[pw][:].rearrange("p (c j) -> p c j", j=128),
                            wi["wcc"][ph, h],
                            WA if pw == 0 else WB, waits=[wuse.get(h - 2)])
                        if ph == 1:
                            ld_a = b.dma("sync", ain[pw][:],
                                         aarr[h * 128:(h + 1) * 128, :],
                                         (LC if pw == 0 else LF),
                                         waits=[tevs.get(h - 2)])
                        for ci in range(28):
                            for tt in range(ntt):
                                last = (ci == 27 and tt == ntt - 1)
                                b.op("tensor", lambda e, pw=pw, ci=ci, tt=tt: e.matmul(
                                    ps_C[pw][:, tt * 512:(tt + 1) * 512],
                                    wc_[pw][:, ci * 128:(ci + 1) * 128],
                                    Ptile[:, ci * nt + tt * 512:
                                          ci * nt + tt * 512 + 512],
                                    start=(ci == 0), stop=(ci == 27)),
                                    waits=(([wtok[h], tevs.get(h - 2),
                                             gtok[ci // 7]]
                                            if h < 2 else
                                            [wtok[h], tevs.get(h - 2)])
                                           if (ci % 7 == 0 and tt == 0)
                                           else ()),
                                    sem=(T if last else None))
                        mmt = T.tok()
                        wuse[h] = mmt
                        if ph == 0:
                            tev = b.op("scalar", lambda e, pw=pw: e.copy(
                                evb[pw][:], ps_C[pw][:]),
                                waits=[mmt, stp.get(h - 2)], sem=SS)
                        else:
                            tev = b.op("vector", lambda e, pw=pw: e.tensor_tensor(
                                evb[pw][:], ain[pw][:], ps_C[pw][:], ALU.add),
                                waits=[mmt, ld_a, stp.get(h - 2)], sem=V)
                        tevs[h] = tev
                        stp[h] = b.dma(
                            "scalar", aarr[h * 128:(h + 1) * 128, :],
                            evb[pw][:], (SA if pw == 0 else SB), waits=[tev])
                    carry = {"mm": wuse[HC - 1], "ev0": tevs[HC - 2],
                             "ev1": tevs[HC - 1], "st0": stp[HC - 2],
                             "st1": stp[HC - 1]}
        STAGE_MARKS.append(('_st_c', {e: len(b.q[e]) for e in ENGINES}))
        _st_c()
        b.barrier()

        def _st_e():
            with ExitStack() as sctx:
                Es = sctx.enter_context
                ptile = Es(nc.sbuf_tensor("ptile", [128, 39 * nt], BF16))
                wd_ = [Es(nc.sbuf_tensor(f"wdE{i}", [128, 39 * 128], BF16))
                       for i in range(2)]
                evb = [Es(nc.sbuf_tensor(f"evE{i}", [128, nt], BF16))
                       for i in range(2)]
                ain0 = Es(nc.sbuf_tensor("ainE0", [128, nt], BF16))
                ain = [ain0, ain0]
                ps_E = [Es(nc.psum_tensor(f"ps_E{i}", [128, ntt * 512], F32))
                        for i in range(2)]
                pass
                carry = {}
                for ph in range(2):
                    koff = ph * 39
                    gtok = {}
                    for ci in range(39):
                        fr = (koff + ci) * 128
                        gtok[min(ci // 10, 3)] = b.dma(
                            "sync", ptile[:, ci * nt:(ci + 1) * nt],
                            pbuf[fr:fr + 128, :],
                            GRP[min(ci // 10, 3)],
                            waits=[carry.get("mm")])
                    wtok = {}
                    wuse = {-2: carry.get("mm"), -1: carry.get("mm")}
                    tevs = {-2: carry.get("ev0"), -1: carry.get("ev1")}
                    stp = {-2: carry.get("st0"), -1: carry.get("st1")}
                    tev = carry.get("ev1")
                    for h in range(HC):
                        pw = h % 2
                        wtok[h] = b.dma(
                            "gpsimd",
                            wd_[pw][:].rearrange("p (c j) -> p c j", j=128),
                            wi["fdc"][ph, h],
                            WA if pw == 0 else WB, waits=[wuse.get(h - 2)])
                        ld_a = b.dma("sync", ain[pw][:],
                                     aarr[h * 128:(h + 1) * 128, :], LC,
                                     waits=[tev])
                        for ci in range(39):
                            for tt in range(ntt):
                                last = (ci == 38 and tt == ntt - 1)
                                b.op("tensor", lambda e, pw=pw, ci=ci, tt=tt: e.matmul(
                                    ps_E[pw][:, tt * 512:(tt + 1) * 512],
                                    wd_[pw][:, ci * 128:(ci + 1) * 128],
                                    ptile[:, ci * nt + tt * 512:
                                          ci * nt + tt * 512 + 512],
                                    start=(ci == 0), stop=(ci == 38)),
                                    waits=(([wtok[h], tevs.get(h - 2),
                                             gtok[min(ci // 10, 3)]]
                                            if h < 2 else
                                            [wtok[h], tevs.get(h - 2)])
                                           if (ci % 10 == 0 and tt == 0)
                                           else ()),
                                    sem=(T if last else None))
                        mmt = T.tok()
                        wuse[h] = mmt
                        tev = b.op("vector", lambda e, pw=pw: e.tensor_tensor(
                            evb[pw][:], ain[pw][:], ps_E[pw][:], ALU.add),
                            waits=[mmt, ld_a, stp.get(h - 2)], sem=V)
                        tevs[h] = tev
                        stp[h] = b.dma(
                            "scalar", aarr[h * 128:(h + 1) * 128, :],
                            evb[pw][:], (SA if pw == 0 else SB), waits=[tev])
                    carry = {"mm": wuse[HC - 1], "ev0": tevs[HC - 2],
                             "ev1": tevs[HC - 1], "st0": stp[HC - 2],
                             "st1": stp[HC - 1]}
        STAGE_MARKS.append(('_st_e', {e: len(b.q[e]) for e in ENGINES}))
        _st_e()
        b.barrier()

        def _st_ln23():
            with ExitStack() as sctx:
                Es = sctx.enter_context
                TL = 256
                ntl = nt // TL
                at_ = [Es(nc.sbuf_tensor(f"at2_{i}", [128, HC * TL], BF16))
                       for i in range(2)]
                A2s = [Es(nc.sbuf_tensor(f"A2s{i}", [128, TL], BF16))
                       for i in range(2)]
                B2s = [Es(nc.sbuf_tensor(f"B2s{i}", [128, TL], BF16))
                       for i in range(2)]
                xt_ = [Es(nc.sbuf_tensor(f"xt2_{i}", [128, HC * TL], F32))
                       for i in range(2)]
                h2_ = [Es(nc.sbuf_tensor(f"h2t_{i}", [128, HC * TL], BF16))
                       for i in range(2)]
                cq_ = [Es(nc.sbuf_tensor(f"cq2_{i}", [128, 2 * TL], BF16))
                       for i in range(2)]
                rows0 = Es(nc.sbuf_tensor("rw2", [1, 2 * TL], F32))
                rows_ = [rows0, rows0]
                ps_sum_ = [Es(nc.psum_tensor(f"pss2{i}", [1, TL], F32))
                           for i in range(2)]
                ps_sq_ = [Es(nc.psum_tensor(f"psq2{i}", [1, TL], F32))
                          for i in range(2)]
                ps_A_ = [Es(nc.psum_tensor(f"psA2{i}", [128, TL], F32))
                         for i in range(2)]
                ps_B_ = [Es(nc.psum_tensor(f"psB2{i}", [128, TL], F32))
                         for i in range(2)]
                done = {}
                h2done = {}
                prev_str = None
                for it in range(ntl):
                    j = it % 2
                    gt = {}
                    lxl = None
                    for c in range(HC):
                        gt[c // 8] = b.dma(
                            "sync", at_[j][:, c * TL:(c + 1) * TL],
                            aarr[c * 128:(c + 1) * 128,
                                 it * TL:(it + 1) * TL],
                            GRPS[j][c // 8], waits=[done.get(it - 2)])
                        lxl = b.dma(
                            "sync", xt_[j][:, c * TL:(c + 1) * TL],
                            x_in[c * 128:(c + 1) * 128,
                                 it * TL:(it + 1) * TL],
                            LEF[j], waits=[done.get(it - 2)])
                    mm_toks = {}
                    stoks = stats_pass(
                        at_[j], TL, HC, None, cq_[j], ps_sum_[j], ps_sq_[j],
                        lambda c, gt=gt: gt[c // 8], mm_toks,
                        extra_w=[done.get(it - 2)] if it >= 2 else (),
                        src_bf=True)
                    rt = ln_rows(ps_sum_[j], ps_sq_[j], rows_[j], stoks, TL)
                    bt = bcast2(ps_A_[j], ps_B_[j], rows_[j], rt, TL)
                    tca = b.op("scalar", lambda e, j=j: e.copy(
                        A2s[j][:], ps_A_[j][:, 0:TL]), waits=[bt[0]], sem=SS)
                    tcb = b.op("scalar", lambda e, j=j: e.copy(
                        B2s[j][:], ps_B_[j][:, 0:TL]), waits=[bt[1]], sem=SS)
                    av = at_[j][:].rearrange("p (c w) -> p c w", w=TL)
                    xv = xt_[j][:].rearrange("p (c w) -> p c w", w=TL)
                    b.op("vector", lambda e, av=av, j=j: e.tensor_tensor(
                        av, av, rep(A2s[j][:], HC, TL), ALU.mult),
                        waits=[tca, tcb] + list(gt.values()), sem=V)
                    b.op("vector", lambda e, av=av, j=j: e.tensor_tensor(
                        av, av, rep(B2s[j][:], HC, TL), ALU.add), sem=V)
                    t3 = b.op("vector", lambda e, av=av, xv=xv: e.tensor_tensor(
                        xv, xv, av, ALU.add),
                        waits=[lxl], sem=V)
                    stx2 = b.dma(
                        "scalar",
                        out_t.rearrange("(c p) t -> p c t", p=128)[
                            :, :, it * TL:(it + 1) * TL],
                        xv, SX[j], waits=[t3])
                    mm_toks2 = {}
                    stoks = stats_pass(xt_[j], TL, HC, None, cq_[j],
                                       ps_sum_[j], ps_sq_[j],
                                       lambda c: t3, mm_toks2)
                    rt = ln_rows(ps_sum_[j], ps_sq_[j], rows_[j], stoks, TL)
                    bt = bcast2(ps_A_[j], ps_B_[j], rows_[j], rt, TL)
                    b.op("vector", lambda e, av=av, xv=xv, j=j: e.tensor_tensor(
                        av, xv, rep(ps_A_[j][:], HC, TL), ALU.mult),
                        waits=bt, sem=V)
                    h2v = h2_[j][:].rearrange("p (c w) -> p c w", w=TL)
                    t5 = b.op("vector",
                              lambda e, av=av, h2v=h2v, j=j: e.tensor_tensor(
                                  h2v, av, rep(ps_B_[j][:], HC, TL), ALU.add),
                              waits=[h2done.get(it - 2)], sem=V)
                    sth2 = b.dma(
                        "scalar",
                        h2b.rearrange("(c p) t -> p c t", p=128)[
                            :, :, it * TL:(it + 1) * TL],
                        h2v, SX[j], waits=[t5])
                    h2done[it] = sth2
                    done[it] = sth2
        STAGE_MARKS.append(('_st_ln23', {e: len(b.q[e]) for e in ENGINES}))
        _st_ln23()
        b.barrier()

        with ExitStack() as fctx:
            wT = fctx.enter_context(
                nc.sbuf_tensor("wT", [128, 8 * nt], BF16))
            av_sb = fctx.enter_context(
                nc.sbuf_tensor("av_sb", [128, 8 * D], BF16))
            wvd_sb = fctx.enter_context(
                nc.sbuf_tensor("wvd_sb", [128, HC * VB], BF16))
            wvu_sb = fctx.enter_context(
                nc.sbuf_tensor("wvu_sb", [128, 2 * D], BF16))
            pass
            pass
            pass
            tav = b.dma("gpsimd",
                        av_sb[:].rearrange("p (s h) -> p s h", h=D),
                        wi["avc"], CS)
            tvd = b.dma("gpsimd",
                        wvd_sb[:].rearrange("p (c m) -> p c m", m=VB),
                        wi["wvdc"], CS)
            tvu = b.dma("gpsimd",
                        wvu_sb[:].rearrange("p (c m) -> p c m", m=D),
                        wi["wvuc"], CS)
            with ExitStack() as qctx:
                q_bf = qctx.enter_context(
                    nc.sbuf_tensor("q_bf", [BOT, nt], BF16))
                q_bf8 = qctx.enter_context(
                    nc.sbuf_tensor("q_bf8", [BOT, nt], BF16))

                def _st_f1():
                    with ExitStack() as sctx:
                        Es = sctx.enter_context
                        h2tile = Es(nc.sbuf_tensor("h2tile", [128, HC * 512], BF16))
                        wq_t = Es(nc.sbuf_tensor("wq_t", [128, HC * 64], BF16))
                        wu1_ = [Es(nc.sbuf_tensor(f"wu1_{i}", [128, HC * 128], BF16))
                                for i in range(2)]
                        uub = Es(nc.sbuf_tensor("uub", [128, 512], BF16))
                        sqf = Es(nc.sbuf_tensor("sqf", [128, 2 * 512], BF16))
                        rows = Es(nc.sbuf_tensor("rows3", [1, 1024], F32))
                        lsl = Es(nc.sbuf_tensor("lsl", [1, 512], F32))
                        ccrow = Es(nc.sbuf_tensor("ccrow", [1, 128], F32))
                        ps_q = Es(nc.psum_tensor("ps_q", [64, 512], F32))
                        ps_uu = [Es(nc.psum_tensor(f"ps_uu{i}", [128, 512], F32))
                                 for i in range(2)]
                        ps_le = Es(nc.psum_tensor("ps_le", [1, 512], F32))
                        ps_sum = Es(nc.psum_tensor("ps_sum3", [1, 512], F32))
                        ps_sq3 = Es(nc.psum_tensor("ps_sq3", [1, 512], F32))
                        pass
                        pass
                        twq = b.dma("gpsimd",
                                    wq_t[:].rearrange("p (c j) -> p c j", j=64),
                                    wi["wqc"], CS)
                        wu1_use = {}
                        uu_prev = None
                        tl_prev = None
                        prev_lvst = None
                        prev_tv = None
                        prev_sq = None
                        tz = b.op("gpsimd", lambda e: e.memset(
                            ccrow[:], 0.0), sem=GP)
                        prev_str = None
                        for it in range(ntt):
                            lh = b.dma(
                                "sync",
                                h2tile[:].rearrange("p (c w) -> p c w", w=512),
                                h2b.rearrange("(c p) t -> p c t", p=128)[
                                    :, :, it * 512:(it + 1) * 512],
                                LA if it % 2 == 0 else LB,
                                waits=[wu1_use.get((it - 1, 7))])
                            for c in range(HC):
                                b.op("tensor", lambda e, c=c: e.matmul(
                                    ps_q[:], wq_t[:, c * 64:(c + 1) * 64],
                                    h2tile[:, c * 512:(c + 1) * 512],
                                    start=(c == 0), stop=(c == HC - 1)),
                                    waits=([twq, lh, prev_sq] if c == 0 else ()),
                                    sem=(T if c == HC - 1 else None))
                            mq = T.tok()
                            tq1 = b.op("scalar", lambda e, it=it: e.activation(
                                q_bf[:, it * 512:(it + 1) * 512], ps_q[:],
                                AF.Identity), waits=[mq], sem=SS)
                            prev_sq = b.op("scalar", lambda e, it=it: e.activation(
                                q_bf8[:, it * 512:(it + 1) * 512], ps_q[:],
                                AF.Identity, scale=0.125), waits=[mq], sem=SS)
                            sqp = None
                            for c in range(HC):
                                pc = c % 2
                                sq_c = sqf[:, pc * 512:(pc + 1) * 512]
                                tsq = b.op("scalar", lambda e, sq_c=sq_c, c=c: e.activation(
                                    sq_c, h2tile[:, c * 512:(c + 1) * 512],
                                    AF.Square),
                                    waits=[lh, sqp], sem=SS)
                                b.op("tensor", lambda e, c=c: e.matmul(
                                    ps_sum[:], onesk[:, 0:1],
                                    h2tile[:, c * 512:(c + 1) * 512],
                                    start=(c == 0), stop=(c == HC - 1)),
                                    waits=[lh, prev_tv] if c == 0 else (),
                                    sem=(T if c == HC - 1 else None))
                                tm2 = b.op("tensor", lambda e, sq_c=sq_c, c=c: e.matmul(
                                    ps_sq3[:], onesk[:, 0:1], sq_c,
                                    start=(c == 0), stop=(c == HC - 1)),
                                    waits=[tsq], sem=T)
                                sqp = tm2
                            mstat = T.tok()
                            tv1 = b.op("vector", lambda e: e.tensor_scalar_mul(
                                rows[0:1, 0:512], ps_sum[0:1, 0:512], 1.0 / D),
                                waits=[mstat, prev_lvst], sem=V)
                            b.op("vector", lambda e: e.tensor_tensor(
                                rows[0:1, 0:512], rows[0:1, 0:512],
                                rows[0:1, 0:512], ALU.mult), sem=V)
                            b.op("vector", lambda e: e.tensor_scalar_mul(
                                rows[0:1, 512:1024], ps_sq3[0:1, 0:512], 1.0 / D),
                                sem=V)
                            tvv = b.op("vector", lambda e: e.tensor_tensor(
                                rows[0:1, 512:1024], rows[0:1, 512:1024],
                                rows[0:1, 0:512], ALU.subtract), sem=V)
                            tlnv = b.op("scalar", lambda e: e.activation(
                                rows[0:1, 512:1024], rows[0:1, 512:1024],
                                AF.Ln, bias=1.0), waits=[tvv], sem=SS)
                            prev_lvst = b.dma(
                                "scalar",
                                rowscr[2:3, it * 512:(it + 1) * 512],
                                rows[0:1, 512:1024], SR, waits=[tlnv])
                            tps = b.op("vector", lambda e: e.reduce_sum(
                                ccrow[0:1, 1:2], rows[0:1, 512:1024], AX.X),
                                waits=[tlnv, tz], sem=V)
                            prev_tv = b.op("vector", lambda e: e.tensor_tensor(
                                ccrow[0:1, 0:1], ccrow[0:1, 0:1],
                                ccrow[0:1, 1:2], ALU.add), sem=V)
                            for sj in range(8):
                                pw = sj % 2
                                wpr = wu1_use.get((it, sj - 2),
                                                  wu1_use.get((it - 1, sj + 6)))
                                wtk = b.dma(
                                    "gpsimd",
                                    wu1_[pw][:].rearrange("p (c j) -> p c j", j=128),
                                    wi["wu1c"][sj],
                                    WA if pw == 0 else WB, waits=[wpr])
                                for c in range(HC):
                                    b.op("tensor", lambda e, pw=pw, c=c: e.matmul(
                                        ps_uu[pw][:],
                                        wu1_[pw][:, c * 128:(c + 1) * 128],
                                        h2tile[:, c * 512:(c + 1) * 512],
                                        start=(c == 0), stop=(c == HC - 1)),
                                        waits=([wtk, uu_prev] if c == 0 else ()),
                                        sem=(T if c == HC - 1 else None))
                                muu = T.tok()
                                wu1_use[(it, sj)] = muu
                                tg = b.op("scalar", lambda e, pw=pw, sj=sj: e.activation(
                                    uub[:], ps_uu[pw][:], AF.Gelu_apprx_tanh,
                                    bias=bu1_t[:, sj:sj + 1]),
                                    waits=[muu, t_bu1, tl_prev], sem=SS)
                                tl_prev = b.op("tensor", lambda e, sj=sj: e.matmul(
                                    ps_le[:], wu2_t[:, sj:sj + 1], uub[:],
                                    start=(sj == 0), stop=(sj == 7)),
                                    waits=[tg, t_wu2], sem=T)
                                uu_prev = tg
                            tle = b.op("scalar", lambda e: e.activation(
                                lsl[0:1, :], ps_le[:],
                                AF.Identity, bias=sc_misc[0:1, 0:1]),
                                waits=[tl_prev, tok_ld, prev_str], sem=SS)
                            prev_str = b.dma(
                                "scalar",
                                rowscr[3:4, it * 512:(it + 1) * 512],
                                lsl[0:1, :], ST, waits=[tle])
                            tl_prev = tle
                        tzz = b.op("vector", lambda e: e.memset(
                            ccrow[0:1, 1:128], 0.0), waits=[prev_tv], sem=V)
                        tst = b.dma("scalar", cc_in, ccrow[:], ST,
                                    waits=[prev_tv, tzz])
                        tcc = b.op("gpsimd", lambda e: e.collective_compute(
                            "AllReduce", ALU.add,
                            replica_groups=[list(range(NCORE))],
                            ins=[cc_in], outs=[cc_out]),
                            waits=[tst], sem=CC, serial=False)
                    return tcc
                STAGE_MARKS.append(('_st_f1', {e: len(b.q[e]) for e in ENGINES}))
                tcc = _st_f1()
                b.barrier()

                def _st_f2():
                    with ExitStack() as sctx:
                        Es = sctx.enter_context
                        ak_sb = Es(nc.sbuf_tensor("ak_sb", [64, NSLOT], BF16))
                        wr_sb = Es(nc.sbuf_tensor("wr_sb", [64, RDIM], BF16))
                        rk_bf = Es(nc.sbuf_tensor("rk_bf", [RDIM, NSLOT], BF16))
                        rq_bf = Es(nc.sbuf_tensor("rq_bf", [RDIM, nt], BF16))
                        LRb = Es(nc.sbuf_tensor("LRb", [128, NSLOT], F32))
                        lrr = Es(nc.sbuf_tensor("lrr", [1, 1024], F32))
                        rout = Es(nc.sbuf_tensor("rout", [128, NSLOT], F32))
                        attn = Es(nc.sbuf_tensor("attn", [128, NSLOT], F32))
                        am_t = Es(nc.sbuf_tensor("am_t", [128, NSLOT], F32))
                        e_t = Es(nc.sbuf_tensor("e_t", [128, NSLOT], F32))
                        msk = Es(nc.sbuf_tensor("msk", [128, NSLOT], F32))
                        t8 = Es(nc.sbuf_tensor("t8", [128, 8], F32))
                        thr = Es(nc.sbuf_tensor("thr", [128, 1], F32))
                        nmx = Es(nc.sbuf_tensor("nmx", [128, 1], F32))
                        ssm = Es(nc.sbuf_tensor("ssm", [128, 1], F32))
                        wbf = Es(nc.sbuf_tensor("wbf", [128, NSLOT], BF16))
                        ps_r = Es(nc.psum_tensor("ps_r", [128, NSLOT], F32))
                        ps_a2 = Es(nc.psum_tensor("ps_a2", [128, NSLOT], F32))
                        ps_t = Es(nc.psum_tensor("ps_t", [128, 128], BF16))
                        tak = b.dma("gpsimd", ak_sb[:], wi["akT"], CS)
                        twr = b.dma("gpsimd", wr_sb[:], wi["WrT"], CS)
                        tl1 = b.dma("sync", lrr[0:1, 0:1024],
                                    wi["lr_row"][:, 0:1024], LD)
                        tl2 = tl1
                        with nc.psum_tensor("ps_rk", [RDIM, NSLOT], F32) as ps_rk:
                            for hh in range(2):
                                tmb = b.op("tensor", lambda e, hh=hh: e.matmul(
                                    ps_r[:, hh * 512:(hh + 1) * 512],
                                    ones1f[0:1, :],
                                    lrr[0:1, hh * 512:(hh + 1) * 512],
                                    start=True, stop=True),
                                    waits=[tl1, tl2], sem=T)
                            tlrb = b.op("scalar", lambda e: e.copy(
                                LRb[:], ps_r[:]), waits=[tmb], sem=SS)
                            for hh in range(2):
                                b.op("tensor", lambda e, hh=hh: e.matmul(
                                    ps_rk[:, hh * 512:(hh + 1) * 512], wr_sb[:],
                                    ak_sb[:, hh * 512:(hh + 1) * 512],
                                    start=True, stop=True),
                                    waits=[twr],
                                    sem=(T if hh == 1 else None))
                            mrk = T.tok()
                            trk = b.op("scalar", lambda e: e.copy(
                                rk_bf[:], ps_rk[:]), waits=[mrk], sem=SS)
                            rq_prev = None
                            for tt in range(ntt):
                                mrq = b.op("tensor", lambda e, tt=tt: e.matmul(
                                    ps_rk[:, 0:512], wr_sb[:],
                                    q_bf[:, tt * 512:(tt + 1) * 512],
                                    start=True, stop=True),
                                    waits=[trk, rq_prev], sem=T)
                                rq_prev = b.op("scalar", lambda e, tt=tt: e.activation(
                                    rq_bf[:, tt * 512:(tt + 1) * 512],
                                    ps_rk[:, 0:512], AF.Identity,
                                    scale=float(1.0 / np.sqrt(RDIM))),
                                    waits=[mrq], sem=SS)
                        trq = rq_prev
                        prev = {}
                        tpp = None
                        for tc in range(nt // 128):
                            for hh in range(2):
                                b.op("tensor", lambda e, tc=tc, hh=hh: e.matmul(
                                    ps_r[:, hh * 512:(hh + 1) * 512],
                                    rq_bf[:, tc * 128:(tc + 1) * 128],
                                    rk_bf[:, hh * 512:(hh + 1) * 512],
                                    start=True, stop=True),
                                    waits=([trq, tlrb, prev.get("rout")]
                                           if hh == 0 else ()),
                                    sem=(T if hh == 1 else None))
                            mr = T.tok()
                            tro = b.op("vector", lambda e: e.tensor_tensor(
                                rout[:], ps_r[:], LRb[:], ALU.add),
                                waits=[mr], sem=V)
                            for hh in range(2):
                                b.op("tensor", lambda e, tc=tc, hh=hh: e.matmul(
                                    ps_a2[:, hh * 512:(hh + 1) * 512],
                                    q_bf8[:, tc * 128:(tc + 1) * 128],
                                    ak_sb[:, hh * 512:(hh + 1) * 512],
                                    start=True, stop=True),
                                    waits=[prev.get("attn")] if hh == 0 else (),
                                    sem=(T if hh == 1 else None))
                            ma = T.tok()
                            tat = b.op("vector", lambda e: e.tensor_tensor(
                                attn[:], ps_a2[:], LRb[:], ALU.add),
                                waits=[ma], sem=V)
                            tt8 = b.op("vector", lambda e: e.max(
                                out=t8[:], in_=rout[:]), waits=[tro], sem=V)
                            b.op("vector", lambda e: e.tensor_reduce(
                                thr[:], t8[:], AX.X, ALU.min), sem=V)
                            tmk = b.op("vector", lambda e: e.tensor_scalar(
                                msk[:], rout[:], thr[:, 0:1], None, ALU.is_ge),
                                sem=V)
                            tam = b.op("vector", lambda e: e.scalar_tensor_tensor(
                                am_t[:], attn[:], 30000.0, msk[:], ALU.add,
                                ALU.mult), waits=[tat, prev.get("am")], sem=V)
                            tnx = b.op("vector", lambda e: e.reduce_max(
                                nmx[:], am_t[:], AX.X, negate=True), sem=V)
                            tex = b.op("scalar", lambda e: e.activation(
                                e_t[:], am_t[:], AF.Exp, bias=nmx[:, 0:1]),
                                waits=[tnx, prev.get("e")], sem=SS)
                            tsm = b.op("vector", lambda e: e.reduce_sum(
                                ssm[:], e_t[:], AX.X), waits=[tex], sem=V)
                            b.op("vector", lambda e: e.reciprocal(
                                ssm[:], ssm[:]), sem=V)
                            twb = b.op("vector", lambda e: e.tensor_scalar(
                                wbf[:], e_t[:], ssm[:, 0:1], None, ALU.mult),
                                waits=[prev.get("wbf")], sem=V)
                            prev["rout"] = tmk
                            prev["attn"] = tam
                            prev["am"] = tex
                            prev["e"] = twb
                            for sb in range(8):
                                ttr = b.op("tensor", lambda e, sb=sb: e.transpose(
                                    ps_t[:], wbf[:, sb * 128:(sb + 1) * 128],
                                    ident[:]), waits=[twb, t_id, tpp], sem=T)
                                tpp = b.op("vector", lambda e, sb=sb, tc=tc: e.tensor_copy(
                                    wT[:, sb * nt + tc * 128:
                                       sb * nt + tc * 128 + 128], ps_t[:]),
                                    waits=[ttr], sem=V)
                            prev["wbf"] = tpp
                        if wsd is not None:
                            b.dma("scalar",
                                  wsd.rearrange("(s p) t -> p s t", p=128),
                                  wT[:].rearrange("p (s t) -> p s t", t=nt),
                                  ST, waits=[tpp])
                STAGE_MARKS.append(('_st_f2', {e: len(b.q[e]) for e in ENGINES}))
                _st_f2()
                b.barrier()

            G = fctx.enter_context(nc.sbuf_tensor("G", [128, nt], F32))
            def _st_gate():
                with ExitStack() as gctx:
                    Eg = gctx.enter_context
                    grow = Eg(nc.sbuf_tensor("grow", [1, nt], F32))
                    lvrow = Eg(nc.sbuf_tensor("lvrow", [1, nt], F32))
                    lnrow = Eg(nc.sbuf_tensor("lnrow", [1, nt], F32))
                    ccrow2 = Eg(nc.sbuf_tensor("ccrow2", [1, 128], F32))
                    ps_G = Eg(nc.psum_tensor("ps_G", [128, 512], F32))
                    b.dma("sync", lvrow[0:1, :], rowscr[2:3, :], LD)
                    trw = b.dma("sync", lnrow[0:1, :], rowscr[3:4, :], LD)
                    tcr = b.dma("gpsimd", ccrow2[:], cc_out, CC2, waits=[tcc])
                    tmean = b.op("vector", lambda e: e.tensor_scalar(
                        sc_misc[0:1, 3:4], ccrow2[0:1, 0:1],
                        1.0 / (NCORE * nt), 1e-6, ALU.mult, ALU.add),
                        waits=[tcr, trw], sem=V)
                    tmr = b.op("vector", lambda e: e.reciprocal(
                        sc_misc[0:1, 3:4], sc_misc[0:1, 3:4]),
                        waits=[tmean], sem=V)
                    tnv = b.op("vector", lambda e: e.tensor_scalar(
                        lvrow[0:1, :], lvrow[0:1, :], sc_misc[0:1, 3:4], 0.5,
                        ALU.mult, ALU.mult), waits=[tmr, trw], sem=V)
                    tsg = b.op("scalar", lambda e: e.activation(
                        lnrow[0:1, :], lnrow[0:1, :], AF.Sigmoid),
                        waits=[trw], sem=SS)
                    b.op("vector", lambda e: e.scalar_tensor_tensor(
                        grow[0:1, :], lnrow[0:1, :], 2.5, lvrow[0:1, :],
                        ALU.mult, ALU.add), waits=[tsg, tnv], sem=V)
                    b.op("vector", lambda e: e.tensor_scalar(
                        grow[0:1, :], grow[0:1, :], 0.0, 5.0,
                        ALU.max, ALU.min), sem=V)
                    b.op("vector", lambda e: e.tensor_scalar(
                        grow[0:1, :], grow[0:1, :], 0.5, float(1.0 / 1.5),
                        ALU.subtract, ALU.mult), sem=V)
                    b.op("vector", lambda e: e.tensor_scalar(
                        grow[0:1, :], grow[0:1, :], 0.0, 1.0,
                        ALU.max, ALU.min), sem=V)
                    tga = b.op("vector", lambda e: e.tensor_scalar(
                        grow[0:1, :], grow[0:1, :], sc_misc[0:1, 1:2],
                        sc_misc[0:1, 2:3], ALU.mult, ALU.add),
                        waits=[t_gw1, t_gb], sem=V)
                    tgs = b.op("scalar", lambda e: e.activation(
                        grow[0:1, :], grow[0:1, :], AF.Sigmoid),
                        waits=[tga], sem=SS)
                    b.op("vector", lambda e: e.tensor_scalar(
                        lvrow[0:1, :], grow[0:1, :], 0.05, None, ALU.is_ge),
                        waits=[tgs], sem=V)
                    tgf = b.op("vector", lambda e: e.tensor_tensor(
                        grow[0:1, :], grow[0:1, :], lvrow[0:1, :], ALU.mult),
                        sem=V)
                    if g8 is not None:
                        b.dma("scalar", g8[0:1, :], grow[0:1, :], ST,
                              waits=[tgf])
                    gd = None
                    for k in range(nt // 512):
                        tbg = b.op("tensor", lambda e, k=k: e.matmul(
                            ps_G[:], ones1f[0:1, :],
                            grow[0:1, k * 512:(k + 1) * 512],
                            start=True, stop=True), waits=[tgf, gd], sem=T)
                        gd = b.op("scalar", lambda e, k=k: e.copy(
                            G[:, k * 512:(k + 1) * 512], ps_G[:]),
                            waits=[tbg], sem=SS)
                    return gd
            g_done = _st_gate()


            def _st_f3():
                with ExitStack() as sctx:
                    Es = sctx.enter_context
                    aux1 = Es(nc.sbuf_tensor("aux1", [128, HC * 512], BF16))
                    v1b = Es(nc.sbuf_tensor("v1b", [128, 2 * 512], BF16))
                    v2st = Es(nc.sbuf_tensor("v2st", [128, HC * 512], BF16))
                    ps_x = [Es(nc.psum_tensor(f"ps_x{i}", [128, 512], F32))
                            for i in range(2)]
                    pe_prev = {}
                    st_prev = None
                    for tt in range(ntt):
                        for hc in range(HC):
                            pw = hc % 2
                            for sc in range(8):
                                b.op("tensor", lambda e, pw=pw, hc=hc, sc=sc, tt=tt: e.matmul(
                                    ps_x[pw][:],
                                    av_sb[:, sc * D + hc * 128:
                                          sc * D + hc * 128 + 128],
                                    wT[:, sc * nt + tt * 512:
                                       sc * nt + tt * 512 + 512],
                                    start=(sc == 0), stop=(sc == 7)),
                                    waits=([tvu, g_done,
                                        pe_prev.get(("a", hc - 2)),
                                            st_prev] if sc == 0 else ()),
                                    sem=(T if sc == 7 else None))
                            mx = T.tok()
                            teva = b.op("scalar", lambda e, pw=pw, hc=hc: e.copy(
                                aux1[:, hc * 512:(hc + 1) * 512], ps_x[pw][:]),
                                waits=[mx], sem=SS)
                            pe_prev[("a", hc)] = teva
                        for vc in range(2):
                            pw = vc % 2
                            for hc in range(HC):
                                b.op("tensor", lambda e, pw=pw, vc=vc, hc=hc: e.matmul(
                                    ps_x[pw][:],
                                    wvd_sb[:, hc * VB + vc * 128:
                                           hc * VB + vc * 128 + 128],
                                    aux1[:, hc * 512:(hc + 1) * 512],
                                    start=(hc == 0), stop=(hc == HC - 1)),
                                    waits=([tvd, teva, pe_prev.get(("v", vc - 2))]
                                           if hc == 0 else ()),
                                    sem=(T if hc == HC - 1 else None))
                            mx = T.tok()
                            tevv = b.op("scalar", lambda e, pw=pw, vc=vc: e.copy(
                                v1b[:, vc * 512:(vc + 1) * 512], ps_x[pw][:]),
                                waits=[mx], sem=SS)
                            pe_prev[("v", vc)] = tevv
                        for hc in range(HC):
                            pw = hc % 2
                            for vc in range(2):
                                b.op("tensor", lambda e, pw=pw, hc=hc, vc=vc: e.matmul(
                                    ps_x[pw][:],
                                    wvu_sb[:, vc * D + hc * 128:
                                           vc * D + hc * 128 + 128],
                                    v1b[:, vc * 512:(vc + 1) * 512],
                                    start=(vc == 0), stop=(vc == 1)),
                                    waits=([tvu, tevv, pe_prev.get(("u", hc - 2))]
                                           if vc == 0 else ()),
                                    sem=(T if vc == 1 else None))
                            mx = T.tok()
                            tevu = b.op(
                                "vector",
                                lambda e, pw=pw, hc=hc, tt=tt: e.tensor_tensor(
                                    v2st[:, hc * 512:(hc + 1) * 512],
                                    ps_x[pw][:],
                                    G[:, tt * 512:(tt + 1) * 512],
                                    ALU.mult),
                                waits=[mx, g_done], sem=V)
                            pe_prev[("u", hc)] = tevu
                        st_prev = b.dma(
                            "scalar",
                            v2b.rearrange("(c p) t -> p c t", p=128)[
                                :, :, tt * 512:(tt + 1) * 512],
                            v2st[:].rearrange("p (c w) -> p c w", w=512),
                            ST, waits=[tevu])
                        for hc in range(HC):
                            pe_prev[("a", hc)] = st_prev
                            pe_prev[("u", hc)] = st_prev
            STAGE_MARKS.append(('_st_f3', {e: len(b.q[e]) for e in ENGINES}))
            _st_f3()
            b.barrier()

        def _st_f4():
            with ExitStack() as sctx:
                Es = sctx.enter_context
                TT4 = 512
                nt4 = nt // TT4
                v2t = Es(nc.sbuf_tensor("v2t", [128, HC * TT4], BF16))
                h2t4 = Es(nc.sbuf_tensor("h2t4", [128, HC * TT4], BF16))
                ft_ = [Es(nc.sbuf_tensor(f"ft{i}", [128, HC * TT4], F32))
                       for i in range(2)]
                fqt = Es(nc.sbuf_tensor("fqt", [128, 2 * TT4], BF16))
                rows = Es(nc.sbuf_tensor("rows4", [1, 2 * TT4], F32))
                ps_sum = Es(nc.psum_tensor("ps_sum4", [1, TT4], F32))
                ps_sq4 = Es(nc.psum_tensor("ps_sq4", [1, TT4], F32))
                ps_A = Es(nc.psum_tensor("ps_A4", [128, TT4], F32))
                ps_B = Es(nc.psum_tensor("ps_B4", [128, TT4], F32))
                prev_out = {}
                prev_t4 = None
                for it in range(nt4):
                    sl = slice(it * TT4, (it + 1) * TT4)
                    ft = ft_[it % 2]
                    lv = b.dma("sync",
                               v2t[:].rearrange("p (c w) -> p c w", w=TT4),
                               v2b.rearrange("(c p) t -> p c t", p=128)[
                                   :, :, sl],
                               LA if it % 2 == 0 else LB, waits=[prev_t4])
                    lh4 = b.dma("sync",
                                h2t4[:].rearrange("p (c w) -> p c w", w=TT4),
                                h2b.rearrange("(c p) t -> p c t", p=128)[
                                    :, :, sl],
                                LC, waits=[prev_t4])
                    fv = ft[:].rearrange("p (c w) -> p c w", w=TT4)
                    hv = h2t4[:].rearrange("p (c w) -> p c w", w=TT4)
                    vv = v2t[:].rearrange("p (c w) -> p c w", w=TT4)
                    t4 = b.op("vector",
                              lambda e, fv=fv, hv=hv, vv=vv: e.tensor_tensor(
                                  fv, hv, vv, ALU.add),
                              waits=[lv, lh4, prev_out.get(it - 2)], sem=V)
                    prev_t4 = t4
                    mm_toks = {}
                    stoks = stats_pass(ft, TT4, HC, None, fqt, ps_sum, ps_sq4,
                                       lambda c: t4, mm_toks)
                    rt = ln_rows(ps_sum, ps_sq4, rows, stoks, TT4)
                    bt = bcast2(ps_A, ps_B, rows, rt, TT4)
                    b.op("vector", lambda e, fv=fv: e.tensor_tensor(
                        fv, fv, rep(ps_A[:], HC, TT4), ALU.mult),
                        waits=bt, sem=V)
                    t7 = b.op("vector", lambda e, fv=fv: e.tensor_tensor(
                        fv, fv, rep(ps_B[:], HC, TT4), ALU.add), sem=V)
                    prev_out[it] = b.op(
                        "gpsimd",
                        lambda e, fv=fv, sl=sl: e.dma_start(
                            out=out_t.rearrange("(c p) t -> p c t", p=128)[
                                :, :, sl],
                            in_=fv, accum_op=ALU.add),
                        waits=[t7], sem=ACX[it % 2], inc=16, serial=False)

        STAGE_MARKS.append(('_st_f4', {e: len(b.q[e]) for e in ENGINES}))
        _st_f4()
        STAGE_MARKS.append(('end', {e: len(b.q[e]) for e in ENGINES}))
        b.wait("sync", ST.tok())
        b.wait("sync", CS.tok())
        b.wait("sync", CC.tok())
        b.wait("sync", CC2.tok())
        b.wait("sync", AC.tok())
        b.wait("sync", AC2.tok())
        b.wait("sync", SX[0].tok())
        b.wait("sync", SX[1].tok())

        with nc.Block() as block:
            b.finalize(block)
    return nc



def _prep_shared(inp):
    g = lambda k: np.ascontiguousarray(np.asarray(inp[k], dtype=np.float32))
    C = np.ascontiguousarray
    lao = g("L0_attn_local_approx_output")
    fuse = g("L0_attn_fuse_weight")
    d = {}
    d["upc"] = C(g("L0_attn_up_w").reshape(8, 128, HC, 128)
                 .transpose(0, 3, 2, 1))
    d["gatec"] = C(g("L0_attn_gate_w").reshape(8, 128, HC, 128)
                   .transpose(0, 3, 2, 1))
    WC = np.concatenate(
        [lao.T[None], fuse.transpose(2, 1, 0)], axis=0).reshape(NP, D)
    d["wcc"] = C(WC.reshape(2, 28, 128, HC, 128).transpose(0, 3, 2, 1, 4))
    d["fgc"] = C(g("L0_attn_ffn_gate_w").reshape(NF // 256, 256, HC, 128)
                 .transpose(0, 3, 2, 1))
    d["fuc"] = C(g("L0_attn_ffn_up_w").reshape(NF // 256, 256, HC, 128)
                 .transpose(0, 3, 2, 1))
    fdT = np.asarray(inp["L0_attn_ffn_down_w"], dtype=np.float32).T
    d["fdc"] = C(fdT.reshape(2, 39, 128, HC, 128).transpose(0, 3, 2, 1, 4))
    d["wqc"] = C(g("L0_ffn_Wq").reshape(BOT, HC, 128).transpose(2, 1, 0))
    d["WrT"] = C(g("L0_ffn_Wr").T)
    d["akT"] = C(g("L0_ffn_aux_keys").T)
    d["avc"] = C(g("L0_ffn_aux_values").reshape(8, 128, D).transpose(1, 0, 2))
    d["wvdc"] = C(g("L0_ffn_Wvd").reshape(VB, HC, 128).transpose(2, 1, 0))
    d["wvuc"] = C(g("L0_ffn_Wvu").reshape(D, 2, 128).transpose(2, 1, 0))
    d["wu1c"] = C(g("L0_ffn_Wu1").reshape(8, 128, HC, 128)
                  .transpose(0, 3, 2, 1))
    d["wu2c"] = C(g("L0_ffn_Wu2").reshape(8, 128).T)
    d["lp_c"] = C(g("L0_attn_local_point").reshape(8, 128).T)
    d["bu1_c"] = C(g("L0_ffn_bu1").reshape(8, 128).T)
    d["lr_row"] = g("L0_ffn_log_rel").reshape(1, NSLOT)
    d["d_row"] = g("L0_attn_discount_factor").reshape(1, 6)
    d["bu2_s"] = g("L0_ffn_bu2").reshape(1, 1)
    d["gw1_s"] = np.asarray(
        inp["L0_ffn_gate_w1"], dtype=np.float32).reshape(1, 1)
    d["gb_s"] = np.asarray(
        inp["L0_ffn_gate_bias"], dtype=np.float32).reshape(1, 1)
    return d


def kernel(**inputs):
    x = np.ascontiguousarray(
        np.asarray(inputs["L0_attn_hidden_states"], dtype=np.float32))
    _, seq, _ = x.shape
    nt = seq // NCORE
    shared = _prep_shared(inputs)
    xt = x[0].T
    nc = build_program(nt=nt)
    in_maps = []
    for c in range(NCORE):
        m = dict(shared)
        m["x_in"] = np.ascontiguousarray(xt[:, c * nt:(c + 1) * nt])
        in_maps.append(m)
    res = run_bass_kernel_spmd(nc, in_maps, core_ids=list(range(NCORE)))
    out = np.concatenate(
        [res.results[c]["out_t"].T for c in range(NCORE)], axis=0)
    return out.reshape(1, seq, D).astype(np.float32)


# version 15 (final)
# speedup vs baseline: 1.0167x; 1.0072x over previous
import numpy as np
from contextlib import ExitStack

import concourse.bass as bass
import concourse.mybir as mybir
import concourse.bass_utils as _bass_utils
from concourse.bass_utils import run_bass_kernel_spmd

if not getattr(_bass_utils, "_ldw_opt_patched", False):
    _orig_run_command = _bass_utils.run_command

    def _run_command_ldw(argv, **kwargs):
        argv = ["--enable-ldw-opt=true" if a == "--enable-ldw-opt=false"
                else a for a in argv]
        return _orig_run_command(argv, **kwargs)

    _bass_utils.run_command = _run_command_ldw
    _bass_utils._ldw_opt_patched = True

F32 = mybir.dt.float32
BF16 = mybir.dt.bfloat16
AF = mybir.ActivationFunctionType
ALU = mybir.AluOpType
AX = mybir.AxisListType

NCORE = 8
D = 4096
S = 1024
NF = 9984
NP = 7 * S
BOT = 64
RDIM = 48
NSLOT = 1024
VB = 256
HC = D // 128
EPS = 1e-5

ENGINES = ("tensor", "scalar", "vector", "gpsimd", "sync")
STAGE_MARKS = []
SERIAL = {"vector", "scalar", "gpsimd"}


class Sem:
    __slots__ = ("h", "n")

    def __init__(self, h):
        self.h = h
        self.n = 0

    def tok(self):
        return (self, self.n)


class Build:
    def __init__(self, nc, ctx):
        self.nc = nc
        self.q = {e: [] for e in ENGINES}
        self.last_wait = {e: {} for e in ENGINES}
        self.esem = {}
        self.all_sems = []
        self.ctx = ctx

    def sem(self, name, barrier=True):
        s = Sem(self.ctx.enter_context(self.nc.semaphore(name)))
        if barrier:
            self.all_sems.append(s)
        return s

    def wait(self, eng, tok):
        if tok is None:
            return
        sem, val = tok
        if val <= 0:
            return
        lw = self.last_wait[eng]
        if lw.get(id(sem), 0) >= val:
            return
        lw[id(sem)] = val
        self.q[eng].append(lambda e, s=sem.h, v=val: e.wait_ge(s, v))

    def op(self, eng, fn, waits=(), sem=None, inc=1, serial=True):
        if eng in SERIAL and serial:
            self.wait(eng, self.esem[eng].tok())
        for t in waits:
            self.wait(eng, t)
        if sem is not None:
            sem.n += inc
            self.q[eng].append(
                lambda e, f=fn, s=sem.h, k=inc: f(e).then_inc(s, k))
            return sem.tok()
        self.q[eng].append(fn)
        return None

    def dma(self, eng, out, in_, sem, waits=()):
        return self.op(eng, lambda e, o=out, i=in_: e.dma_start(out=o, in_=i),
                       waits=waits, sem=sem, inc=16, serial=False)

    def barrier(self, excl=()):
        toks = [s.tok() for s in self.all_sems if s.n > 0]
        for e in ENGINES:
            if e in excl:
                continue
            for t in toks:
                self.wait(e, t)

    def finalize(self, block):
        for name in ENGINES:
            fns = self.q[name]
            getattr(block, name)(lambda eng, fns=fns: [f(eng) for f in fns])


def rep(ap2d, n, w):
    return ap2d.rearrange("p (o w) -> p o w", o=1).to_broadcast(
        [ap2d.shape[0], n, w])


def build_program(nt=2048, dbg=()):
    assert nt % 512 == 0
    ntt = nt // 512
    nc = bass.Bass("TRN2", target_bir_lowering=False)
    dbg = set(dbg)
    STAGE_MARKS.clear()

    def dram(name, shape, dtype):
        kind = "ExternalOutput" if name in dbg else "Internal"
        return nc.dram_tensor(name, shape, dtype, kind=kind).ap()

    x_in = nc.dram_tensor("x_in", [D, nt], F32, kind="ExternalInput").ap()
    out_t = nc.dram_tensor("out_t", [D, nt], F32, kind="ExternalOutput").ap()
    wi = {}
    for name, shape in [
        ("upc", [8, 128, HC, 128]), ("gatec", [8, 128, HC, 128]),
        ("wcc", [2, HC, 128, 28, 128]),
        ("fgc", [NF // 256, 128, HC, 256]), ("fuc", [NF // 256, 128, HC, 256]),
        ("fdc", [2, HC, 128, 39, 128]),
        ("wqc", [128, HC, 64]), ("WrT", [BOT, RDIM]), ("akT", [BOT, NSLOT]),
        ("avc", [128, 8, D]), ("wvdc", [128, HC, VB]), ("wvuc", [128, 2, D]),
        ("wu1c", [8, 128, HC, 128]), ("wu2c", [128, 8]),
        ("lp_c", [128, 8]), ("bu1_c", [128, 8]), ("lr_row", [1, NSLOT]),
        ("d_row", [1, 6]), ("bu2_s", [1, 1]), ("gw1_s", [1, 1]),
        ("gb_s", [1, 1]),
    ]:
        wi[name] = nc.dram_tensor(name, shape, F32, kind="ExternalInput").ap()

    Pbuf = dram("Pbuf", [7, S, nt], BF16)
    pbuf = dram("pbuf", [NF, nt], BF16)
    aarr = dram("aarr", [D, nt], BF16)
    x2b = dram("x2b", [D, nt], F32)
    h2b = dram("h2b", [D, nt], BF16)
    v2b = dram("v2b", [D, nt], BF16)
    rowscr = dram("rowscr", [8, nt], F32)
    cc_in = dram("cc_in", [1, 128], F32)
    cc_out = dram("cc_out", [1, 128], F32)
    h1d = dram("h1d", [D, nt], BF16) if "h1d" in dbg else None
    wsd = dram("wsd", [NSLOT, nt], BF16) if "wsd" in dbg else None
    g8 = dram("g8", [8, nt], F32) if "g8" in dbg else None

    with ExitStack() as ctx:
        b = Build(nc, ctx)
        E = ctx.enter_context

        T = b.sem("T")
        SS = b.sem("SS")
        V = b.sem("V")
        GP = b.sem("GP")
        b.esem = {"vector": V, "scalar": SS, "gpsimd": GP}
        ST = b.sem("ST")
        CS = b.sem("CS")
        CC = b.sem("CC", barrier=False)
        CC2 = b.sem("CC2", barrier=False)
        WA = b.sem("WA")
        WB = b.sem("WB")
        LA = b.sem("LA")
        LB = b.sem("LB")
        LC = b.sem("LC")
        LD = b.sem("LD")
        G2 = b.sem("G2")
        G3 = b.sem("G3")
        LE = b.sem("LE")
        SA = b.sem("SA")
        SB = b.sem("SB")
        LF = b.sem("LF")
        CS0 = b.sem("CS0")
        CS1 = b.sem("CS1")
        CSU = b.sem("CSU")
        SR = b.sem("SR")
        AC = b.sem("AC")
        AC2 = b.sem("AC2")
        ACX = [AC, AC2]
        SX = [b.sem("SX0"), b.sem("SX1")]
        G4 = b.sem("G4")
        G5 = b.sem("G5")
        G6 = b.sem("G6")
        G7 = b.sem("G7")
        GRP = [LA, LB, G2, G3]
        GRPS = [[LA, LB, G2, G3], [G4, G5, G6, G7]]
        LEF = [LE, LF]

        ones1f = E(nc.sbuf_tensor("ones1f", [1, 128], F32))
        onesk = E(nc.sbuf_tensor("onesk", [128, 1], BF16))
        oneskf = E(nc.sbuf_tensor("oneskf", [128, 1], F32))
        ident = E(nc.sbuf_tensor("ident", [128, 128], BF16))
        lp_n = E(nc.sbuf_tensor("lp_n", [128, 8], F32))
        bu1_t = E(nc.sbuf_tensor("bu1_t", [128, 8], F32))
        wu2_t = E(nc.sbuf_tensor("wu2_t", [128, 8], BF16))
        E6 = E(nc.sbuf_tensor("E6", [128, 6], F32))
        sc_misc = E(nc.sbuf_tensor("sc_misc", [1, 32], F32))
        constt = E(nc.sbuf_tensor("constt", [1, 4], F32))
        b.op("gpsimd", lambda e: e.memset(constt[0:1, 0:1], EPS), sem=GP)
        b.op("gpsimd", lambda e: e.memset(constt[0:1, 1:2], 1.0), sem=GP)
        nc.const_aps.register(F32, EPS, constt[0:1, 0:1])
        nc.const_aps.register(F32, 1.0, constt[0:1, 1:2])

        t_lp = b.dma("sync", lp_n[:], wi["lp_c"], LD)
        t_bu1 = b.dma("sync", bu1_t[:], wi["bu1_c"], LD)
        t_wu2 = b.dma("gpsimd", wu2_t[:], wi["wu2c"], CS)
        t_d = b.dma("sync", sc_misc[0:1, 20:26], wi["d_row"], LD)
        t_bu2 = b.dma("sync", sc_misc[0:1, 0:1], wi["bu2_s"], LD)
        t_gw1 = b.dma("sync", sc_misc[0:1, 1:2], wi["gw1_s"], LD)
        t_gb = b.dma("sync", sc_misc[0:1, 2:3], wi["gb_s"], LD)
        tok_ld = t_gb
        t_o1 = b.op("gpsimd", lambda e: e.memset(ones1f[:], 1.0), sem=GP)
        t_ok = b.op("gpsimd", lambda e: e.memset(onesk[:], 1.0), sem=GP)
        t_okf = b.op("gpsimd", lambda e: e.memset(oneskf[:], 1.0), sem=GP)
        b.op("gpsimd", lambda e: e.memset(ident[:], 0.0), sem=GP)
        t_id = b.op("gpsimd", lambda e: e.affine_select(
            out=ident[:], in_=ident[:], compare_op=ALU.not_equal, fill=1.0,
            base=0, pattern=[[-1, 128]], channel_multiplier=1), sem=GP)
        t_lpn = b.op("vector", lambda e: e.tensor_scalar_mul(
            lp_n[:], lp_n[:], -1.0), waits=[tok_ld], sem=V)
        t_e6a = b.op("scalar", lambda e: e.activation(
            sc_misc[0:1, 8:14], sc_misc[0:1, 20:26], AF.Exp, scale=-1.0),
            waits=[tok_ld], sem=SS)
        with nc.psum_tensor("ps_set", [128, 8], F32) as ps_set:
            t_e6m = b.op("tensor", lambda e: e.matmul(
                ps_set[:, 0:6], ones1f[0:1, :], sc_misc[0:1, 8:14],
                start=True, stop=True), waits=[t_e6a, t_o1], sem=T)
            b.op("scalar", lambda e: e.copy(E6[:], ps_set[:, 0:6]),
                 waits=[t_e6m], sem=SS)
        b.barrier()

        def ln_rows(ps_sum, ps_sq, rows, wtoks, w):
            mean = rows[0:1, 0:w]
            r = rows[0:1, w:2 * w]
            b.op("vector", lambda e: e.tensor_scalar_mul(
                mean, ps_sum[0:1, 0:w], 1.0 / D), waits=wtoks, sem=V)
            b.op("vector", lambda e: e.tensor_scalar_mul(
                r, ps_sq[0:1, 0:w], 1.0 / D), waits=wtoks, sem=V)
            b.op("vector", lambda e: e.tensor_tensor(
                ps_sq[0:1, 0:w], mean, mean, ALU.mult), sem=V)
            tvar = b.op("vector", lambda e: e.tensor_tensor(
                r, r, ps_sq[0:1, 0:w], ALU.subtract), sem=V)
            b.op("scalar", lambda e: e.activation(
                r, r, AF.Sqrt, bias=EPS), waits=[tvar], sem=SS)
            trc = b.op("vector", lambda e: e.reciprocal(r, r),
                       waits=[SS.tok()], sem=V)
            tnm = b.op("vector", lambda e: e.scalar_tensor_tensor(
                mean, mean, -1.0, r, ALU.mult, ALU.mult), sem=V)
            return [trc, tnm, tvar]

        def bcast2(ps_a, ps_b, rows, wtoks, w, dst_a=None, dst_b=None):
            ta = b.op("tensor", lambda e: e.matmul(
                ps_a[:, 0:w], ones1f[0:1, :], rows[0:1, w:2 * w],
                start=True, stop=True), waits=wtoks, sem=T)
            tb = b.op("tensor", lambda e: e.matmul(
                ps_b[:, 0:w], ones1f[0:1, :], rows[0:1, 0:w],
                start=True, stop=True), sem=T)
            return [ta, tb]

        def stats_pass(src_tile, w, HCn, cbt, cqt, ps_sum, ps_sq, ld_tok,
                       mm_toks, extra_w=(), src_bf=False):
            for c in range(HCn):
                pc = c % 2
                sq_c = cqt[:, pc * w:(pc + 1) * w]
                tsq = b.op("scalar", lambda e, sq_c=sq_c, c=c: e.activation(
                    sq_c, src_tile[:, c * w:(c + 1) * w], AF.Square),
                    waits=[ld_tok(c), mm_toks.get(("q", c - 2))] + list(extra_w),
                    sem=SS)
                tm1 = b.op("tensor", lambda e, c=c: e.matmul(
                    ps_sum[:, 0:w],
                    (onesk if src_bf else oneskf)[:, 0:1],
                    src_tile[:, c * w:(c + 1) * w],
                    start=(c == 0), stop=(c == HCn - 1)),
                    waits=[ld_tok(c), t_okf, t_ok] + list(extra_w),
                    sem=T)
                tm2 = b.op("tensor", lambda e, sq_c=sq_c, c=c: e.matmul(
                    ps_sq[:, 0:w], onesk[:, 0:1], sq_c,
                    start=(c == 0), stop=(c == HCn - 1)),
                    waits=[tsq], sem=T)
                mm_toks[("s", c)] = tm1
                mm_toks[("q", c)] = tm2
            return [tm1, tm2]

        with ExitStack() as hctx:
            h1sb = hctx.enter_context(
                nc.sbuf_tensor("h1sb", [128, HC * nt], BF16))

            def _st_ln1():
                with ExitStack() as sctx:
                    Es = sctx.enter_context
                    TL = 256
                    ntl = nt // TL
                    xt_ = [Es(nc.sbuf_tensor(f"xt1_{i}", [128, HC * TL], F32))
                           for i in range(2)]
                    cq_ = [Es(nc.sbuf_tensor(f"cq1_{i}", [128, 2 * TL], BF16))
                           for i in range(2)]
                    rows0 = Es(nc.sbuf_tensor("rw1", [1, 2 * TL], F32))
                    rows_ = [rows0, rows0]
                    ps_sum_ = [Es(nc.psum_tensor(f"pss1{i}", [1, TL], F32))
                               for i in range(2)]
                    ps_sq_ = [Es(nc.psum_tensor(f"psq1{i}", [1, TL], F32))
                              for i in range(2)]
                    ps_A_ = [Es(nc.psum_tensor(f"psA1{i}", [128, TL], F32))
                             for i in range(2)]
                    ps_B_ = [Es(nc.psum_tensor(f"psB1{i}", [128, TL], F32))
                             for i in range(2)]
                    done = {}
                    for it in range(ntl):
                        j = it % 2
                        gt = {}
                        for c in range(HC):
                            gt[c // 8] = b.dma(
                                "sync", xt_[j][:, c * TL:(c + 1) * TL],
                                x_in[c * 128:(c + 1) * 128,
                                     it * TL:(it + 1) * TL],
                                GRPS[j][c // 8], waits=[done.get(it - 2)])
                        mm_toks = {}
                        stoks = stats_pass(
                            xt_[j], TL, HC, None, cq_[j], ps_sum_[j],
                            ps_sq_[j], lambda c, gt=gt: gt[c // 8], mm_toks,
                            extra_w=[done.get(it - 2)] if it >= 2 else ())
                        rt = ln_rows(ps_sum_[j], ps_sq_[j], rows_[j], stoks, TL)
                        bt = bcast2(ps_A_[j], ps_B_[j], rows_[j], rt, TL)
                        xv = xt_[j][:].rearrange("p (c w) -> p c w", w=TL)
                        h1v = h1sb[:].rearrange("p (c t) -> p c t", t=nt)[
                            :, :, it * TL:(it + 1) * TL]
                        b.op("vector", lambda e, xv=xv, j=j: e.tensor_tensor(
                            xv, xv, rep(ps_A_[j][:], HC, TL), ALU.mult),
                            waits=bt + list(gt.values()), sem=V)
                        tap2 = b.op("vector",
                                    lambda e, xv=xv, h1v=h1v, j=j: e.tensor_tensor(
                                        h1v, xv, rep(ps_B_[j][:], HC, TL),
                                        ALU.add), sem=V)
                        done[it] = tap2
                    prev_apply = tap2
                    if h1d is not None:
                        b.dma("scalar",
                              h1d.rearrange("(c p) t -> p c t", p=128),
                              h1sb[:].rearrange("p (c t) -> p c t", t=nt),
                              ST, waits=[prev_apply])
                    return prev_apply
            STAGE_MARKS.append(('_st_ln1', {e: len(b.q[e]) for e in ENGINES}))
            ln1_done = _st_ln1()
            b.barrier(excl=("gpsimd",))

            def _st_ab():
                with ExitStack() as sctx:
                    Es = sctx.enter_context
                    wup = Es(nc.sbuf_tensor("wup", [128, HC * 128], BF16))
                    wgt = Es(nc.sbuf_tensor("wgt", [128, HC * 128], BF16))
                    Uj = Es(nc.sbuf_tensor("Uj", [128, nt], F32))
                    dj = Es(nc.sbuf_tensor("dj", [128, nt], F32))
                    keepj = Es(nc.sbuf_tensor("keepj", [128, nt], F32))
                    Ra = Es(nc.sbuf_tensor("Ra", [128, nt], F32))
                    Rb = Es(nc.sbuf_tensor("Rb", [128, nt], F32))
                    Pst = [Es(nc.sbuf_tensor(f"Pst{i}", [128, nt], BF16))
                           for i in range(2)]
                    U_bf = Es(nc.sbuf_tensor("U_bf", [128, nt], BF16))
                    ps_U = Es(nc.psum_tensor("ps_U", [128, ntt * 512], F32))
                    ps_d = Es(nc.psum_tensor("ps_d", [128, ntt * 512], F32))
                    pass
                    pass
                    wup_use = wgt_use = ln1_done
                    evU_prev = evd_prev = None
                    uj_free = []
                    dj_free = None
                    pst_tok = [None, None]
                    tab_wait = None
                    st0_prev = None
                    for sj in range(8):
                        wup_tok = b.dma(
                            "gpsimd", wup[:].rearrange("p (c j) -> p c j", j=128),
                            wi["upc"][sj], WA,
                            waits=[wup_use])
                        wgt_tok = b.dma(
                            "gpsimd", wgt[:].rearrange("p (c j) -> p c j", j=128),
                            wi["gatec"][sj], WB,
                            waits=[wgt_use])
                        for tt in range(ntt):
                            for c in range(HC):
                                last = (tt == ntt - 1 and c == HC - 1)
                                b.op("tensor", lambda e, tt=tt, c=c: e.matmul(
                                    ps_U[:, tt * 512:(tt + 1) * 512],
                                    wup[:, c * 128:(c + 1) * 128],
                                    h1sb[:, c * nt + tt * 512:
                                         c * nt + tt * 512 + 512],
                                    start=(c == 0), stop=(c == HC - 1)),
                                    waits=([wup_tok, evU_prev]
                                           if (tt == 0 and c == 0) else ()),
                                    sem=(T if last else None))
                        mmU = T.tok()
                        wup_use = mmU
                        for tt in range(ntt):
                            for c in range(HC):
                                last = (tt == ntt - 1 and c == HC - 1)
                                b.op("tensor", lambda e, tt=tt, c=c: e.matmul(
                                    ps_d[:, tt * 512:(tt + 1) * 512],
                                    wgt[:, c * 128:(c + 1) * 128],
                                    h1sb[:, c * nt + tt * 512:
                                         c * nt + tt * 512 + 512],
                                    start=(c == 0), stop=(c == HC - 1)),
                                    waits=([wgt_tok, evd_prev]
                                           if (tt == 0 and c == 0) else ()),
                                    sem=(T if last else None))
                        mmD = T.tok()
                        wgt_use = mmD
                        evU = b.op("scalar", lambda e: e.copy(Uj[:], ps_U[:]),
                                   waits=[mmU] + uj_free, sem=SS)
                        evU_prev = evU
                        evd = b.op("scalar", lambda e, sj=sj: e.activation(
                            dj[:], ps_d[:], AF.Identity, bias=lp_n[:, sj:sj + 1]),
                            waits=[mmD, dj_free, t_lpn], sem=SS)
                        evd_prev = evd
                        tab = b.op("scalar", lambda e: e.activation(
                            Ra[:], dj[:], AF.Abs), waits=[tab_wait], sem=SS)
                        tke = b.op("vector", lambda e: e.tensor_scalar(
                            keepj[:], Ra[:], 2.5, None, ALU.is_le),
                            waits=[tab], sem=V)
                        evU2 = b.op("scalar", lambda e: e.copy(
                            U_bf[:], Uj[:]), waits=[st0_prev], sem=SS)
                        st0 = b.dma("sync",
                                    Pbuf[0, sj * 128:(sj + 1) * 128, :],
                                    U_bf[:], CSU, waits=[evU2])
                        st0_prev = st0
                        src = Uj
                        Rd = [Ra, Rb]
                        tr1 = None
                        for m in range(1, 7):
                            dst = Rd[m % 2]
                            tr = b.op("vector", lambda e, s=src, d=dst: e.tensor_tensor(
                                d[:], s[:], dj[:], ALU.mult),
                                waits=[evU, evd], sem=V)
                            if m == 1:
                                tr1 = tr
                            pc = m % 2
                            tp = b.op("vector", lambda e, d=dst, m=m, pc=pc: e.tensor_scalar(
                                Pst[pc][:], d[:], E6[:, m - 1:m], None, ALU.mult),
                                waits=[pst_tok[pc]], sem=V)
                            if m >= 5:
                                tp = b.op("vector", lambda e, pc=pc: e.tensor_tensor(
                                    Pst[pc][:], Pst[pc][:], keepj[:], ALU.mult),
                                    sem=V)
                            if m == 6:
                                tab_wait = tp
                            pst_tok[pc] = b.dma(
                                "sync", Pbuf[m, sj * 128:(sj + 1) * 128, :],
                                Pst[pc][:], (CS0 if pc == 0 else CS1),
                                waits=[tp])
                            src = dst
                        uj_free = [tr1]
                        dj_free = tp
            STAGE_MARKS.append(('_st_ab', {e: len(b.q[e]) for e in ENGINES}))
            _st_ab()
            b.barrier()

            def _st_d():
                with ExitStack() as sctx:
                    Es = sctx.enter_context
                    wg_ = [Es(nc.sbuf_tensor(f"wgD{i}", [128, HC * 256], BF16))
                           for i in range(2)]
                    wu_ = [Es(nc.sbuf_tensor(f"wuD{i}", [128, HC * 256], BF16))
                           for i in range(2)]
                    sg_ = [Es(nc.sbuf_tensor(f"sgD{i}", [128, 512], F32))
                           for i in range(2)]
                    pb_ = [Es(nc.sbuf_tensor(f"pbD{i}", [128, 512], BF16))
                           for i in range(2)]
                    ps_g = [Es(nc.psum_tensor(f"ps_g{i}", [128, 512], F32))
                            for i in range(2)]
                    ps_u = [Es(nc.psum_tensor(f"ps_u{i}", [128, 512], F32))
                            for i in range(2)]
                    pass
                    pass
                    NFG = NF // 256
                    wuse = {}
                    ev_toks = {}
                    st_toks = {}
                    for fg in range(NFG):
                        pw = fg % 2
                        wsem = WA if pw == 0 else WB
                        wg_tok = b.dma(
                            "gpsimd",
                            wg_[pw][:].rearrange("p (c j) -> p c j", j=256),
                            wi["fgc"][fg], wsem,
                            waits=[wuse.get(fg - 2)])
                        wu_tok = b.dma(
                            "gpsimd",
                            wu_[pw][:].rearrange("p (c j) -> p c j", j=256),
                            wi["fuc"][fg], wsem,
                            waits=[wuse.get(fg - 2)])
                        for j in range(2):
                            for tt in range(ntt):
                                un = (fg * 2 + j) * ntt + tt
                                ui = un % 2
                                prev_ev = ev_toks.get(un - 2)
                                for c in range(HC):
                                    b.op("tensor", lambda e, pw=pw, j=j, tt=tt, c=c, ui=ui: e.matmul(
                                        ps_g[ui][:],
                                        wg_[pw][:, c * 256 + j * 128:
                                                c * 256 + j * 128 + 128],
                                        h1sb[:, c * nt + tt * 512:
                                             c * nt + tt * 512 + 512],
                                        start=(c == 0), stop=(c == HC - 1)),
                                        waits=([wu_tok, prev_ev] if c == 0 else ()))
                                for c in range(HC):
                                    last = c == HC - 1
                                    b.op("tensor", lambda e, pw=pw, j=j, tt=tt, c=c, ui=ui: e.matmul(
                                        ps_u[ui][:],
                                        wu_[pw][:, c * 256 + j * 128:
                                                c * 256 + j * 128 + 128],
                                        h1sb[:, c * nt + tt * 512:
                                             c * nt + tt * 512 + 512],
                                        start=(c == 0), stop=(c == HC - 1)),
                                        waits=([wu_tok, prev_ev] if c == 0 else ()),
                                        sem=(T if last else None))
                                mmt = T.tok()
                                if j == 1 and tt == ntt - 1:
                                    wuse[fg] = mmt
                                tsg = b.op("scalar", lambda e, ui=ui: e.activation(
                                    sg_[ui][:], ps_g[ui][:], AF.Silu),
                                    waits=[mmt, st_toks.get(un - 2)], sem=SS)
                                tpb = b.op("vector", lambda e, ui=ui: e.tensor_tensor(
                                    pb_[ui][:], sg_[ui][:], ps_u[ui][:], ALU.mult),
                                    waits=[tsg], sem=V)
                                ev_toks[un] = tpb
                                frow = fg * 256 + j * 128
                                st_toks[un] = b.dma(
                                    "scalar",
                                    pbuf[frow:frow + 128,
                                         tt * 512:(tt + 1) * 512],
                                    pb_[ui][:], (SA if ui == 0 else SB),
                                    waits=[tpb])
            STAGE_MARKS.append(('_st_d', {e: len(b.q[e]) for e in ENGINES}))
            _st_d()
            b.barrier()

        def _st_c():
            with ExitStack() as sctx:
                Es = sctx.enter_context
                Ptile = Es(nc.sbuf_tensor("Ptile", [128, 28 * nt], BF16))
                wc_ = [Es(nc.sbuf_tensor(f"wcC{i}", [128, 28 * 128], BF16))
                       for i in range(2)]
                evb = [Es(nc.sbuf_tensor(f"evC{i}", [128, nt], BF16))
                       for i in range(2)]
                ain = [Es(nc.sbuf_tensor(f"ainC{i}", [128, nt], BF16))
                       for i in range(2)]
                ps_C = [Es(nc.psum_tensor(f"ps_C{i}", [128, ntt * 512], F32))
                        for i in range(2)]
                pass
                carry = {}
                for ph in range(2):
                    koff = ph * 28
                    gtok = {}
                    for ci in range(28):
                        cg = koff + ci
                        m, sc = divmod(cg, 8)
                        gtok[ci // 7] = b.dma(
                            "sync", Ptile[:, ci * nt:(ci + 1) * nt],
                            Pbuf[m, sc * 128:(sc + 1) * 128, :],
                            GRP[ci // 7],
                            waits=[carry.get("mm")])
                    wtok = {}
                    wuse = {-2: carry.get("mm"), -1: carry.get("mm")}
                    tevs = {-2: carry.get("ev0"), -1: carry.get("ev1")}
                    stp = {-2: carry.get("st0"), -1: carry.get("st1")}
                    for h in range(HC):
                        pw = h % 2
                        wtok[h] = b.dma(
                            "gpsimd",
                            wc_[pw][:].rearrange("p (c j) -> p c j", j=128),
                            wi["wcc"][ph, h],
                            WA if pw == 0 else WB, waits=[wuse.get(h - 2)])
                        if ph == 1:
                            ld_a = b.dma("sync", ain[pw][:],
                                         aarr[h * 128:(h + 1) * 128, :],
                                         (LC if pw == 0 else LF),
                                         waits=[tevs.get(h - 2)])
                        for ci in range(28):
                            for tt in range(ntt):
                                last = (ci == 27 and tt == ntt - 1)
                                b.op("tensor", lambda e, pw=pw, ci=ci, tt=tt: e.matmul(
                                    ps_C[pw][:, tt * 512:(tt + 1) * 512],
                                    wc_[pw][:, ci * 128:(ci + 1) * 128],
                                    Ptile[:, ci * nt + tt * 512:
                                          ci * nt + tt * 512 + 512],
                                    start=(ci == 0), stop=(ci == 27)),
                                    waits=(([wtok[h], tevs.get(h - 2),
                                             gtok[ci // 7]]
                                            if h < 2 else
                                            [wtok[h], tevs.get(h - 2)])
                                           if (ci % 7 == 0 and tt == 0)
                                           else ()),
                                    sem=(T if last else None))
                        mmt = T.tok()
                        wuse[h] = mmt
                        if ph == 0:
                            tev = b.op("scalar", lambda e, pw=pw: e.copy(
                                evb[pw][:], ps_C[pw][:]),
                                waits=[mmt, stp.get(h - 2)], sem=SS)
                        else:
                            tev = b.op("vector", lambda e, pw=pw: e.tensor_tensor(
                                evb[pw][:], ain[pw][:], ps_C[pw][:], ALU.add),
                                waits=[mmt, ld_a, stp.get(h - 2)], sem=V)
                        tevs[h] = tev
                        stp[h] = b.dma(
                            "scalar", aarr[h * 128:(h + 1) * 128, :],
                            evb[pw][:], (SA if pw == 0 else SB), waits=[tev])
                    carry = {"mm": wuse[HC - 1], "ev0": tevs[HC - 2],
                             "ev1": tevs[HC - 1], "st0": stp[HC - 2],
                             "st1": stp[HC - 1]}
        STAGE_MARKS.append(('_st_c', {e: len(b.q[e]) for e in ENGINES}))
        _st_c()
        b.barrier()

        def _st_e():
            with ExitStack() as sctx:
                Es = sctx.enter_context
                ptile = Es(nc.sbuf_tensor("ptile", [128, 39 * nt], BF16))
                wd_ = [Es(nc.sbuf_tensor(f"wdE{i}", [128, 39 * 128], BF16))
                       for i in range(2)]
                evb = [Es(nc.sbuf_tensor(f"evE{i}", [128, nt], BF16))
                       for i in range(2)]
                ain0 = Es(nc.sbuf_tensor("ainE0", [128, nt], BF16))
                ain = [ain0, ain0]
                ps_E = [Es(nc.psum_tensor(f"ps_E{i}", [128, ntt * 512], F32))
                        for i in range(2)]
                pass
                carry = {}
                for ph in range(2):
                    koff = ph * 39
                    gtok = {}
                    for ci in range(39):
                        fr = (koff + ci) * 128
                        gtok[min(ci // 10, 3)] = b.dma(
                            "sync", ptile[:, ci * nt:(ci + 1) * nt],
                            pbuf[fr:fr + 128, :],
                            GRP[min(ci // 10, 3)],
                            waits=[carry.get("mm")])
                    wtok = {}
                    wuse = {-2: carry.get("mm"), -1: carry.get("mm")}
                    tevs = {-2: carry.get("ev0"), -1: carry.get("ev1")}
                    stp = {-2: carry.get("st0"), -1: carry.get("st1")}
                    tev = carry.get("ev1")
                    for h in range(HC):
                        pw = h % 2
                        wtok[h] = b.dma(
                            "gpsimd",
                            wd_[pw][:].rearrange("p (c j) -> p c j", j=128),
                            wi["fdc"][ph, h],
                            WA if pw == 0 else WB, waits=[wuse.get(h - 2)])
                        ld_a = b.dma("sync", ain[pw][:],
                                     aarr[h * 128:(h + 1) * 128, :], LC,
                                     waits=[tev])
                        for ci in range(39):
                            for tt in range(ntt):
                                last = (ci == 38 and tt == ntt - 1)
                                b.op("tensor", lambda e, pw=pw, ci=ci, tt=tt: e.matmul(
                                    ps_E[pw][:, tt * 512:(tt + 1) * 512],
                                    wd_[pw][:, ci * 128:(ci + 1) * 128],
                                    ptile[:, ci * nt + tt * 512:
                                          ci * nt + tt * 512 + 512],
                                    start=(ci == 0), stop=(ci == 38)),
                                    waits=(([wtok[h], tevs.get(h - 2),
                                             gtok[min(ci // 10, 3)]]
                                            if h < 2 else
                                            [wtok[h], tevs.get(h - 2)])
                                           if (ci % 10 == 0 and tt == 0)
                                           else ()),
                                    sem=(T if last else None))
                        mmt = T.tok()
                        wuse[h] = mmt
                        tev = b.op("vector", lambda e, pw=pw: e.tensor_tensor(
                            evb[pw][:], ain[pw][:], ps_E[pw][:], ALU.add),
                            waits=[mmt, ld_a, stp.get(h - 2)], sem=V)
                        tevs[h] = tev
                        stp[h] = b.dma(
                            "scalar", aarr[h * 128:(h + 1) * 128, :],
                            evb[pw][:], (SA if pw == 0 else SB), waits=[tev])
                    carry = {"mm": wuse[HC - 1], "ev0": tevs[HC - 2],
                             "ev1": tevs[HC - 1], "st0": stp[HC - 2],
                             "st1": stp[HC - 1]}
        STAGE_MARKS.append(('_st_e', {e: len(b.q[e]) for e in ENGINES}))
        _st_e()
        b.barrier()

        def _st_ln23():
            with ExitStack() as sctx:
                Es = sctx.enter_context
                TL = 256
                ntl = nt // TL
                at_ = [Es(nc.sbuf_tensor(f"at2_{i}", [128, HC * TL], BF16))
                       for i in range(2)]
                A2s = [Es(nc.sbuf_tensor(f"A2s{i}", [128, TL], BF16))
                       for i in range(2)]
                B2s = [Es(nc.sbuf_tensor(f"B2s{i}", [128, TL], BF16))
                       for i in range(2)]
                xt_ = [Es(nc.sbuf_tensor(f"xt2_{i}", [128, HC * TL], F32))
                       for i in range(2)]
                h2_ = [Es(nc.sbuf_tensor(f"h2t_{i}", [128, HC * TL], BF16))
                       for i in range(2)]
                cq_ = [Es(nc.sbuf_tensor(f"cq2_{i}", [128, 2 * TL], BF16))
                       for i in range(2)]
                rows0 = Es(nc.sbuf_tensor("rw2", [1, 2 * TL], F32))
                rows_ = [rows0, rows0]
                ps_sum_ = [Es(nc.psum_tensor(f"pss2{i}", [1, TL], F32))
                           for i in range(2)]
                ps_sq_ = [Es(nc.psum_tensor(f"psq2{i}", [1, TL], F32))
                          for i in range(2)]
                ps_A_ = [Es(nc.psum_tensor(f"psA2{i}", [128, TL], F32))
                         for i in range(2)]
                ps_B_ = [Es(nc.psum_tensor(f"psB2{i}", [128, TL], F32))
                         for i in range(2)]
                done = {}
                h2done = {}
                prev_str = None
                for it in range(ntl):
                    j = it % 2
                    gt = {}
                    lxl = None
                    for c in range(HC):
                        gt[c // 8] = b.dma(
                            "sync", at_[j][:, c * TL:(c + 1) * TL],
                            aarr[c * 128:(c + 1) * 128,
                                 it * TL:(it + 1) * TL],
                            GRPS[j][c // 8], waits=[done.get(it - 2)])
                        lxl = b.dma(
                            "sync", xt_[j][:, c * TL:(c + 1) * TL],
                            x_in[c * 128:(c + 1) * 128,
                                 it * TL:(it + 1) * TL],
                            LEF[j], waits=[done.get(it - 2)])
                    mm_toks = {}
                    stoks = stats_pass(
                        at_[j], TL, HC, None, cq_[j], ps_sum_[j], ps_sq_[j],
                        lambda c, gt=gt: gt[c // 8], mm_toks,
                        extra_w=[done.get(it - 2)] if it >= 2 else (),
                        src_bf=True)
                    rt = ln_rows(ps_sum_[j], ps_sq_[j], rows_[j], stoks, TL)
                    bt = bcast2(ps_A_[j], ps_B_[j], rows_[j], rt, TL)
                    tca = b.op("scalar", lambda e, j=j: e.copy(
                        A2s[j][:], ps_A_[j][:, 0:TL]), waits=[bt[0]], sem=SS)
                    tcb = b.op("scalar", lambda e, j=j: e.copy(
                        B2s[j][:], ps_B_[j][:, 0:TL]), waits=[bt[1]], sem=SS)
                    av = at_[j][:].rearrange("p (c w) -> p c w", w=TL)
                    xv = xt_[j][:].rearrange("p (c w) -> p c w", w=TL)
                    b.op("vector", lambda e, av=av, j=j: e.tensor_tensor(
                        av, av, rep(A2s[j][:], HC, TL), ALU.mult),
                        waits=[tca, tcb] + list(gt.values()), sem=V)
                    b.op("vector", lambda e, av=av, j=j: e.tensor_tensor(
                        av, av, rep(B2s[j][:], HC, TL), ALU.add), sem=V)
                    t3 = b.op("vector", lambda e, av=av, xv=xv: e.tensor_tensor(
                        xv, xv, av, ALU.add),
                        waits=[lxl], sem=V)
                    stx2 = b.dma(
                        "scalar",
                        out_t.rearrange("(c p) t -> p c t", p=128)[
                            :, :, it * TL:(it + 1) * TL],
                        xv, SX[j], waits=[t3])
                    mm_toks2 = {}
                    stoks = stats_pass(xt_[j], TL, HC, None, cq_[j],
                                       ps_sum_[j], ps_sq_[j],
                                       lambda c: t3, mm_toks2)
                    rt = ln_rows(ps_sum_[j], ps_sq_[j], rows_[j], stoks, TL)
                    bt = bcast2(ps_A_[j], ps_B_[j], rows_[j], rt, TL)
                    b.op("vector", lambda e, av=av, xv=xv, j=j: e.tensor_tensor(
                        av, xv, rep(ps_A_[j][:], HC, TL), ALU.mult),
                        waits=bt, sem=V)
                    h2v = h2_[j][:].rearrange("p (c w) -> p c w", w=TL)
                    t5 = b.op("vector",
                              lambda e, av=av, h2v=h2v, j=j: e.tensor_tensor(
                                  h2v, av, rep(ps_B_[j][:], HC, TL), ALU.add),
                              waits=[h2done.get(it - 2)], sem=V)
                    sth2 = b.dma(
                        "scalar",
                        h2b.rearrange("(c p) t -> p c t", p=128)[
                            :, :, it * TL:(it + 1) * TL],
                        h2v, SX[j], waits=[t5])
                    h2done[it] = sth2
                    done[it] = sth2
        STAGE_MARKS.append(('_st_ln23', {e: len(b.q[e]) for e in ENGINES}))
        _st_ln23()
        b.barrier()

        with ExitStack() as fctx:
            wT = fctx.enter_context(
                nc.sbuf_tensor("wT", [128, 8 * nt], BF16))
            av_sb = fctx.enter_context(
                nc.sbuf_tensor("av_sb", [128, 8 * D], BF16))
            wvd_sb = fctx.enter_context(
                nc.sbuf_tensor("wvd_sb", [128, HC * VB], BF16))
            wvu_sb = fctx.enter_context(
                nc.sbuf_tensor("wvu_sb", [128, 2 * D], BF16))
            pass
            pass
            pass
            tav = b.dma("gpsimd",
                        av_sb[:].rearrange("p (s h) -> p s h", h=D),
                        wi["avc"], CS)
            tvd = b.dma("gpsimd",
                        wvd_sb[:].rearrange("p (c m) -> p c m", m=VB),
                        wi["wvdc"], CS)
            tvu = b.dma("gpsimd",
                        wvu_sb[:].rearrange("p (c m) -> p c m", m=D),
                        wi["wvuc"], CS)
            with ExitStack() as qctx:
                q_bf = qctx.enter_context(
                    nc.sbuf_tensor("q_bf", [BOT, nt], BF16))
                q_bf8 = qctx.enter_context(
                    nc.sbuf_tensor("q_bf8", [BOT, nt], BF16))

                def _st_f1():
                    with ExitStack() as sctx:
                        Es = sctx.enter_context
                        h2tile = Es(nc.sbuf_tensor("h2tile", [128, HC * 512], BF16))
                        wq_t = Es(nc.sbuf_tensor("wq_t", [128, HC * 64], BF16))
                        wu1_ = [Es(nc.sbuf_tensor(f"wu1_{i}", [128, HC * 128], BF16))
                                for i in range(2)]
                        uub = Es(nc.sbuf_tensor("uub", [128, 512], BF16))
                        sqf = Es(nc.sbuf_tensor("sqf", [128, 2 * 512], BF16))
                        rows = Es(nc.sbuf_tensor("rows3", [1, 1024], F32))
                        lsl = Es(nc.sbuf_tensor("lsl", [1, 512], F32))
                        ccrow = Es(nc.sbuf_tensor("ccrow", [1, 128], F32))
                        ps_q = Es(nc.psum_tensor("ps_q", [64, 512], F32))
                        ps_uu = [Es(nc.psum_tensor(f"ps_uu{i}", [128, 512], F32))
                                 for i in range(2)]
                        ps_le = Es(nc.psum_tensor("ps_le", [1, 512], F32))
                        ps_sum = Es(nc.psum_tensor("ps_sum3", [1, 512], F32))
                        ps_sq3 = Es(nc.psum_tensor("ps_sq3", [1, 512], F32))
                        pass
                        pass
                        twq = b.dma("gpsimd",
                                    wq_t[:].rearrange("p (c j) -> p c j", j=64),
                                    wi["wqc"], CS)
                        wu1_use = {}
                        uu_prev = None
                        tl_prev = None
                        prev_lvst = None
                        prev_tv = None
                        prev_sq = None
                        tz = b.op("gpsimd", lambda e: e.memset(
                            ccrow[:], 0.0), sem=GP)
                        prev_str = None
                        for it in range(ntt):
                            lh = b.dma(
                                "sync",
                                h2tile[:].rearrange("p (c w) -> p c w", w=512),
                                h2b.rearrange("(c p) t -> p c t", p=128)[
                                    :, :, it * 512:(it + 1) * 512],
                                LA if it % 2 == 0 else LB,
                                waits=[wu1_use.get((it - 1, 7))])
                            for c in range(HC):
                                b.op("tensor", lambda e, c=c: e.matmul(
                                    ps_q[:], wq_t[:, c * 64:(c + 1) * 64],
                                    h2tile[:, c * 512:(c + 1) * 512],
                                    start=(c == 0), stop=(c == HC - 1)),
                                    waits=([twq, lh, prev_sq] if c == 0 else ()),
                                    sem=(T if c == HC - 1 else None))
                            mq = T.tok()
                            tq1 = b.op("scalar", lambda e, it=it: e.activation(
                                q_bf[:, it * 512:(it + 1) * 512], ps_q[:],
                                AF.Identity), waits=[mq], sem=SS)
                            prev_sq = b.op("scalar", lambda e, it=it: e.activation(
                                q_bf8[:, it * 512:(it + 1) * 512], ps_q[:],
                                AF.Identity, scale=0.125), waits=[mq], sem=SS)
                            sqp = None
                            for c in range(HC):
                                pc = c % 2
                                sq_c = sqf[:, pc * 512:(pc + 1) * 512]
                                tsq = b.op("scalar", lambda e, sq_c=sq_c, c=c: e.activation(
                                    sq_c, h2tile[:, c * 512:(c + 1) * 512],
                                    AF.Square),
                                    waits=[lh, sqp], sem=SS)
                                b.op("tensor", lambda e, c=c: e.matmul(
                                    ps_sum[:], onesk[:, 0:1],
                                    h2tile[:, c * 512:(c + 1) * 512],
                                    start=(c == 0), stop=(c == HC - 1)),
                                    waits=[lh, prev_tv] if c == 0 else (),
                                    sem=(T if c == HC - 1 else None))
                                tm2 = b.op("tensor", lambda e, sq_c=sq_c, c=c: e.matmul(
                                    ps_sq3[:], onesk[:, 0:1], sq_c,
                                    start=(c == 0), stop=(c == HC - 1)),
                                    waits=[tsq], sem=T)
                                sqp = tm2
                            mstat = T.tok()
                            tv1 = b.op("vector", lambda e: e.tensor_scalar_mul(
                                rows[0:1, 0:512], ps_sum[0:1, 0:512], 1.0 / D),
                                waits=[mstat, prev_lvst], sem=V)
                            b.op("vector", lambda e: e.tensor_tensor(
                                rows[0:1, 0:512], rows[0:1, 0:512],
                                rows[0:1, 0:512], ALU.mult), sem=V)
                            b.op("vector", lambda e: e.tensor_scalar_mul(
                                rows[0:1, 512:1024], ps_sq3[0:1, 0:512], 1.0 / D),
                                sem=V)
                            tvv = b.op("vector", lambda e: e.tensor_tensor(
                                rows[0:1, 512:1024], rows[0:1, 512:1024],
                                rows[0:1, 0:512], ALU.subtract), sem=V)
                            tlnv = b.op("scalar", lambda e: e.activation(
                                rows[0:1, 512:1024], rows[0:1, 512:1024],
                                AF.Ln, bias=1.0), waits=[tvv], sem=SS)
                            prev_lvst = b.dma(
                                "scalar",
                                rowscr[2:3, it * 512:(it + 1) * 512],
                                rows[0:1, 512:1024], SR, waits=[tlnv])
                            tps = b.op("vector", lambda e: e.reduce_sum(
                                ccrow[0:1, 1:2], rows[0:1, 512:1024], AX.X),
                                waits=[tlnv, tz], sem=V)
                            prev_tv = b.op("vector", lambda e: e.tensor_tensor(
                                ccrow[0:1, 0:1], ccrow[0:1, 0:1],
                                ccrow[0:1, 1:2], ALU.add), sem=V)
                            for sj in range(8):
                                pw = sj % 2
                                wpr = wu1_use.get((it, sj - 2),
                                                  wu1_use.get((it - 1, sj + 6)))
                                wtk = b.dma(
                                    "gpsimd",
                                    wu1_[pw][:].rearrange("p (c j) -> p c j", j=128),
                                    wi["wu1c"][sj],
                                    WA if pw == 0 else WB, waits=[wpr])
                                for c in range(HC):
                                    b.op("tensor", lambda e, pw=pw, c=c: e.matmul(
                                        ps_uu[pw][:],
                                        wu1_[pw][:, c * 128:(c + 1) * 128],
                                        h2tile[:, c * 512:(c + 1) * 512],
                                        start=(c == 0), stop=(c == HC - 1)),
                                        waits=([wtk, uu_prev] if c == 0 else ()),
                                        sem=(T if c == HC - 1 else None))
                                muu = T.tok()
                                wu1_use[(it, sj)] = muu
                                tg = b.op("scalar", lambda e, pw=pw, sj=sj: e.activation(
                                    uub[:], ps_uu[pw][:], AF.Gelu_apprx_tanh,
                                    bias=bu1_t[:, sj:sj + 1]),
                                    waits=[muu, t_bu1, tl_prev], sem=SS)
                                tl_prev = b.op("tensor", lambda e, sj=sj: e.matmul(
                                    ps_le[:], wu2_t[:, sj:sj + 1], uub[:],
                                    start=(sj == 0), stop=(sj == 7)),
                                    waits=[tg, t_wu2], sem=T)
                                uu_prev = tg
                            tle = b.op("scalar", lambda e: e.activation(
                                lsl[0:1, :], ps_le[:],
                                AF.Identity, bias=sc_misc[0:1, 0:1]),
                                waits=[tl_prev, tok_ld, prev_str], sem=SS)
                            prev_str = b.dma(
                                "scalar",
                                rowscr[3:4, it * 512:(it + 1) * 512],
                                lsl[0:1, :], ST, waits=[tle])
                            tl_prev = tle
                        tzz = b.op("vector", lambda e: e.memset(
                            ccrow[0:1, 1:128], 0.0), waits=[prev_tv], sem=V)
                        tst = b.dma("scalar", cc_in, ccrow[:], ST,
                                    waits=[prev_tv, tzz])
                        tcc = b.op("gpsimd", lambda e: e.collective_compute(
                            "AllReduce", ALU.add,
                            replica_groups=[list(range(NCORE))],
                            ins=[cc_in], outs=[cc_out]),
                            waits=[tst], sem=CC, serial=False)
                    return tcc
                STAGE_MARKS.append(('_st_f1', {e: len(b.q[e]) for e in ENGINES}))
                tcc = _st_f1()
                b.barrier()

                def _st_f2():
                    with ExitStack() as sctx:
                        Es = sctx.enter_context
                        ak_sb = Es(nc.sbuf_tensor("ak_sb", [64, NSLOT], BF16))
                        wr_sb = Es(nc.sbuf_tensor("wr_sb", [64, RDIM], BF16))
                        rk_bf = Es(nc.sbuf_tensor("rk_bf", [RDIM, NSLOT], BF16))
                        rq_bf = Es(nc.sbuf_tensor("rq_bf", [RDIM, nt], BF16))
                        LRb = Es(nc.sbuf_tensor("LRb", [128, NSLOT], F32))
                        lrr = Es(nc.sbuf_tensor("lrr", [1, 1024], F32))
                        rout = Es(nc.sbuf_tensor("rout", [128, NSLOT], F32))
                        attn = Es(nc.sbuf_tensor("attn", [128, NSLOT], F32))
                        am_t = Es(nc.sbuf_tensor("am_t", [128, NSLOT], F32))
                        e_t = Es(nc.sbuf_tensor("e_t", [128, NSLOT], F32))
                        msk = Es(nc.sbuf_tensor("msk", [128, NSLOT], F32))
                        t8 = Es(nc.sbuf_tensor("t8", [128, 8], F32))
                        thr = Es(nc.sbuf_tensor("thr", [128, 1], F32))
                        nmx = Es(nc.sbuf_tensor("nmx", [128, 1], F32))
                        ssm = Es(nc.sbuf_tensor("ssm", [128, 1], F32))
                        wbf = Es(nc.sbuf_tensor("wbf", [128, NSLOT], BF16))
                        ps_r = Es(nc.psum_tensor("ps_r", [128, NSLOT], F32))
                        ps_a2 = Es(nc.psum_tensor("ps_a2", [128, NSLOT], F32))
                        ps_t = Es(nc.psum_tensor("ps_t", [128, 128], BF16))
                        tak = b.dma("gpsimd", ak_sb[:], wi["akT"], CS)
                        twr = b.dma("gpsimd", wr_sb[:], wi["WrT"], CS)
                        tl1 = b.dma("sync", lrr[0:1, 0:1024],
                                    wi["lr_row"][:, 0:1024], LD)
                        tl2 = tl1
                        with nc.psum_tensor("ps_rk", [RDIM, NSLOT], F32) as ps_rk:
                            for hh in range(2):
                                tmb = b.op("tensor", lambda e, hh=hh: e.matmul(
                                    ps_r[:, hh * 512:(hh + 1) * 512],
                                    ones1f[0:1, :],
                                    lrr[0:1, hh * 512:(hh + 1) * 512],
                                    start=True, stop=True),
                                    waits=[tl1, tl2], sem=T)
                            tlrb = b.op("scalar", lambda e: e.copy(
                                LRb[:], ps_r[:]), waits=[tmb], sem=SS)
                            for hh in range(2):
                                b.op("tensor", lambda e, hh=hh: e.matmul(
                                    ps_rk[:, hh * 512:(hh + 1) * 512], wr_sb[:],
                                    ak_sb[:, hh * 512:(hh + 1) * 512],
                                    start=True, stop=True),
                                    waits=[twr],
                                    sem=(T if hh == 1 else None))
                            mrk = T.tok()
                            trk = b.op("scalar", lambda e: e.copy(
                                rk_bf[:], ps_rk[:]), waits=[mrk], sem=SS)
                            rq_prev = None
                            for tt in range(ntt):
                                mrq = b.op("tensor", lambda e, tt=tt: e.matmul(
                                    ps_rk[:, 0:512], wr_sb[:],
                                    q_bf[:, tt * 512:(tt + 1) * 512],
                                    start=True, stop=True),
                                    waits=[trk, rq_prev], sem=T)
                                rq_prev = b.op("scalar", lambda e, tt=tt: e.activation(
                                    rq_bf[:, tt * 512:(tt + 1) * 512],
                                    ps_rk[:, 0:512], AF.Identity,
                                    scale=float(1.0 / np.sqrt(RDIM))),
                                    waits=[mrq], sem=SS)
                        trq = rq_prev
                        prev = {}
                        tpp = None
                        for tc in range(nt // 128):
                            for hh in range(2):
                                b.op("tensor", lambda e, tc=tc, hh=hh: e.matmul(
                                    ps_r[:, hh * 512:(hh + 1) * 512],
                                    rq_bf[:, tc * 128:(tc + 1) * 128],
                                    rk_bf[:, hh * 512:(hh + 1) * 512],
                                    start=True, stop=True),
                                    waits=([trq, tlrb, prev.get("rout")]
                                           if hh == 0 else ()),
                                    sem=(T if hh == 1 else None))
                            mr = T.tok()
                            tro = b.op("vector", lambda e: e.tensor_tensor(
                                rout[:], ps_r[:], LRb[:], ALU.add),
                                waits=[mr], sem=V)
                            for hh in range(2):
                                b.op("tensor", lambda e, tc=tc, hh=hh: e.matmul(
                                    ps_a2[:, hh * 512:(hh + 1) * 512],
                                    q_bf8[:, tc * 128:(tc + 1) * 128],
                                    ak_sb[:, hh * 512:(hh + 1) * 512],
                                    start=True, stop=True),
                                    waits=[prev.get("attn")] if hh == 0 else (),
                                    sem=(T if hh == 1 else None))
                            ma = T.tok()
                            tat = b.op("vector", lambda e: e.tensor_tensor(
                                attn[:], ps_a2[:], LRb[:], ALU.add),
                                waits=[ma], sem=V)
                            tt8 = b.op("vector", lambda e: e.max(
                                out=t8[:], in_=rout[:]), waits=[tro], sem=V)
                            b.op("vector", lambda e: e.tensor_reduce(
                                thr[:], t8[:], AX.X, ALU.min), sem=V)
                            tmk = b.op("vector", lambda e: e.tensor_scalar(
                                msk[:], rout[:], thr[:, 0:1], None, ALU.is_ge),
                                sem=V)
                            tam = b.op("vector", lambda e: e.scalar_tensor_tensor(
                                am_t[:], attn[:], 30000.0, msk[:], ALU.add,
                                ALU.mult), waits=[tat, prev.get("am")], sem=V)
                            tnx = b.op("vector", lambda e: e.reduce_max(
                                nmx[:], am_t[:], AX.X, negate=True), sem=V)
                            tex = b.op("scalar", lambda e: e.activation(
                                e_t[:], am_t[:], AF.Exp, bias=nmx[:, 0:1]),
                                waits=[tnx, prev.get("e")], sem=SS)
                            tsm = b.op("vector", lambda e: e.reduce_sum(
                                ssm[:], e_t[:], AX.X), waits=[tex], sem=V)
                            b.op("vector", lambda e: e.reciprocal(
                                ssm[:], ssm[:]), sem=V)
                            twb = b.op("vector", lambda e: e.tensor_scalar(
                                wbf[:], e_t[:], ssm[:, 0:1], None, ALU.mult),
                                waits=[prev.get("wbf")], sem=V)
                            prev["rout"] = tmk
                            prev["attn"] = tam
                            prev["am"] = tex
                            prev["e"] = twb
                            for sb in range(8):
                                ttr = b.op("tensor", lambda e, sb=sb: e.transpose(
                                    ps_t[:], wbf[:, sb * 128:(sb + 1) * 128],
                                    ident[:]), waits=[twb, t_id, tpp], sem=T)
                                tpp = b.op("vector", lambda e, sb=sb, tc=tc: e.tensor_copy(
                                    wT[:, sb * nt + tc * 128:
                                       sb * nt + tc * 128 + 128], ps_t[:]),
                                    waits=[ttr], sem=V)
                            prev["wbf"] = tpp
                        if wsd is not None:
                            b.dma("scalar",
                                  wsd.rearrange("(s p) t -> p s t", p=128),
                                  wT[:].rearrange("p (s t) -> p s t", t=nt),
                                  ST, waits=[tpp])
                STAGE_MARKS.append(('_st_f2', {e: len(b.q[e]) for e in ENGINES}))
                _st_f2()
                b.barrier()

            G = fctx.enter_context(nc.sbuf_tensor("G", [128, nt], F32))
            def _st_gate():
                with ExitStack() as gctx:
                    Eg = gctx.enter_context
                    grow = Eg(nc.sbuf_tensor("grow", [1, nt], F32))
                    lvrow = Eg(nc.sbuf_tensor("lvrow", [1, nt], F32))
                    lnrow = Eg(nc.sbuf_tensor("lnrow", [1, nt], F32))
                    ccrow2 = Eg(nc.sbuf_tensor("ccrow2", [1, 128], F32))
                    ps_G = Eg(nc.psum_tensor("ps_G", [128, 512], F32))
                    b.dma("sync", lvrow[0:1, :], rowscr[2:3, :], LD)
                    trw = b.dma("sync", lnrow[0:1, :], rowscr[3:4, :], LD)
                    tcr = b.dma("gpsimd", ccrow2[:], cc_out, CC2, waits=[tcc])
                    tmean = b.op("vector", lambda e: e.tensor_scalar(
                        sc_misc[0:1, 3:4], ccrow2[0:1, 0:1],
                        1.0 / (NCORE * nt), 1e-6, ALU.mult, ALU.add),
                        waits=[tcr, trw], sem=V)
                    tmr = b.op("vector", lambda e: e.reciprocal(
                        sc_misc[0:1, 3:4], sc_misc[0:1, 3:4]),
                        waits=[tmean], sem=V)
                    tnv = b.op("vector", lambda e: e.tensor_scalar(
                        lvrow[0:1, :], lvrow[0:1, :], sc_misc[0:1, 3:4], 0.5,
                        ALU.mult, ALU.mult), waits=[tmr, trw], sem=V)
                    tsg = b.op("scalar", lambda e: e.activation(
                        lnrow[0:1, :], lnrow[0:1, :], AF.Sigmoid),
                        waits=[trw], sem=SS)
                    b.op("vector", lambda e: e.scalar_tensor_tensor(
                        grow[0:1, :], lnrow[0:1, :], 2.5, lvrow[0:1, :],
                        ALU.mult, ALU.add), waits=[tsg, tnv], sem=V)
                    b.op("vector", lambda e: e.tensor_scalar(
                        grow[0:1, :], grow[0:1, :], 0.0, 5.0,
                        ALU.max, ALU.min), sem=V)
                    b.op("vector", lambda e: e.tensor_scalar(
                        grow[0:1, :], grow[0:1, :], 0.5, float(1.0 / 1.5),
                        ALU.subtract, ALU.mult), sem=V)
                    b.op("vector", lambda e: e.tensor_scalar(
                        grow[0:1, :], grow[0:1, :], 0.0, 1.0,
                        ALU.max, ALU.min), sem=V)
                    tga = b.op("vector", lambda e: e.tensor_scalar(
                        grow[0:1, :], grow[0:1, :], sc_misc[0:1, 1:2],
                        sc_misc[0:1, 2:3], ALU.mult, ALU.add),
                        waits=[t_gw1, t_gb], sem=V)
                    tgs = b.op("scalar", lambda e: e.activation(
                        grow[0:1, :], grow[0:1, :], AF.Sigmoid),
                        waits=[tga], sem=SS)
                    b.op("vector", lambda e: e.tensor_scalar(
                        lvrow[0:1, :], grow[0:1, :], 0.05, None, ALU.is_ge),
                        waits=[tgs], sem=V)
                    tgf = b.op("vector", lambda e: e.tensor_tensor(
                        grow[0:1, :], grow[0:1, :], lvrow[0:1, :], ALU.mult),
                        sem=V)
                    if g8 is not None:
                        b.dma("scalar", g8[0:1, :], grow[0:1, :], ST,
                              waits=[tgf])
                    gd = None
                    for k in range(nt // 512):
                        tbg = b.op("tensor", lambda e, k=k: e.matmul(
                            ps_G[:], ones1f[0:1, :],
                            grow[0:1, k * 512:(k + 1) * 512],
                            start=True, stop=True), waits=[tgf, gd], sem=T)
                        gd = b.op("scalar", lambda e, k=k: e.copy(
                            G[:, k * 512:(k + 1) * 512], ps_G[:]),
                            waits=[tbg], sem=SS)
                    return gd
            g_done = _st_gate()


            def _st_f3():
                with ExitStack() as sctx:
                    Es = sctx.enter_context
                    aux1 = Es(nc.sbuf_tensor("aux1", [128, HC * 512], BF16))
                    v1b = Es(nc.sbuf_tensor("v1b", [128, 2 * 512], BF16))
                    v2st = Es(nc.sbuf_tensor("v2st", [128, HC * 512], BF16))
                    ps_x = [Es(nc.psum_tensor(f"ps_x{i}", [128, 512], F32))
                            for i in range(2)]
                    pe_prev = {}
                    st_prev = None
                    for tt in range(ntt):
                        for hc in range(HC):
                            pw = hc % 2
                            for sc in range(8):
                                b.op("tensor", lambda e, pw=pw, hc=hc, sc=sc, tt=tt: e.matmul(
                                    ps_x[pw][:],
                                    av_sb[:, sc * D + hc * 128:
                                          sc * D + hc * 128 + 128],
                                    wT[:, sc * nt + tt * 512:
                                       sc * nt + tt * 512 + 512],
                                    start=(sc == 0), stop=(sc == 7)),
                                    waits=([tvu, g_done,
                                        pe_prev.get(("a", hc - 2)),
                                            st_prev] if sc == 0 else ()),
                                    sem=(T if sc == 7 else None))
                            mx = T.tok()
                            teva = b.op("scalar", lambda e, pw=pw, hc=hc: e.copy(
                                aux1[:, hc * 512:(hc + 1) * 512], ps_x[pw][:]),
                                waits=[mx], sem=SS)
                            pe_prev[("a", hc)] = teva
                        for vc in range(2):
                            pw = vc % 2
                            for hc in range(HC):
                                b.op("tensor", lambda e, pw=pw, vc=vc, hc=hc: e.matmul(
                                    ps_x[pw][:],
                                    wvd_sb[:, hc * VB + vc * 128:
                                           hc * VB + vc * 128 + 128],
                                    aux1[:, hc * 512:(hc + 1) * 512],
                                    start=(hc == 0), stop=(hc == HC - 1)),
                                    waits=([tvd, teva, pe_prev.get(("v", vc - 2))]
                                           if hc == 0 else ()),
                                    sem=(T if hc == HC - 1 else None))
                            mx = T.tok()
                            tevv = b.op("scalar", lambda e, pw=pw, vc=vc: e.copy(
                                v1b[:, vc * 512:(vc + 1) * 512], ps_x[pw][:]),
                                waits=[mx], sem=SS)
                            pe_prev[("v", vc)] = tevv
                        for hc in range(HC):
                            pw = hc % 2
                            for vc in range(2):
                                b.op("tensor", lambda e, pw=pw, hc=hc, vc=vc: e.matmul(
                                    ps_x[pw][:],
                                    wvu_sb[:, vc * D + hc * 128:
                                           vc * D + hc * 128 + 128],
                                    v1b[:, vc * 512:(vc + 1) * 512],
                                    start=(vc == 0), stop=(vc == 1)),
                                    waits=([tvu, tevv, pe_prev.get(("u", hc - 2))]
                                           if vc == 0 else ()),
                                    sem=(T if vc == 1 else None))
                            mx = T.tok()
                            tevu = b.op(
                                "vector",
                                lambda e, pw=pw, hc=hc, tt=tt: e.tensor_tensor(
                                    v2st[:, hc * 512:(hc + 1) * 512],
                                    ps_x[pw][:],
                                    G[:, tt * 512:(tt + 1) * 512],
                                    ALU.mult),
                                waits=[mx, g_done], sem=V)
                            pe_prev[("u", hc)] = tevu
                        st_prev = b.dma(
                            "scalar",
                            v2b.rearrange("(c p) t -> p c t", p=128)[
                                :, :, tt * 512:(tt + 1) * 512],
                            v2st[:].rearrange("p (c w) -> p c w", w=512),
                            ST, waits=[tevu])
                        for hc in range(HC):
                            pe_prev[("a", hc)] = st_prev
                            pe_prev[("u", hc)] = st_prev
            STAGE_MARKS.append(('_st_f3', {e: len(b.q[e]) for e in ENGINES}))
            _st_f3()
            b.barrier()

        def _st_f4():
            with ExitStack() as sctx:
                Es = sctx.enter_context
                TT4 = 512
                nt4 = nt // TT4
                v2t = Es(nc.sbuf_tensor("v2t", [128, HC * TT4], BF16))
                h2t4 = Es(nc.sbuf_tensor("h2t4", [128, HC * TT4], BF16))
                ft_ = [Es(nc.sbuf_tensor(f"ft{i}", [128, HC * TT4], BF16))
                       for i in range(2)]
                A4s = Es(nc.sbuf_tensor("A4s", [128, TT4], BF16))
                B4s = Es(nc.sbuf_tensor("B4s", [128, TT4], BF16))
                fqt = Es(nc.sbuf_tensor("fqt", [128, 2 * TT4], BF16))
                rows = Es(nc.sbuf_tensor("rows4", [1, 2 * TT4], F32))
                ps_sum = Es(nc.psum_tensor("ps_sum4", [1, TT4], F32))
                ps_sq4 = Es(nc.psum_tensor("ps_sq4", [1, TT4], F32))
                ps_A = Es(nc.psum_tensor("ps_A4", [128, TT4], F32))
                ps_B = Es(nc.psum_tensor("ps_B4", [128, TT4], F32))
                prev_out = {}
                prev_t4 = None
                for it in range(nt4):
                    sl = slice(it * TT4, (it + 1) * TT4)
                    ft = ft_[it % 2]
                    lv = b.dma("sync",
                               v2t[:].rearrange("p (c w) -> p c w", w=TT4),
                               v2b.rearrange("(c p) t -> p c t", p=128)[
                                   :, :, sl],
                               LA if it % 2 == 0 else LB, waits=[prev_t4])
                    lh4 = b.dma("sync",
                                h2t4[:].rearrange("p (c w) -> p c w", w=TT4),
                                h2b.rearrange("(c p) t -> p c t", p=128)[
                                    :, :, sl],
                                LC, waits=[prev_t4])
                    fv = ft[:].rearrange("p (c w) -> p c w", w=TT4)
                    hv = h2t4[:].rearrange("p (c w) -> p c w", w=TT4)
                    vv = v2t[:].rearrange("p (c w) -> p c w", w=TT4)
                    t4 = b.op("vector",
                              lambda e, fv=fv, hv=hv, vv=vv: e.tensor_tensor(
                                  fv, hv, vv, ALU.add),
                              waits=[lv, lh4, prev_out.get(it - 2)], sem=V)
                    prev_t4 = t4
                    mm_toks = {}
                    stoks = stats_pass(ft, TT4, HC, None, fqt, ps_sum,
                                       ps_sq4, lambda c: t4, mm_toks,
                                       src_bf=True)
                    rt = ln_rows(ps_sum, ps_sq4, rows, stoks, TT4)
                    bt = bcast2(ps_A, ps_B, rows, rt, TT4)
                    tca4 = b.op("scalar", lambda e: e.copy(
                        A4s[:], ps_A[:, 0:TT4]), waits=[bt[0]], sem=SS)
                    tcb4 = b.op("scalar", lambda e: e.copy(
                        B4s[:], ps_B[:, 0:TT4]), waits=[bt[1]], sem=SS)
                    b.op("vector", lambda e, fv=fv: e.tensor_tensor(
                        fv, fv, rep(A4s[:], HC, TT4), ALU.mult),
                        waits=[tca4, tcb4], sem=V)
                    t7 = b.op("vector", lambda e, fv=fv: e.tensor_tensor(
                        fv, fv, rep(B4s[:], HC, TT4), ALU.add), sem=V)
                    prev_out[it] = b.op(
                        "gpsimd",
                        lambda e, fv=fv, sl=sl: e.dma_start(
                            out=out_t.rearrange("(c p) t -> p c t", p=128)[
                                :, :, sl],
                            in_=fv, accum_op=ALU.add),
                        waits=[t7], sem=ACX[it % 2], inc=16, serial=False)

        STAGE_MARKS.append(('_st_f4', {e: len(b.q[e]) for e in ENGINES}))
        _st_f4()
        STAGE_MARKS.append(('end', {e: len(b.q[e]) for e in ENGINES}))
        b.wait("sync", ST.tok())
        b.wait("sync", CS.tok())
        b.wait("sync", CC.tok())
        b.wait("sync", CC2.tok())
        b.wait("sync", AC.tok())
        b.wait("sync", AC2.tok())
        b.wait("sync", SX[0].tok())
        b.wait("sync", SX[1].tok())

        with nc.Block() as block:
            b.finalize(block)
    return nc



def _prep_shared(inp):
    g = lambda k: np.ascontiguousarray(np.asarray(inp[k], dtype=np.float32))
    C = np.ascontiguousarray
    lao = g("L0_attn_local_approx_output")
    fuse = g("L0_attn_fuse_weight")
    d = {}
    d["upc"] = C(g("L0_attn_up_w").reshape(8, 128, HC, 128)
                 .transpose(0, 3, 2, 1))
    d["gatec"] = C(g("L0_attn_gate_w").reshape(8, 128, HC, 128)
                   .transpose(0, 3, 2, 1))
    WC = np.concatenate(
        [lao.T[None], fuse.transpose(2, 1, 0)], axis=0).reshape(NP, D)
    d["wcc"] = C(WC.reshape(2, 28, 128, HC, 128).transpose(0, 3, 2, 1, 4))
    d["fgc"] = C(g("L0_attn_ffn_gate_w").reshape(NF // 256, 256, HC, 128)
                 .transpose(0, 3, 2, 1))
    d["fuc"] = C(g("L0_attn_ffn_up_w").reshape(NF // 256, 256, HC, 128)
                 .transpose(0, 3, 2, 1))
    fdT = np.asarray(inp["L0_attn_ffn_down_w"], dtype=np.float32).T
    d["fdc"] = C(fdT.reshape(2, 39, 128, HC, 128).transpose(0, 3, 2, 1, 4))
    d["wqc"] = C(g("L0_ffn_Wq").reshape(BOT, HC, 128).transpose(2, 1, 0))
    d["WrT"] = C(g("L0_ffn_Wr").T)
    d["akT"] = C(g("L0_ffn_aux_keys").T)
    d["avc"] = C(g("L0_ffn_aux_values").reshape(8, 128, D).transpose(1, 0, 2))
    d["wvdc"] = C(g("L0_ffn_Wvd").reshape(VB, HC, 128).transpose(2, 1, 0))
    d["wvuc"] = C(g("L0_ffn_Wvu").reshape(D, 2, 128).transpose(2, 1, 0))
    d["wu1c"] = C(g("L0_ffn_Wu1").reshape(8, 128, HC, 128)
                  .transpose(0, 3, 2, 1))
    d["wu2c"] = C(g("L0_ffn_Wu2").reshape(8, 128).T)
    d["lp_c"] = C(g("L0_attn_local_point").reshape(8, 128).T)
    d["bu1_c"] = C(g("L0_ffn_bu1").reshape(8, 128).T)
    d["lr_row"] = g("L0_ffn_log_rel").reshape(1, NSLOT)
    d["d_row"] = g("L0_attn_discount_factor").reshape(1, 6)
    d["bu2_s"] = g("L0_ffn_bu2").reshape(1, 1)
    d["gw1_s"] = np.asarray(
        inp["L0_ffn_gate_w1"], dtype=np.float32).reshape(1, 1)
    d["gb_s"] = np.asarray(
        inp["L0_ffn_gate_bias"], dtype=np.float32).reshape(1, 1)
    return d


def kernel(**inputs):
    x = np.ascontiguousarray(
        np.asarray(inputs["L0_attn_hidden_states"], dtype=np.float32))
    _, seq, _ = x.shape
    nt = seq // NCORE
    shared = _prep_shared(inputs)
    xt = x[0].T
    nc = build_program(nt=nt)
    in_maps = []
    for c in range(NCORE):
        m = dict(shared)
        m["x_in"] = np.ascontiguousarray(xt[:, c * nt:(c + 1) * nt])
        in_maps.append(m)
    res = run_bass_kernel_spmd(nc, in_maps, core_ids=list(range(NCORE)))
    out = np.concatenate(
        [res.results[c]["out_t"].T for c in range(NCORE)], axis=0)
    return out.reshape(1, seq, D).astype(np.float32)
